# version 01
# baseline (speedup 1.0000x reference)
import numpy as np
import ml_dtypes
import concourse.bass as bass
import concourse.mybir as mybir
from concourse.bass_utils import run_bass_kernel_spmd

F32 = mybir.dt.float32
BF16 = mybir.dt.bfloat16
I8 = mybir.dt.int8
I32 = mybir.dt.int32
AF = mybir.ActivationFunctionType
ALU = mybir.AluOpType
AX = mybir.AxisListType
ds = bass.ds
EngT = mybir.EngineType


class CFG:
    B = 4
    T = 4096
    DM = 2048
    H = 16
    DH = 128
    DFF = 8192
    G = 256
    EPS = 1e-5

    @property
    def TOK(self): return self.T // 2
    @property
    def HC(self): return self.H // 2
    @property
    def DCH(self): return self.DM // 128
    @property
    def FCH(self): return self.DFF // 128
    @property
    def F512(self): return self.DFF // 512
    @property
    def NG(self): return self.TOK // self.G
    @property
    def NT(self): return self.G // 128
    @property
    def QT(self): return self.T // 128
    @property
    def OUTF(self): return 3 * self.HC * self.DH


cfg = CFG()

INV_SQRT_D_HI = float(np.float32(1.0 / np.sqrt(128.0)))
INV_SQRT_D_LO = float(np.float32(1.0 / np.sqrt(np.float64(128.0)) - np.float64(np.float32(1.0 / np.sqrt(128.0)))))
TWO_PI = 2.0 * np.pi
_c1 = float(np.float32(6.28125))
_c2 = float(np.float32(TWO_PI - _c1))
_c3 = float(np.float32(TWO_PI - _c1 - np.float64(np.float32(TWO_PI - _c1))))
HALF_PI = np.pi / 2.0
_h1 = float(np.float32(1.5703125))
_h2 = float(np.float32(HALF_PI - _h1))
INV_TWO_PI = float(np.float32(1.0 / TWO_PI))


class Sems:
    def __init__(self, nc, ctx):
        self.nc = nc
        self._ctx = ctx
        self.h = {}
        self.n = {}

    def new(self, name):
        self.h[name] = self._ctx.enter_context(self.nc.semaphore(name))
        self.n[name] = 0
        return name

    def inc(self, name, inst, k=1):
        inst.then_inc(self.h[name], k)
        self.n[name] += k
        return self.n[name]

    def val(self, name):
        return self.n[name]

    def wait(self, eng, name, v=None):
        eng.wait_ge(self.h[name], self.n[name] if v is None else v)


ENGS = ("sync", "vector", "scalar", "tensor", "gpsimd")


class Rec:
    def __init__(self, sems):
        self.q = {k: [] for k in ENGS}
        self.sems = sems
        self.trace = {k: [] for k in ENGS}

    @staticmethod
    def _site():
        return "?"

    def o(self, eng, fn, inc=None, k=1):
        self.trace[eng].append(("inc" if inc else "op", inc, k, self._site()))
        if inc is not None:
            self.sems.n[inc] += k
            h = self.sems.h[inc]
            def wrapped(e, env, fn=fn, h=h, k=k):
                fn(e, env).then_inc(h, k)
            self.q[eng].append(wrapped)
        else:
            self.q[eng].append(lambda e, env, fn=fn: fn(e, env))
        return self.sems.n[inc] if inc is not None else None

    def w(self, eng, sem, v=None):
        val = self.sems.n[sem] if v is None else v
        h = self.sems.h[sem]
        self.trace[eng].append(("wait", sem, val, self._site()))
        self.q[eng].append(lambda e, env, h=h, val=val: e.wait_ge(h, val))

    def drain(self, eng):
        self.trace[eng].append(("drain", None, 0, self._site()))
        self.q[eng].append(lambda e, env: e.drain())

    def switch(self, eng, body, incs=()):
        for s, k in incs:
            self.sems.n[s] += k
            self.trace[eng].append(("inc", s, k, self._site()))
        def fn(e, env, body=body):
            for rho in env["nc"].Switch(engines=[e], index=[env["pars"][env["_engname"]]], n=2):
                body(e, env, rho)
        self.q[eng].append(("SWITCH", fn))

    def replay(self, block, env):
        def mk(name):
            def body(e):
                pre = env.get("prelude", {}).get(name)
                if pre is not None:
                    pre(e, env)
                if name in env.get("offload", {}):
                    e.wait_ge(env["ld_h"], env["ld_v"])
                    _, vals = env["nc"].values_load_multi_w_load_instructions(
                        env["par_ap"], engines=[env["offload"][name]], min_val=0,
                        max_val=1, skip_runtime_bounds_check=True)
                    env.setdefault("pars", {})[name] = vals[0]
                env["_engname"] = name
                for fn in self.q[name]:
                    if isinstance(fn, tuple):
                        fn[1](e, env)
                    else:
                        fn(e, env)
            return body
        for name in ENGS:
            getattr(block, name)(mk(name))


def ceil_div(a, b):
    return (a + b - 1) // b


def _noop():
    pass


class Arena:
    def __init__(self, nc, bytes_per_part):
        self.t = nc.alloc_sbuf_tensor("arena", [128, bytes_per_part], I8)
        self.views = {F32: self.t.bitcast(F32), BF16: self.t.bitcast(BF16), I8: self.t, I32: self.t.bitcast(I32)}
        self.bpp = bytes_per_part

    def v(self, dtype, byte_off, cols, rows=128):
        iz = {F32: 4, BF16: 2, I8: 1, I32: 4}[dtype]
        assert byte_off % iz == 0 and byte_off + cols * iz <= self.bpp, (byte_off, cols, dtype)
        e0 = byte_off // iz
        ap = self.views[dtype].ap()[0:rows, e0:e0 + cols]
        return ap



class MK:

    def __init__(self, c: CFG):
        self.c = c
        nc = bass.Bass("TRN2", target_bir_lowering=False)
        self.nc = nc
        TOK, DM, T, DFF, OUTF = c.TOK, c.DM, c.T, c.DFF, c.OUTF
        HALF_F = DFF // 2
        self.x_in = nc.dram_tensor("x", [TOK, DM], F32, kind="ExternalInput").ap()
        self.qkvwT = nc.dram_tensor("qkvwT", [DM, OUTF], F32, kind="ExternalInput").ap()
        self.qkvb = nc.dram_tensor("qkvb", [1, OUTF], F32, kind="ExternalInput").ap()
        self.outwT = nc.dram_tensor("outwT", [DM, DM], F32, kind="ExternalInput").ap()
        self.outb = nc.dram_tensor("outb", [1, DM], F32, kind="ExternalInput").ap()
        self.w1Th = nc.dram_tensor("w1Th", [DM, HALF_F], F32, kind="ExternalInput").ap()
        self.w2Th = nc.dram_tensor("w2Th", [HALF_F, DM], F32, kind="ExternalInput").ap()
        self.poffs = nc.dram_tensor("poffs", [1, 8], I32, kind="ExternalInput").ap()
        self.y_out = nc.dram_tensor("y", [TOK, DM], F32, kind="ExternalOutput").ap()
        self.d_x1 = nc.dram_tensor("d_x1", [TOK, DM], F32).ap()
        self.d_x2 = nc.dram_tensor("d_x2", [TOK, DM], F32).ap()
        self.d_qkv = nc.dram_tensor("d_qkv", [OUTF, T], F32).ap()
        self.d_emb = nc.dram_tensor("d_emb", [256, T], F32).ap()
        self.bar_in = nc.dram_tensor("bar_in", [1, 8], I32).ap()
        self.bar_out = nc.dram_tensor("bar_out", [8, 8], I32).ap()
        self.d_ln2Tl = nc.dram_tensor("d_ln2Tl", [DM, TOK], F32).ap()
        self.d_oTl = nc.dram_tensor("d_oTl", [c.HC * 128, T], F32).ap()
        self.d_ocl = nc.dram_tensor("d_ocl", [DM, TOK], F32).ap()
        self.d_t1Tl = nc.dram_tensor("d_t1Tl", [DM, HALF_F], BF16).ap()
        self.d_t2Tl = nc.dram_tensor("d_t2Tl", [HALF_F, DM], BF16).ap()
        self.cc_in = nc.dram_tensor("cc_in", [1, 2], F32).ap()
        self.cc_out = nc.dram_tensor("cc_out", [1, 2], F32).ap()
        self.sh_ln2T = nc.dram_tensor("sh_ln2T", [DM, T], F32, addr_space="Shared").ap()
        self.sh_ocat = nc.dram_tensor("sh_ocat", [DM, T], F32, addr_space="Shared").ap()
        self.sh_t1T = nc.dram_tensor("sh_t1T", [DM, DFF], BF16, addr_space="Shared").ap()
        self.sh_t2T = nc.dram_tensor("sh_t2T", [DFF, DM], BF16, addr_space="Shared").ap()

        from contextlib import ExitStack
        self.ctx = ExitStack()
        self.sems = Sems(nc, self.ctx)
        self.DMA_SEMS = ("sLd1", "sLdX", "sWA0", "sWA1", "sWB0", "sWB1", "sLdQW",
                         "sQL0", "sQL1", "sLdQKV", "sLdOC", "sWO0", "sWO1",
                         "sStCX", "sStLN", "sStQ0", "sStQ1", "sStO0", "sStO1",
                         "sStX2", "sStEMB", "sStW", "sLd2")
        for s in ("sT", "sV", "sS", "sG", "sCC") + self.DMA_SEMS:
            self.sems.new(s)
        self.R = Rec(self.sems)
        self.ar = Arena(nc, 196 * 1024)
        self._psum = [self.ctx.enter_context(nc.psum_tensor("pb%d" % i, [128, 512], F32))
                      for i in range(8)]
        self.pguard = [None] * 8
        self._arena_map()

    def _arena_map(self):
        c, ar = self.c, self.ar
        NT, DM, DFF, T, G, DCH, FCH, OUTF = c.NT, c.DM, c.DFF, c.T, c.G, c.DCH, c.FCH, c.OUTF
        v = ar.v
        self.ID32 = v(F32, 0, 128); self.IDBF = v(BF16, 512, 128)
        self.MASK = v(F32, 768, 128); self.IOTA = v(I32, 1280, 128)
        self.ONESB = v(F32, 1792, 128)
        self.ONESC = v(F32, 1792, 1)
        self.ONESR = v(F32, 1792, 128, rows=1)
        self.POFF = v(I32, 2304, 8, rows=1)
        self._stat_base = 2368
        self._pers_end = self._stat_base + 4 * 96
        assert self._pers_end <= 4096
        FFN = 4096
        def seq(base, *sizes):
            offs = []
            o = base
            for s in sizes:
                offs.append(o); o += s
            offs.append(o)
            return offs
        (self.A_H, self.B_LN, self.C_X, self.D_ST, self.QB, self.QT8, self.HQ,
         self.HT, self.WA, self.WB, self.I8A, self.UX, ffn_end) = seq(
            FFN, NT * DFF * 4, DM * 4, NT * DM * 4, DCH * 128 * 4, DM * 2, DCH * G * 2,
            DFF * 2, FCH * G * 2, 2 * DCH * 256 * 2, 4 * 512 * 2, DFF, DM * 2)
        assert ffn_end <= ar.bpp, ffn_end
        self.OGT = min(6, OUTF // 128)
        (self.QWT, self.QLN, self.QEMB, self.QST, qkv_end) = seq(
            FFN, DCH * self.OGT * 128 * 4, 2 * DCH * 512 * 4, 2 * T * 4, 2 * 512 * 4)
        assert qkv_end <= ar.bpp, qkv_end
        (self.AQ, self.AK, self.AVV, self.AVT, self.APS, self.APT, self.AO,
         self.AOS, self.AP0, att_end) = seq(
            FFN, T * 4, T * 4, T * 4, T * 4, T * 4, T * 4, T * 4, 1024 * 4, 4096 * 4)
        assert att_end <= ar.bpp, att_end
        self.OPG = min(512, c.TOK)
        (self.OC, self.WO, self.X1G, self.X2S, self.OB, op_end) = seq(
            FFN, DCH * self.OPG * 4, 2 * 512 * 4, (self.OPG // 128) * DM * 4,
            (self.OPG // 128) * DM * 4, DM * 4)
        assert op_end <= ar.bpp, op_end

    def stat(self, i, n=1, rows=128):
        return self.ar.v(F32, self._stat_base + 4 * i, n, rows=rows)

    def mm(self, out_ap, lhsT, rhs, start, stop, inc=None):
        return self.R.o("tensor", lambda e, _: e.matmul(out_ap, lhsT, rhs, start=start, stop=stop),
                        inc=inc)

    def fence(self, name=""):
        R, sems = self.R, self.sems
        eng_sem = {"vector": "sV", "scalar": "sS", "tensor": "sT", "gpsimd": "sG"}
        vals = {}
        for eng, s in eng_sem.items():
            R.o(eng, lambda e, _: e.drain(), inc=s)
            vals[eng] = sems.val(s)
        vals_dma = {s: sems.val(s) for s in self.DMA_SEMS}
        for eng in ENGS:
            for other, s in eng_sem.items():
                if other != eng:
                    R.w(eng, s, vals[other])
            for s, vv in vals_dma.items():
                R.w(eng, s, vv)

    def barrier(self):
        R = self.R
        v = R.o("gpsimd", lambda e, _: e.collective_compute(
            "AllGather", ALU.bypass, replica_groups=[list(range(8))],
            ins=[self.bar_in], outs=[self.bar_out]), inc="sCC")
        R.w("gpsimd", "sCC", v)
        return v

    def vop(self, fn, inc="sV"):
        self.R.drain("vector")
        return self.R.o("vector", fn, inc=inc)

    def transp_batch(self, src_ap_fn, n_tiles, dst_base_fn, banks, bf16=False,
                     evac_eng="vector", evac_scale=None):
        R, S = self.R, self.sems
        ident = self.IDBF if bf16 else self.ID32
        nb = len(banks)
        for i0 in range(0, n_tiles, 4):
            w = min(4, n_tiles - i0)
            bk = banks[(i0 // 4) % nb]
            psbank = self._psum[bk]
            psap = psbank.bitcast(BF16).ap() if bf16 else psbank.ap()
            if self.pguard[bk] is not None:
                sem, val = self.pguard[bk]
                R.w("tensor", sem, val)
            for j in range(w):
                tv = self.R.o("tensor",
                              lambda e, _, j=j, i0=i0, psap=psap: e.transpose(
                                  psap[:, j * 128:(j + 1) * 128], src_ap_fn(i0 + j), ident),
                              inc="sT")
            R.w(evac_eng, "sT", tv)
            dst = dst_base_fn(i0, w * 128)
            if evac_scale is not None:
                ev = R.o(evac_eng, lambda e, _, psap=psap, dst=dst, w=w: e.activation(
                    out=dst, in_=psap[:, 0:w * 128], func=AF.Copy, scale=evac_scale),
                    inc="sS" if evac_eng == "scalar" else "sV")
            elif evac_eng == "scalar":
                ev = R.o("scalar", lambda e, _, psap=psap, dst=dst, w=w: e.copy(
                    dst, psap[:, 0:w * 128]), inc="sS")
            else:
                ev = R.o("vector", lambda e, _, psap=psap, dst=dst, w=w: e.tensor_copy(
                    dst, psap[:, 0:w * 128]), inc="sV")
            self.pguard[bk] = (("sS" if evac_eng == "scalar" else "sV"), ev)

    def quant(self, x_ap, s_ap, out_ap, i8_ap, u_ap, clip=False):
        R, S = self.R, self.sems
        R.w("scalar", "sG", S.val("sG"))
        R.w("scalar", "sV", S.val("sV"))
        R.w("gpsimd", "sT", S.val("sT"))
        vS = R.o("scalar", lambda e, _: e.activation(out=i8_ap, in_=x_ap, func=AF.Copy,
                                                     scale=s_ap, bias=-1.0), inc="sS")
        R.w("vector", "sS", vS)
        vV = self.vop(lambda e, _: e.scalar_tensor_tensor(
            out=x_ap, in0=x_ap, scalar=s_ap, in1=i8_ap, op0=ALU.mult, op1=ALU.subtract))
        R.w("gpsimd", "sV", vV)
        vG = R.o("gpsimd", lambda e, _: e.tensor_scalar(
            out=u_ap, in0=x_ap, scalar1=0.5, scalar2=None, op0=ALU.is_ge), inc="sG")
        R.w("vector", "sG", vG)
        vV2 = self.vop(lambda e, _: e.scalar_tensor_tensor(
            out=x_ap, in0=x_ap, scalar=1.5, in1=u_ap, op0=ALU.is_ge, op1=ALU.add))
        R.w("gpsimd", "sV", vV2)
        if clip:
            vG2 = R.o("gpsimd", lambda e, _: e.tensor_tensor(
                out=u_ap, in0=i8_ap, in1=x_ap, op=ALU.add), inc="sG")
            R.w("vector", "sG", vG2)
            vfin = self.vop(lambda e, _: e.tensor_scalar(
                out=out_ap, in0=u_ap, scalar1=-1.0, scalar2=1.0, op0=ALU.max, op1=ALU.min))
            return ("sV", vfin)
        vG2 = R.o("gpsimd", lambda e, _: e.tensor_tensor(
            out=out_ap, in0=i8_ap, in1=x_ap, op=ALU.add), inc="sG")
        return ("sG", vG2)

    def ln_tile(self, x_ap, out_ap, sq_scratch_ap, sb):
        R, DM, EPS = self.R, self.c.DM, self.c.EPS
        st_sum, st_mu, st_msq, st_vpe, st_inv, st_nb = (self.stat(sb + i) for i in range(6))
        self.vop(lambda e, _: e.tensor_reduce(out=st_sum, in_=x_ap, axis=AX.X, op=ALU.add))
        vS = R.o("scalar", lambda e, _: e.activation(out=sq_scratch_ap, in_=x_ap,
                                                     func=AF.Square, accum_out=st_msq), inc="sS")
        R.w("vector", "sS", vS)
        self.vop(lambda e, _: e.tensor_scalar(out=st_mu, in0=st_sum, scalar1=1.0 / DM,
                                              scalar2=None, op0=ALU.mult))
        self.vop(lambda e, _: e.tensor_scalar(out=st_msq, in0=st_msq, scalar1=1.0 / DM,
                                              scalar2=None, op0=ALU.mult))
        self.vop(lambda e, _: e.scalar_tensor_tensor(out=st_vpe, in0=st_mu, scalar=st_mu,
                                                     in1=st_msq, op0=ALU.mult, op1=ALU.subtract))
        self.vop(lambda e, _: e.tensor_scalar(out=st_vpe, in0=st_vpe, scalar1=-1.0,
                                              scalar2=EPS, op0=ALU.mult, op1=ALU.add))
        vV = self.vop(lambda e, _: e.reciprocal(out=st_vpe, in_=st_vpe))
        R.w("scalar", "sV", vV)
        vS2 = R.o("scalar", lambda e, _: e.activation(out=st_inv, in_=st_vpe,
                                                      func=AF.Sqrt), inc="sS")
        R.w("vector", "sS", vS2)
        self.vop(lambda e, _: e.tensor_scalar(out=st_nb, in0=st_mu, scalar1=st_inv,
                                              scalar2=-1.0, op0=ALU.mult, op1=ALU.mult))
        va = self.vop(lambda e, _: e.tensor_scalar(out=out_ap, in0=x_ap, scalar1=st_inv,
                                                   scalar2=st_nb, op0=ALU.mult, op1=ALU.add))
        return ("sV", va), st_inv

    def p0(self):
        c, R, S, ar = self.c, self.R, self.sems, self.ar
        nc = self.nc
        T, DM, DFF = c.T, c.DM, c.DFF
        HALF_F = DFF // 2
        R.w("gpsimd", "sLd1", 16)
        bi = R.o("gpsimd", lambda e, _: e.dma_start(out=self.bar_in[:, :], in_=self.POFF),
                 inc="sStW", k=16)
        R.w("gpsimd", "sStW", bi)
        vG = R.o("gpsimd", lambda e, _: e.iota(self.IOTA, pattern=[[-1, 128]], base=0,
                                               channel_multiplier=1), inc="sG")
        R.o("gpsimd", lambda e, _: e.memset(self.ONESB, 1.0), inc="sG")
        R.w("vector", "sG", S.val("sG"))
        self.vop(lambda e, _: e.tensor_scalar(out=self.ID32, in0=self.IOTA, scalar1=0.0,
                                              scalar2=None, op0=ALU.is_equal))
        self.vop(lambda e, _: e.tensor_copy(self.IDBF, self.ID32))
        self.vop(lambda e, _: e.tensor_scalar(out=self.MASK, in0=self.IOTA, scalar1=0.0,
                                              scalar2=-1e30, op0=ALU.is_lt, op1=ALU.mult))
        ANG = ar.v(F32, self.A_H, T)
        EMB = ar.v(F32, self.A_H + T * 4, T)
        KIF = ar.v(I32, self.A_H + 2 * T * 4, T)
        KF = ar.v(F32, self.A_H + 3 * T * 4, T)
        PCOL = ar.v(I32, self.A_H + 4 * T * 4, 1)
        st_pf, st_u, st_fr, st_o1, st_o2 = (self.stat(80 + i) for i in range(5))
        vG2 = R.o("gpsimd", lambda e, _: e.iota(ar.v(I32, self.A_H, T), pattern=[[1, T]],
                                                base=0, channel_multiplier=0), inc="sG")
        R.o("gpsimd", lambda e, _: e.iota(PCOL, pattern=[[0, 1]], base=0,
                                          channel_multiplier=1), inc="sG")
        R.w("vector", "sG", S.val("sG"))
        self.vop(lambda e, _: e.tensor_copy(st_pf, PCOL))
        self.vop(lambda e, _: e.tensor_scalar(out=st_u, in0=st_pf, scalar1=64.0, scalar2=None,
                                              op0=ALU.is_ge))
        self.vop(lambda e, _: e.tensor_scalar(out=st_fr, in0=st_pf, scalar1=1.0 / 64.0,
                                              scalar2=None, op0=ALU.mult))
        self.vop(lambda e, _: e.tensor_tensor(out=st_fr, in0=st_fr, in1=st_u, op=ALU.subtract))
        self.vop(lambda e, _: e.tensor_scalar(out=st_o1, in0=st_u, scalar1=_h1, scalar2=None,
                                              op0=ALU.mult))
        self.vop(lambda e, _: e.tensor_scalar(out=st_o2, in0=st_u, scalar1=_h2, scalar2=None,
                                              op0=ALU.mult))
        self.vop(lambda e, _: e.tensor_scalar(out=ANG, in0=ar.v(I32, self.A_H, T),
                                              scalar1=st_fr, scalar2=st_o1,
                                              op0=ALU.mult, op1=ALU.add))
        self.vop(lambda e, _: e.tensor_scalar(out=ANG, in0=ANG, scalar1=st_o2, scalar2=None,
                                              op0=ALU.add))
        self.vop(lambda e, _: e.tensor_scalar(out=KF, in0=ANG, scalar1=INV_TWO_PI,
                                              scalar2=None, op0=ALU.mult))
        self.vop(lambda e, _: e.tensor_copy(KIF, KF))
        self.vop(lambda e, _: e.tensor_copy(KF, KIF))
        for cc in (_c1, _c2, _c3):
            self.vop(lambda e, _, cc=cc: e.scalar_tensor_tensor(
                out=ANG, in0=KF, scalar=-cc, in1=ANG, op0=ALU.mult, op1=ALU.add))
        self.vop(lambda e, _: e.tensor_scalar(out=KF, in0=ANG, scalar1=float(np.pi),
                                              scalar2=None, op0=ALU.is_gt))
        for cc in (_c1, _c2, _c3):
            self.vop(lambda e, _, cc=cc: e.scalar_tensor_tensor(
                out=ANG, in0=KF, scalar=-cc, in1=ANG, op0=ALU.mult, op1=ALU.add))
        vz = S.val("sV")
        R.w("scalar", "sV", vz)
        vS = R.o("scalar", lambda e, _: e.activation(out=EMB, in_=ANG, func=AF.Sin), inc="sS")
        R.w("vector", "sS", vS)
        self.vop(lambda e, _: e.tensor_scalar(out=ANG, in0=EMB, scalar1=INV_SQRT_D_HI,
                                              scalar2=None, op0=ALU.mult))
        vq = self.vop(lambda e, _: e.scalar_tensor_tensor(out=ANG, in0=EMB,
                                                          scalar=INV_SQRT_D_LO, in1=ANG,
                                                          op0=ALU.mult, op1=ALU.add))
        R.w("gpsimd", "sV", vq)
        R.o("gpsimd", lambda e, _: e.dma_start(out=self.d_emb[0:128, :], in_=ANG), inc="sStEMB", k=16)
        R.o("gpsimd", lambda e, _: e.dma_start(out=self.d_emb[128:256, :], in_=EMB), inc="sStEMB", k=16)

        st_a1, st_a2, st_t = self.stat(86), self.stat(87), self.stat(88)
        st_m = self.stat(89, n=4, rows=1)
        W1C = ar.v(F32, self.A_H, HALF_F)
        R.w("sync", "sStEMB", S.val("sStEMB"))
        nch1 = DM // 128
        vsum = None
        for i in range(nch1):
            ld = R.o("sync", lambda e, _, i=i: e.dma_start(
                out=W1C, in_=self.w1Th[i * 128:(i + 1) * 128, :]), inc="sLd1", k=16)
            R.w("vector", "sLd1", ld)
            self.vop(lambda e, _: e.tensor_reduce(out=st_t, in_=W1C, axis=AX.X, op=ALU.add,
                                                  apply_absolute_value=True))
            if i == 0:
                self.vop(lambda e, _: e.tensor_copy(st_a1, st_t))
            else:
                self.vop(lambda e, _: e.tensor_tensor(out=st_a1, in0=st_a1, in1=st_t, op=ALU.add))
            R.w("sync", "sV", S.val("sV"))
        W2C = ar.v(F32, self.A_H, DM)
        nch2 = HALF_F // 128
        for i in range(nch2):
            ld = R.o("sync", lambda e, _, i=i: e.dma_start(
                out=W2C, in_=self.w2Th[i * 128:(i + 1) * 128, :]), inc="sLd1", k=16)
            R.w("vector", "sLd1", ld)
            self.vop(lambda e, _: e.tensor_reduce(out=st_t, in_=W2C, axis=AX.X, op=ALU.add,
                                                  apply_absolute_value=True))
            if i == 0:
                self.vop(lambda e, _: e.tensor_copy(st_a2, st_t))
            else:
                self.vop(lambda e, _: e.tensor_tensor(out=st_a2, in0=st_a2, in1=st_t, op=ALU.add))
            R.w("sync", "sV", S.val("sV"))
        ps7 = self._psum[7]
        vv = S.val("sV")
        R.w("tensor", "sV", vv)
        self.mm(ps7.ap()[0:1, 0:1], self.ONESC, st_a1, True, True, inc="sT")
        t2 = self.mm(ps7.ap()[0:1, 1:2], self.ONESC, st_a2, True, True, inc="sT")
        R.w("vector", "sT", t2)
        vcp = self.vop(lambda e, _: e.tensor_copy(st_m[0:1, 0:2], ps7.ap()[0:1, 0:2]))
        self.pguard[7] = ("sV", vcp)
        R.w("gpsimd", "sV", vcp)
        wv = R.o("gpsimd", lambda e, env: e.dma_start(
            out=self.cc_in[0:1, 0:2], in_=st_m[0:1, 0:2]), inc="sStW", k=16)
        R.w("gpsimd", "sStW", wv)
        cv = R.o("gpsimd", lambda e, _: e.collective_compute(
            "AllReduce", ALU.add, replica_groups=[list(range(8))],
            ins=[self.cc_in], outs=[self.cc_out]), inc="sCC")
        R.w("sync", "sCC", S.val("sCC"))
        ld = R.o("sync", lambda e, _: e.dma_start(out=st_m[0:1, 0:2], in_=self.cc_out[0:1, 0:2]),
                 inc="sLd2", k=16)
        R.w("vector", "sLd2", ld)
        st_mm = self.stat(93, n=4, rows=1)
        self.vop(lambda e, _: e.tensor_scalar(out=st_mm[0:1, 0:2], in0=st_m[0:1, 0:2],
                                              scalar1=1.0 / (4.0 * DM * DFF), scalar2=1e-5,
                                              op0=ALU.mult, op1=ALU.max))
        vr = self.vop(lambda e, _: e.reciprocal(out=st_mm[0:1, 2:4], in_=st_mm[0:1, 0:2]))
        R.w("tensor", "sV", vr)
        tb = self.mm(ps7.ap()[:, 0:4], self.ONESR, st_mm[0:1, 0:4], True, True, inc="sT")
        R.w("vector", "sT", tb)
        self.M1B = self.stat(94)
        self.M2B = self.stat(95)
        self.SW1B = self.stat(76)
        self.SW2B = self.stat(77)
        self.M1B127 = self.stat(78)
        self.M2B127h = self.stat(79)
        vcp2 = self.vop(lambda e, _: e.tensor_copy(self.M1B, ps7.ap()[:, 0:1]))
        self.vop(lambda e, _: e.tensor_copy(self.M2B, ps7.ap()[:, 1:2]))
        self.vop(lambda e, _: e.tensor_copy(self.SW1B, ps7.ap()[:, 2:3]))
        vcp3 = self.vop(lambda e, _: e.tensor_copy(self.SW2B, ps7.ap()[:, 3:4]))
        self.pguard[7] = ("sV", vcp3)
        self.vop(lambda e, _: e.tensor_scalar(out=self.M1B127, in0=self.M1B,
                                              scalar1=1.0 / 127.0, scalar2=None, op0=ALU.mult))
        self.vop(lambda e, _: e.tensor_scalar(out=self.M2B127h, in0=self.M2B,
                                              scalar1=0.5 / 127.0, scalar2=None, op0=ALU.mult))

        WQ = ar.v(BF16, self.A_H + HALF_F * 4, HALF_F)
        WU = ar.v(BF16, self.A_H + HALF_F * 4 + HALF_F * 2, HALF_F)
        WI8 = ar.v(I8, self.A_H + HALF_F * 4 + 2 * HALF_F * 2, HALF_F)
        for i in range(nch1):
            R.w("sync", "sStW", S.val("sStW"))
            ld = R.o("sync", lambda e, _, i=i: e.dma_start(
                out=W1C, in_=self.w1Th[i * 128:(i + 1) * 128, :]), inc="sLd1", k=16)
            for eng in ("scalar", "vector", "gpsimd"):
                R.w(eng, "sLd1", ld)
            sem, v = self.quant(W1C, self.SW1B, WQ, WI8, WU, clip=True)
            R.w("gpsimd", sem, v)
            R.o("gpsimd", lambda e, env, i=i: e.dma_start(
                out=self.d_t1Tl[i * 128:(i + 1) * 128, :], in_=WQ), inc="sStW", k=16)
        W2Cb = ar.v(F32, self.A_H, DM)
        WQ2 = ar.v(BF16, self.A_H + DM * 4, DM)
        WU2 = ar.v(BF16, self.A_H + DM * 4 + DM * 2, DM)
        WI82 = ar.v(I8, self.A_H + DM * 4 + 2 * DM * 2, DM)
        for i in range(nch2):
            R.w("sync", "sStW", S.val("sStW"))
            ld = R.o("sync", lambda e, _, i=i: e.dma_start(
                out=W2Cb, in_=self.w2Th[i * 128:(i + 1) * 128, :]), inc="sLd1", k=16)
            for eng in ("scalar", "vector", "gpsimd"):
                R.w(eng, "sLd1", ld)
            sem, v = self.quant(W2Cb, self.SW2B, WQ2, WI82, WU2, clip=True)
            R.w("gpsimd", sem, v)
            R.o("gpsimd", lambda e, env, i=i: e.dma_start(
                out=self.d_t2Tl[i * 128:(i + 1) * 128, :], in_=WQ2), inc="sStW", k=16)
        R.w("gpsimd", "sStW", S.val("sStW"))
        semh = S.h["sStW"]
        def _tcopy(e, env, rho):
            e.dma_start(out=self.sh_t1T[:, rho * HALF_F:(rho + 1) * HALF_F],
                        in_=self.d_t1Tl[:, :]).then_inc(semh, 16)
            e.dma_start(out=self.sh_t2T[rho * HALF_F:(rho + 1) * HALF_F, :],
                        in_=self.d_t2Tl[:, :]).then_inc(semh, 16)
        R.switch("gpsimd", _tcopy, incs=(("sStW", 32),))
        R.w("gpsimd", "sStW", S.val("sStW"))
        self.barrier()

    def ffn_call(self, x_src, dst, do_ln2, bar_weights):
        c, R, S, ar = self.c, self.R, self.sems, self.ar
        G, NT, DM, DFF, DCH, FCH, F512, NG, TOK = (c.G, c.NT, c.DM, c.DFF, c.DCH,
                                                   c.FCH, c.F512, c.NG, c.TOK)
        NF256 = DFF // 256
        DOW = min(512, DM)
        WAb = [ar.v(BF16, self.WA + i * DCH * 256 * 2, DCH * 256) for i in range(2)]
        WBb = [ar.v(BF16, self.WB + i * 512 * 2, DOW) for i in range(4)]
        CX = ar.v(F32, self.C_X, NT * DM)
        BLN = ar.v(F32, self.B_LN, DM)
        DST = ar.v(F32, self.D_ST, DM)
        QB8 = ar.v(BF16, self.QB, DM)
        UX = ar.v(BF16, self.UX, DM)
        R.w("sync", "sCC", bar_weights)
        for g in range(NG):
            sb = 8
            ldx = 0
            R.w("sync", "sLdX", S.val("sLdX"))
            for tt in range(NT):
                ldx = R.o("sync", lambda e, _, g=g, tt=tt: e.dma_start(
                    out=CX[:, tt * DM:(tt + 1) * DM],
                    in_=x_src[(g * G + tt * 128):(g * G + tt * 128 + 128), :]),
                    inc="sLdX", k=16)
            R.w("scalar", "sV", S.val("sV"))
            for tt in range(NT):
                for eng in ("vector", "scalar"):
                    R.w(eng, "sLdX", ldx)
                R.w("scalar", "sG", S.val("sG"))
                R.w("scalar", "sS", S.val("sS"))
                R.w("scalar", "sV", S.val("sV"))
                R.w("vector", "sG", S.val("sG"))
                xa = CX[:, tt * DM:(tt + 1) * DM]
                (sA, vA), st_inv = self.ln_tile(xa, BLN, BLN, sb + tt * 12)
                st_mx = self.stat(sb + tt * 12 + 6)
                st_sa = self.stat(sb + tt * 12 + 7)
                st_scA = self.stat(sb + tt * 12 + 8)
                if sA != "sV":
                    R.w("vector", sA, vA)
                self.vop(lambda e, _, st_mx=st_mx: e.tensor_reduce(
                    out=st_mx, in_=BLN, axis=AX.X, op=ALU.max, apply_absolute_value=True))
                self.vop(lambda e, _, st_mx=st_mx: e.tensor_scalar(
                    out=st_mx, in0=st_mx, scalar1=1e-5, scalar2=None, op0=ALU.max))
                self.vop(lambda e, _, st_sa=st_sa, st_mx=st_mx: e.reciprocal(out=st_sa, in_=st_mx))
                self.vop(lambda e, _, st_sa=st_sa: e.tensor_scalar(
                    out=st_sa, in0=st_sa, scalar1=127.0, scalar2=None, op0=ALU.mult))
                vscl = self.vop(lambda e, _, st_scA=st_scA, st_mx=st_mx: e.tensor_tensor(
                    out=st_scA, in0=st_mx, in1=self.M1B127, op=ALU.mult))
                R.w("scalar", "sV", vscl)
                qsem, qv = self.quant(BLN, st_sa, QB8, ar.v(I8, self.I8A, DM), UX)
                R.w("tensor", qsem, qv)
                self.transp_batch(lambda i: QB8[:, i * 128:(i + 1) * 128], DCH,
                                  lambda i0, w, tt=tt: ar.v(BF16, self.QT8, w,)[:, 0:0] if False else
                                  ar.v(BF16, self.QT8 + (tt * DCH + i0) * 128 * 2, w),
                                  banks=(4, 5), bf16=True, evac_eng="vector")
            qt8_ready = S.val("sV")
            R.w("tensor", "sV", qt8_ready)
            qt8_tile = lambda dc, tt: ar.v(BF16, self.QT8 + (tt * DCH + dc) * 128 * 2, 128)
            AH = lambda tt: ar.v(F32, self.A_H + tt * DFF * 4, DFF)
            last_mm_v = None
            for fs in range(NF256):
                buf = WAb[fs % 2]
                if fs >= 2:
                    R.w("sync", "sT", self._wa_guard[fs % 2])
                wsem = "sWA0" if fs % 2 == 0 else "sWA1"
                R.w("sync", wsem, S.val(wsem))
                ldw = R.o("sync", lambda e, _, fs=fs: e.dma_start(
                    out=WAb[fs % 2].rearrange("p (c f) -> p c f", c=DCH),
                    in_=self.sh_t1T[:, fs * 256:(fs + 1) * 256].rearrange(
                        "(c p) f -> p c f", p=128)), inc=wsem, k=16)
                R.w("tensor", wsem, ldw)
                for tt in range(NT):
                    bk = (fs * NT + tt) % 4
                    ps = self._psum[bk]
                    if self.pguard[bk] is not None:
                        R.w("tensor", *self.pguard[bk])
                        self.pguard[bk] = None
                    for dc in range(DCH):
                        tv = self.mm(ps.ap()[:, 0:256], qt8_tile(dc, tt),
                                     WAb[fs % 2][:, dc * 256:(dc + 1) * 256],
                                     start=(dc == 0), stop=(dc == DCH - 1),
                                     inc="sT" if dc == DCH - 1 else None)
                    if tt == NT - 1:
                        self._wa_guard[fs % 2] = tv
                    R.w("scalar", "sT", tv)
                    st_scA = self.stat(sb + tt * 12 + 8)
                    sv = R.o("scalar", lambda e, _, ps=ps, fs=fs, tt=tt, st_scA=st_scA:
                             e.activation(out=AH(tt)[:, fs * 256:(fs + 1) * 256],
                                          in_=ps.ap()[:, 0:256], func=AF.Silu,
                                          scale=st_scA), inc="sS")
                    self.pguard[bk] = ("sS", sv)
            hsil_v = S.val("sS")
            for tt in range(NT):
                st_mh = self.stat(sb + tt * 12 + 9)
                st_sh = self.stat(sb + tt * 12 + 10)
                st_sc2 = self.stat(sb + tt * 12 + 11)
                R.w("vector", "sS", hsil_v)
                self.vop(lambda e, _, st_mh=st_mh, tt=tt: e.tensor_reduce(
                    out=st_mh, in_=AH(tt), axis=AX.X, op=ALU.max, apply_absolute_value=True))
                self.vop(lambda e, _, st_mh=st_mh: e.tensor_scalar(
                    out=st_mh, in0=st_mh, scalar1=1e-5, scalar2=None, op0=ALU.max))
                self.vop(lambda e, _, st_sh=st_sh, st_mh=st_mh: e.reciprocal(
                    out=st_sh, in_=st_mh))
                self.vop(lambda e, _, st_sh=st_sh: e.tensor_scalar(
                    out=st_sh, in0=st_sh, scalar1=127.0, scalar2=None, op0=ALU.mult))
                vsc2 = self.vop(lambda e, _, st_sc2=st_sc2, st_mh=st_mh: e.tensor_tensor(
                    out=st_sc2, in0=st_mh, in1=self.M2B127h, op=ALU.mult))
                R.w("scalar", "sV", vsc2)
                HQb = ar.v(BF16, self.HQ, DFF)
                qsem, qv = self.quant(AH(tt), st_sh, HQb, ar.v(I8, self.I8A, DFF), HQb)
                R.w("tensor", qsem, qv)
                self.transp_batch(lambda i: HQb[:, i * 128:(i + 1) * 128], FCH,
                                  lambda i0, w, tt=tt: ar.v(
                                      BF16, self.HT + (tt * FCH + i0) * 128 * 2, w),
                                  banks=(4, 5), bf16=True, evac_eng="vector")
            ht_ready = S.val("sV")
            R.w("tensor", "sV", ht_ready)
            ht_tile = lambda fc, tt: ar.v(BF16, self.HT + (tt * FCH + fc) * 128 * 2, 128)
            for do in range(DM // DOW):
                for fc in range(FCH):
                    wbuf = fc % 4
                    wsem2 = "sWB0" if fc % 2 == 0 else "sWB1"
                    if self._wb_guard[wbuf]:
                        R.w("sync", "sT", self._wb_guard[wbuf])
                    R.w("sync", wsem2, S.val(wsem2))
                    ldw2 = R.o("sync", lambda e, _, fc=fc, do=do, wbuf=wbuf: e.dma_start(
                        out=WBb[wbuf],
                        in_=self.sh_t2T[fc * 128:(fc + 1) * 128, do * DOW:(do + 1) * DOW]),
                        inc=wsem2, k=16)
                    R.w("tensor", wsem2, ldw2)
                    last_in_batch = (fc % 4 == 3) or (fc == FCH - 1)
                    for tt in range(NT):
                        bk = 6 if tt == 0 else 7
                        ps = self._psum[bk]
                        if fc == 0 and self.pguard[bk] is not None:
                            R.w("tensor", *self.pguard[bk])
                            self.pguard[bk] = None
                        tv2 = self.mm(ps.ap()[:, 0:DOW], ht_tile(fc, tt), WBb[wbuf],
                                      start=(fc == 0), stop=(fc == FCH - 1),
                                      inc="sT" if (last_in_batch and tt == NT - 1) else None)
                    if last_in_batch:
                        for b in range(4):
                            self._wb_guard[b] = tv2
                for tt in range(NT):
                    bk = 6 if tt == 0 else 7
                    ps = self._psum[bk]
                    st_sc2 = self.stat(sb + tt * 12 + 11)
                    R.w("vector", "sT", S.val("sT"))
                    R.w("vector", "sS", S.val("sS"))
                    R.w("vector", "sLdX", S.val("sLdX"))
                    ev = self.vop(lambda e, _, ps=ps, tt=tt, do=do, st_sc2=st_sc2:
                                  e.scalar_tensor_tensor(
                        out=CX[:, tt * DM + do * DOW: tt * DM + (do + 1) * DOW],
                        in0=ps.ap()[:, 0:DOW], scalar=st_sc2,
                        in1=CX[:, tt * DM + do * DOW: tt * DM + (do + 1) * DOW],
                        op0=ALU.mult, op1=ALU.add))
                    self.pguard[bk] = ("sV", ev)
            for tt in range(NT):
                R.w("gpsimd", "sV", S.val("sV"))
                stv = R.o("gpsimd", lambda e, _, g=g, tt=tt: e.dma_start(
                    out=dst[(g * G + tt * 128):(g * G + tt * 128 + 128), :],
                    in_=CX[:, tt * DM:(tt + 1) * DM]), inc="sStCX", k=16)
                if do_ln2:
                    R.w("scalar", "sV", S.val("sV"))
                    R.w("scalar", "sT", S.val("sT"))
                    R.w("vector", "sT", S.val("sT"))
                    (sA2, vA2), _ = self.ln_tile(CX[:, tt * DM:(tt + 1) * DM], BLN, BLN,
                                                 56 + tt * 12)
                    R.w("tensor", sA2, vA2)
                    R.w("vector", "sStLN", S.val("sStLN"))
                    self.transp_batch(lambda i: BLN[:, i * 128:(i + 1) * 128], DCH,
                                      lambda i0, w: ar.v(F32, self.D_ST + i0 * 128 * 4, w),
                                      banks=(4, 5), bf16=False, evac_eng="vector")
                    R.w("gpsimd", "sV", S.val("sV"))
                    R.o("gpsimd", lambda e, env, g=g, tt=tt: e.dma_start(
                        out=self.d_ln2Tl.rearrange("(c p) t -> p c t", p=128)[
                            :, :, (g * G + tt * 128):(g * G + tt * 128 + 128)],
                        in_=DST.rearrange("p (c t) -> p c t", c=DCH)), inc="sStLN", k=16)
            R.w("sync", "sStCX", S.val("sStCX"))
            R.w("sync", "sS", S.val("sS"))
            R.w("sync", "sV", S.val("sV"))
            R.w("sync", "sG", S.val("sG"))
            R.w("sync", "sT", S.val("sT"))

    def qkv(self, bar_ln2):
        c, R, S, ar = self.c, self.R, self.sems, self.ar
        T, DM, DCH, OUTF, HC = c.T, c.DM, c.DCH, c.OUTF, c.HC
        OGT = self.OGT
        NTILE = OUTF // 128
        NOG = ceil_div(NTILE, OGT)
        TC = min(512, T)
        EMQ = ar.v(F32, self.QEMB, T)
        EMK = ar.v(F32, self.QEMB + T * 4, T)
        QSTb = [ar.v(F32, self.QST + i * 512 * 4, TC) for i in range(2)]
        QLNb = [ar.v(F32, self.QLN + i * DCH * 512 * 4, DCH * TC) for i in range(2)]
        BIASQ = self.stat(32, n=NTILE)
        l1 = R.o("sync", lambda e, _: e.dma_start(out=EMQ, in_=self.d_emb[0:128, :]), inc="sLd1", k=16)
        l2 = R.o("sync", lambda e, _: e.dma_start(out=EMK, in_=self.d_emb[128:256, :]), inc="sLd1", k=16)
        l3 = R.o("sync", lambda e, _: e.dma_start(
            out=BIASQ, in_=self.qkvb[0, :].rearrange("(o p) -> p o", p=128),
            allow_slow_non_contiguous=True), inc="sLd1", k=16)
        R.w("sync", "sCC", bar_ln2)
        qst_guard = [0, 0]
        qln_guard = [0, 0]
        for og in range(NOG):
            nt = min(OGT, NTILE - og * OGT)
            if og > 0:
                R.w("sync", "sT", S.val("sT"))
            R.w("sync", "sLdQW", S.val("sLdQW"))
            lw = R.o("sync", lambda e, _, og=og, nt=nt: e.dma_start(
                out=ar.v(F32, self.QWT, DCH * nt * 128).rearrange("p (c o) -> p c o", c=DCH),
                in_=self.qkvwT[:, og * OGT * 128: og * OGT * 128 + nt * 128].rearrange(
                    "(c p) o -> p c o", p=128)), inc="sLdQW", k=16)
            for tc in range(T // TC):
                bln = tc % 2
                lnsem = "sQL0" if bln == 0 else "sQL1"
                if qln_guard[bln]:
                    R.w("sync", "sT", qln_guard[bln])
                R.w("sync", lnsem, S.val(lnsem))
                ln = R.o("sync", lambda e, _, tc=tc, bln=bln: e.dma_start(
                    out=QLNb[bln].rearrange("p (c t) -> p c t", c=DCH),
                    in_=self.sh_ln2T[:, tc * TC:(tc + 1) * TC].rearrange(
                        "(c p) t -> p c t", p=128)), inc=lnsem, k=16)
                for ot in range(nt):
                    gt = og * OGT + ot
                    bk = (tc * nt + ot) % 2
                    ps = self._psum[bk]
                    if self.pguard[bk] is not None:
                        R.w("tensor", *self.pguard[bk])
                        self.pguard[bk] = None
                    R.w("tensor", "sLdQW", lw)
                    R.w("tensor", lnsem, ln)
                    R.w("tensor", "sLd1", l3)
                    for dc in range(DCH):
                        tv = self.mm(ps.ap()[:, 0:TC],
                                     ar.v(F32, self.QWT + (dc * nt + ot) * 128 * 4, 128),
                                     QLNb[bln][:, dc * TC:(dc + 1) * TC],
                                     start=(dc == 0), stop=(dc == DCH - 1),
                                     inc="sT" if dc == DCH - 1 else None)
                    if ot == nt - 1:
                        qln_guard[bln] = tv
                    sbuf_i = (tc * nt + ot) % 2
                    if qst_guard[sbuf_i]:
                        R.w("vector", "sStQ0" if sbuf_i == 0 else "sStQ1", qst_guard[sbuf_i])
                    R.w("vector", "sT", tv)
                    R.w("vector", "sLd1", max(l1, l2, l3))
                    kind = gt // HC
                    if kind < 2:
                        emb = EMQ if kind == 0 else EMK
                        vv = self.vop(lambda e, _, ps=ps, gt=gt, sbuf_i=sbuf_i, emb=emb, tc=tc:
                                      e.scalar_tensor_tensor(
                            out=QSTb[sbuf_i], in0=ps.ap()[:, 0:TC], scalar=BIASQ[:, gt:gt + 1],
                            in1=emb[:, tc * TC:(tc + 1) * TC], op0=ALU.add, op1=ALU.mult))
                    else:
                        vv = self.vop(lambda e, _, ps=ps, gt=gt, sbuf_i=sbuf_i: e.tensor_scalar(
                            out=QSTb[sbuf_i], in0=ps.ap()[:, 0:TC], scalar1=BIASQ[:, gt:gt + 1],
                            scalar2=None, op0=ALU.add))
                    self.pguard[bk] = ("sV", vv)
                    dep = ("sV", vv)
                    R.w("gpsimd", *dep)
                    gv = R.o("gpsimd", lambda e, _, gt=gt, tc=tc, sbuf_i=sbuf_i: e.dma_start(
                        out=self.d_qkv[gt * 128:(gt + 1) * 128, tc * TC:(tc + 1) * TC],
                        in_=QSTb[sbuf_i]), inc="sStQ0" if sbuf_i == 0 else "sStQ1", k=16)
                    qst_guard[sbuf_i] = gv

    def attn(self):
        c, R, S, ar = self.c, self.R, self.sems, self.ar
        T, HC, QT = c.T, c.HC, c.QT
        NKT = T // 128
        AQ = ar.v(F32, self.AQ, T); AK = ar.v(F32, self.AK, T)
        AVV = ar.v(F32, self.AVV, T); AVT = ar.v(F32, self.AVT, T)
        APS = ar.v(F32, self.APS, T); APT = ar.v(F32, self.APT, T)
        AO = ar.v(F32, self.AO, T)
        AOSb = [ar.v(F32, self.AOS + i * 512 * 4, 512) for i in range(2)]
        p0 = self.AP0
        KMX = ar.v(F32, p0, 32, rows=1); KPR = ar.v(F32, p0 + 128, 32, rows=1)
        QMX = ar.v(F32, p0 + 256, 32, rows=1); NGC = ar.v(F32, p0 + 384, 32, rows=1)
        NEGC = ar.v(F32, p0 + 512, 32)
        LPART = ar.v(F32, p0 + 1024, 16)
        RML = ar.v(F32, p0 + 1536, 1)
        aos_guard = [0, 0]
        for h in range(HC):
            base_q, base_k, base_v = h, HC + h, 2 * HC + h
            R.w("sync", "sStO0", S.val("sStO0"))
            R.w("sync", "sStO1", S.val("sStO1"))
            R.w("sync", "sLdQKV", S.val("sLdQKV"))
            R.w("sync", "sStQ0", S.val("sStQ0"))
            R.w("sync", "sStQ1", S.val("sStQ1"))
            lq = R.o("sync", lambda e, _, r=base_q: e.dma_start(
                out=AQ, in_=self.d_qkv[r * 128:(r + 1) * 128, :]), inc="sLdQKV", k=16)
            lk = R.o("sync", lambda e, _, r=base_k: e.dma_start(
                out=AK, in_=self.d_qkv[r * 128:(r + 1) * 128, :]), inc="sLdQKV", k=16)
            lv = R.o("sync", lambda e, _, r=base_v: e.dma_start(
                out=AVV, in_=self.d_qkv[r * 128:(r + 1) * 128, :]), inc="sLdQKV", k=16)
            R.w("tensor", "sLdQKV", lv)
            R.w("vector", "sLdQKV", lv)
            self.transp_batch(lambda i: AVV[:, i * 128:(i + 1) * 128], NKT,
                              lambda i0, w: ar.v(F32, self.AVT + i0 * 128 * 4, w),
                              banks=(2, 3), bf16=False, evac_eng="vector")
            ps7 = self._psum[7]
            for which, src, ld, dst in (("k", AK, lk, KMX), ("q", AQ, lq, QMX)):
                R.w("vector", "sLdQKV", lv)
                vv = self.vop(lambda e, _, src=src: e.tensor_tensor(
                    out=APS, in0=src, in1=src, op=ALU.mult))
                R.w("tensor", "sV", vv)
                for ch in range(T // 512):
                    if self.pguard[7] is not None:
                        R.w("tensor", *self.pguard[7])
                        self.pguard[7] = None
                    tv = self.mm(ps7.ap()[0:1, 0:512], self.ONESC,
                                 APS[:, ch * 512:(ch + 1) * 512], True, True, inc="sT")
                    R.w("vector", "sT", tv)
                    rv = self.vop(lambda e, _, ch=ch, dst=dst: e.tensor_reduce(
                        out=dst[0:1, ch * 4:(ch + 1) * 4],
                        in_=ps7.ap()[0:1, 0:512].rearrange("p (c t) -> p c t", t=128),
                        axis=AX.X, op=ALU.max))
                    self.pguard[7] = ("sV", rv)
            for qt in range(QT):
                self.R.q["vector"].append(lambda e, _: None) if False else None
                rv = self.vop(lambda e, _, qt=qt: e.tensor_reduce(
                    out=KPR[0:1, qt:qt + 1], in_=KMX[0:1, 0:qt + 1], axis=AX.X, op=ALU.max))
            self.vop(lambda e, _: e.tensor_tensor(out=NGC[0:1, 0:QT], in0=QMX[0:1, 0:QT],
                                                  in1=KPR[0:1, 0:QT], op=ALU.mult))
            vs = S.val("sV")
            R.w("scalar", "sV", vs)
            sv = R.o("scalar", lambda e, _: e.activation(out=NGC[0:1, 0:QT],
                                                         in_=NGC[0:1, 0:QT], func=AF.Sqrt),
                     inc="sS")
            R.w("vector", "sS", sv)
            vneg = self.vop(lambda e, _: e.tensor_scalar(out=NGC[0:1, 0:QT],
                                                         in0=NGC[0:1, 0:QT], scalar1=-1.0,
                                                         scalar2=None, op0=ALU.mult))
            R.w("tensor", "sV", vneg)
            if self.pguard[7] is not None:
                R.w("tensor", *self.pguard[7]); self.pguard[7] = None
            tb = self.mm(ps7.ap()[:, 0:QT], self.ONESR, NGC[0:1, 0:QT], True, True, inc="sT")
            R.w("vector", "sT", tb)
            vb = self.vop(lambda e, _: e.tensor_copy(NEGC[:, 0:QT], ps7.ap()[:, 0:QT]))
            self.pguard[7] = ("sV", vb)
            aps_guard_t = 0
            aps_guard_v = 0
            for qt in range(QT):
                Lk = 128 * (qt + 1)
                R.w("scalar", "sT", aps_guard_t)
                R.w("scalar", "sV", aps_guard_v)
                chunks = []
                c0 = 0
                while c0 + 512 <= Lk - 128:
                    chunks.append((c0, 512)); c0 += 512
                if Lk - c0 > 0:
                    chunks.append((c0, Lk - c0))
                nch = len(chunks)
                for ci, (cb, cw) in enumerate(chunks):
                    bk = ci % 2
                    ps = self._psum[bk]
                    if self.pguard[bk] is not None:
                        R.w("tensor", *self.pguard[bk]); self.pguard[bk] = None
                    tv = self.mm(ps.ap()[:, 0:cw], AQ[:, qt * 128:(qt + 1) * 128],
                                 AK[:, cb:cb + cw], True, True, inc="sT")
                    last = (ci == nch - 1)
                    dep_sem, dep_v = "sT", tv
                    if last:
                        R.w("vector", "sT", tv)
                        mv = self.vop(lambda e, _, ps=ps, cw=cw: e.tensor_tensor(
                            out=ps.ap()[:, cw - 128:cw], in0=ps.ap()[:, cw - 128:cw],
                            in1=self.MASK, op=ALU.add))
                        dep_sem, dep_v = "sV", mv
                    R.w("scalar", dep_sem, dep_v)
                    sv = R.o("scalar", lambda e, _, ps=ps, cb=cb, cw=cw, ci=ci, qt=qt:
                             e.activation(out=APS[:, cb:cb + cw], in_=ps.ap()[:, 0:cw],
                                          func=AF.Exp, bias=NEGC[:, qt:qt + 1],
                                          accum_out=LPART[:, ci:ci + 1]), inc="sS")
                    self.pguard[bk] = ("sS", sv)
                R.w("vector", "sS", S.val("sS"))
                self.vop(lambda e, _, nch=nch: e.tensor_reduce(
                    out=RML, in_=LPART[:, 0:nch], axis=AX.X, op=ALU.add))
                vr = self.vop(lambda e, _: e.reciprocal(out=RML, in_=RML))
                R.w("tensor", "sS", S.val("sS"))
                self.transp_batch(lambda i: APS[:, i * 128:(i + 1) * 128], qt + 1,
                                  lambda i0, w: ar.v(F32, self.APT + i0 * 128 * 4, w),
                                  banks=(2, 3), bf16=False, evac_eng="vector")
                aps_guard_t = S.val("sT")
                aps_guard_v = S.val("sV")
                R.w("tensor", "sV", S.val("sV"))
                bk = 4
                if self.pguard[bk] is not None:
                    R.w("tensor", *self.pguard[bk]); self.pguard[bk] = None
                ps4 = self._psum[bk]
                for kt in range(qt + 1):
                    tv = self.mm(ps4.ap()[:, 0:128], APT[:, kt * 128:(kt + 1) * 128],
                                 AVT[:, kt * 128:(kt + 1) * 128],
                                 start=(kt == 0), stop=(kt == qt),
                                 inc="sT" if kt == qt else None)
                R.w("scalar", "sT", tv)
                R.w("scalar", "sV", vr)
                so = R.o("scalar", lambda e, _, qt=qt: e.activation(
                    out=AO[:, qt * 128:(qt + 1) * 128], in_=ps4.ap()[:, 0:128],
                    func=AF.Copy, scale=RML), inc="sS")
                self.pguard[bk] = ("sS", so)
            sT_before = S.val("sS")
            for i0 in range(0, NKT, 4):
                w = min(4, NKT - i0)
                bk = 5 if (i0 // 4) % 2 == 0 else 6
                ps = self._psum[bk]
                if self.pguard[bk] is not None:
                    R.w("tensor", *self.pguard[bk]); self.pguard[bk] = None
                R.w("tensor", "sS", sT_before)
                for j in range(w):
                    tv = R.o("tensor", lambda e, _, j=j, i0=i0, ps=ps: e.transpose(
                        ps.ap()[:, j * 128:(j + 1) * 128],
                        AO[:, (i0 + j) * 128:(i0 + j + 1) * 128], self.ID32), inc="sT")
                half = (i0 // 4) % 2
                if aos_guard[half]:
                    R.w("vector", "sStO0" if half == 0 else "sStO1", aos_guard[half])
                R.w("vector", "sT", tv)
                ev = self.vop(lambda e, _, ps=ps, half=half, w=w: e.tensor_copy(
                    AOSb[half][:, 0:w * 128], ps.ap()[:, 0:w * 128]))
                self.pguard[bk] = ("sV", ev)
                R.w("gpsimd", "sV", ev)
                gv = R.o("gpsimd", lambda e, env, h=h, i0=i0, w=w, half=half: e.dma_start(
                    out=self.d_oTl[h * 128:(h + 1) * 128, i0 * 128:(i0 + w) * 128],
                    in_=AOSb[half][:, 0:w * 128]), inc="sStO0" if half == 0 else "sStO1", k=16)
                aos_guard[half] = gv

    def outproj(self, bar_o):
        c, R, S, ar = self.c, self.R, self.sems, self.ar
        T, DM, DCH, TOK = c.T, c.DM, c.DCH, c.TOK
        OPG = self.OPG
        DOW = min(512, DM)
        NT2 = OPG // 128
        OC = ar.v(F32, self.OC, DCH * OPG)
        WOb = [ar.v(F32, self.WO + i * 512 * 4, DOW) for i in range(2)]
        X1Gv = ar.v(F32, self.X1G, NT2 * DM)
        X2Sv = ar.v(F32, self.X2S, NT2 * DM)
        OBROW = self.stat(32, n=DM // 128, rows=1)
        ob = ar.v(F32, self.OB, DM, rows=1)
        lob = R.o("sync", lambda e, _: e.dma_start(out=ob, in_=self.outb[0:1, :]), inc="sLd1", k=16)
        R.w("sync", "sCC", bar_o)
        wo_guard = [0, 0]
        R.w("sync", "sStCX", S.val("sStCX"))
        R.w("sync", "sCC", bar_o)
        R.w("sync", "sLd1", S.val("sLd1"))
        semoc2 = S.h["sLd1"]
        TOKc = TOK
        def _ocfetch(e, env, rho):
            e.dma_start(out=self.d_ocl[:, :],
                        in_=self.sh_ocat[:, rho * TOKc:(rho + 1) * TOKc]).then_inc(semoc2, 16)
        R.switch("sync", _ocfetch, incs=(("sLd1", 16),))
        ocf = S.val("sLd1")
        for grp in range(TOK // OPG):
            R.w("sync", "sStX2", S.val("sStX2"))
            R.w("sync", "sLdOC", S.val("sLdOC"))
            R.w("sync", "sLd1", ocf)
            lo = R.o("sync", lambda e, env, grp=grp: e.dma_start(
                out=OC.rearrange("p (c t) -> p c t", c=DCH),
                in_=self.d_ocl.rearrange("(c p) t -> p c t", p=128)[
                    :, :, grp * OPG:(grp + 1) * OPG]), inc="sLdOC", k=16)
            lx = R.o("sync", lambda e, _, grp=grp: e.dma_start(
                out=X1Gv.rearrange("p (n m) -> p n m", n=NT2),
                in_=self.d_x1[grp * OPG:(grp + 1) * OPG, :].rearrange(
                    "(n p) m -> p n m", p=128)), inc="sLdOC", k=16)
            for do in range(DM // DOW):
                for tt2 in range(NT2):
                    bk = tt2
                    if self.pguard[bk] is not None:
                        R.w("tensor", *self.pguard[bk]); self.pguard[bk] = None
                for dc in range(DCH):
                    wbuf = (do * DCH + dc) % 2
                    wos = "sWO0" if wbuf == 0 else "sWO1"
                    if wo_guard[wbuf]:
                        R.w("sync", "sT", wo_guard[wbuf])
                    R.w("sync", wos, S.val(wos))
                    lw = R.o("sync", lambda e, _, dc=dc, do=do, wbuf=wbuf: e.dma_start(
                        out=WOb[wbuf],
                        in_=self.outwT[dc * 128:(dc + 1) * 128, do * DOW:(do + 1) * DOW]),
                        inc=wos, k=16)
                    R.w("tensor", wos, lw)
                    R.w("tensor", "sLdOC", lx)
                    R.w("tensor", "sLd1", lob)
                    for tt2 in range(NT2):
                        tv = self.mm(self._psum[tt2].ap()[:, 0:DOW],
                                     OC[:, dc * OPG + tt2 * 128: dc * OPG + tt2 * 128 + 128],
                                     WOb[wbuf], start=(dc == 0), stop=False,
                                     inc="sT" if tt2 == NT2 - 1 else None)
                    wo_guard[wbuf] = tv
                for tt2 in range(NT2):
                    tb = self.mm(self._psum[tt2].ap()[:, 0:DOW], self.ONESR,
                                 ob[0:1, do * DOW:(do + 1) * DOW], False, True, inc="sT")
                    R.w("vector", "sT", tb)
                    R.w("vector", "sLdOC", lx)
                    ev = self.vop(lambda e, _, tt2=tt2, do=do: e.scalar_tensor_tensor(
                        out=X2Sv[:, tt2 * DM + do * DOW: tt2 * DM + (do + 1) * DOW],
                        in0=self._psum[tt2].ap()[:, 0:DOW], scalar=1.0,
                        in1=X1Gv[:, tt2 * DM + do * DOW: tt2 * DM + (do + 1) * DOW],
                        op0=ALU.mult, op1=ALU.add))
                    self.pguard[tt2] = ("sV", ev)
            for tt2 in range(NT2):
                R.w("gpsimd", "sV", S.val("sV"))
                R.o("gpsimd", lambda e, _, grp=grp, tt2=tt2: e.dma_start(
                    out=self.d_x2[grp * OPG + tt2 * 128: grp * OPG + tt2 * 128 + 128, :],
                    in_=X2Sv[:, tt2 * DM:(tt2 + 1) * DM]), inc="sStX2", k=16)

    def build(self):
        R, S = self.R, self.sems
        self._wa_guard = [0, 0]
        self._wb_guard = [0, 0, 0, 0]
        S.n["sLd1"] += 16
        self.p0()
        bar_w = S.val("sCC")
        self.fence()
        self.ffn_call(self.x_in, self.d_x1, True, bar_w)
        R.w("gpsimd", "sStLN", S.val("sStLN"))
        R.w("gpsimd", "sStCX", S.val("sStCX"))
        semln = S.h["sStLN"]
        TOKc = self.c.TOK
        def _lncopy(e, env, rho):
            e.dma_start(out=self.sh_ln2T[:, rho * TOKc:(rho + 1) * TOKc],
                        in_=self.d_ln2Tl[:, :]).then_inc(semln, 16)
        R.switch("gpsimd", _lncopy, incs=(("sStLN", 16),))
        R.w("gpsimd", "sStLN", S.val("sStLN"))
        bar1 = self.barrier()
        self.fence()
        self.qkv(bar1)
        self.fence()
        self.attn()
        R.w("gpsimd", "sStO0", S.val("sStO0"))
        R.w("gpsimd", "sStO1", S.val("sStO1"))
        semoc = S.h["sStO0"]
        HCD = self.c.HC * 128
        def _occopy(e, env, rho):
            e.dma_start(out=self.sh_ocat[rho * HCD:(rho + 1) * HCD, :],
                        in_=self.d_oTl[:, :]).then_inc(semoc, 16)
        R.switch("gpsimd", _occopy, incs=(("sStO0", 16),))
        R.w("gpsimd", "sStO0", S.val("sStO0"))
        bar2 = self.barrier()
        self.fence()
        self.outproj(bar2)
        self.fence()
        self.ffn_call(self.d_x2, self.y_out, False, bar_w)
        self.fence()
        env = {
            "nc": self.nc,
            "ld_h": S.h["sLd1"], "ld_v": 16,
            "offload": {"sync": EngT.SP, "gpsimd": EngT.Pool},
            "par_ap": self.POFF[0:1, 3:4],
            "prelude": {"sync": lambda e, env: e.dma_start(
                out=self.POFF, in_=self.poffs[:, :]).then_inc(S.h["sLd1"], 16)},
        }
        with self.nc.Block() as block:
            self.R.replay(block, env)
        return self.nc


def host_in_maps(c: CFG, x_full, qkv_w, qkv_b, out_w, out_b, W1, W2):
    DM, DH, HC, TOK, HALF_F = c.DM, c.DH, c.HC, c.TOK, c.DFF // 2
    H = c.H
    w1T = np.ascontiguousarray(W1.T)
    w2T = np.ascontiguousarray(W2.T)
    outwT = np.ascontiguousarray(out_w.T)
    in_maps = []
    for core in range(8):
        p, rho = core // 2, core % 2
        heads = [rho * HC + hh for hh in range(HC)]
        rows = []
        for kind in range(3):
            for hh in heads:
                rows.extend(range(kind * H * DH + hh * DH, kind * H * DH + (hh + 1) * DH))
        rows = np.array(rows)
        qkvwT = np.ascontiguousarray(qkv_w[rows, :].T)
        qkvb_s = np.ascontiguousarray(qkv_b[rows])[None, :]
        poffs = np.array([[rho * TOK, rho * (DM // 2), rho * HALF_F, rho,
                           rho * 2, rho * HALF_F, 0, 0]], dtype=np.int32)
        in_maps.append({
            "x": np.ascontiguousarray(x_full[p, rho * TOK:(rho + 1) * TOK, :]),
            "qkvwT": qkvwT, "qkvb": qkvb_s,
            "outwT": outwT, "outb": np.ascontiguousarray(out_b)[None, :],
            "w1Th": np.ascontiguousarray(w1T[:, rho * HALF_F:(rho + 1) * HALF_F]),
            "w2Th": np.ascontiguousarray(w2T[rho * HALF_F:(rho + 1) * HALF_F, :]),
            "poffs": poffs,
        })
    return in_maps


def host_gather(c: CFG, results, B):
    out = np.zeros((B, c.T, c.DM), dtype=np.float32)
    for core in range(8):
        p, rho = core // 2, core % 2
        out[p, rho * c.TOK:(rho + 1) * c.TOK, :] = results[core]["y"]
    return out




def kernel(L0_attn_x, L0_attn_qkv_w, L0_attn_qkv_b, L0_attn_out_w, L0_attn_out_b,
           L0_ffn_W1, L0_ffn_W2):
    c = CFG()
    mk = MK(c)
    nc = mk.build()
    in_maps = host_in_maps(c, np.asarray(L0_attn_x), np.asarray(L0_attn_qkv_w),
                           np.asarray(L0_attn_qkv_b), np.asarray(L0_attn_out_w),
                           np.asarray(L0_attn_out_b), np.asarray(L0_ffn_W1),
                           np.asarray(L0_ffn_W2))
    res = run_bass_kernel_spmd(nc, in_maps, core_ids=list(range(8)))
    return host_gather(c, res.results, 4)


# version 02
# speedup vs baseline: 1.0442x; 1.0442x over previous
import numpy as np
import ml_dtypes
import concourse.bass as bass
import concourse.mybir as mybir
from concourse.bass_utils import run_bass_kernel_spmd

F32 = mybir.dt.float32
BF16 = mybir.dt.bfloat16
I8 = mybir.dt.int8
I32 = mybir.dt.int32
AF = mybir.ActivationFunctionType
ALU = mybir.AluOpType
AX = mybir.AxisListType
ds = bass.ds
EngT = mybir.EngineType


class CFG:
    B = 4
    T = 4096
    DM = 2048
    H = 16
    DH = 128
    DFF = 8192
    G = 256
    EPS = 1e-5

    @property
    def TOK(self): return self.T // 2
    @property
    def HC(self): return self.H // 2
    @property
    def DCH(self): return self.DM // 128
    @property
    def FCH(self): return self.DFF // 128
    @property
    def F512(self): return self.DFF // 512
    @property
    def NG(self): return self.TOK // self.G
    @property
    def NT(self): return self.G // 128
    @property
    def QT(self): return self.T // 128
    @property
    def OUTF(self): return 3 * self.HC * self.DH


cfg = CFG()

INV_SQRT_D_HI = float(np.float32(1.0 / np.sqrt(128.0)))
INV_SQRT_D_LO = float(np.float32(1.0 / np.sqrt(np.float64(128.0)) - np.float64(np.float32(1.0 / np.sqrt(128.0)))))
TWO_PI = 2.0 * np.pi
_c1 = float(np.float32(6.28125))
_c2 = float(np.float32(TWO_PI - _c1))
_c3 = float(np.float32(TWO_PI - _c1 - np.float64(np.float32(TWO_PI - _c1))))
HALF_PI = np.pi / 2.0
_h1 = float(np.float32(1.5703125))
_h2 = float(np.float32(HALF_PI - _h1))
INV_TWO_PI = float(np.float32(1.0 / TWO_PI))


class Sems:
    def __init__(self, nc, ctx):
        self.nc = nc
        self._ctx = ctx
        self.h = {}
        self.n = {}

    def new(self, name):
        self.h[name] = self._ctx.enter_context(self.nc.semaphore(name))
        self.n[name] = 0
        return name

    def inc(self, name, inst, k=1):
        inst.then_inc(self.h[name], k)
        self.n[name] += k
        return self.n[name]

    def val(self, name):
        return self.n[name]

    def wait(self, eng, name, v=None):
        eng.wait_ge(self.h[name], self.n[name] if v is None else v)


ENGS = ("sync", "vector", "scalar", "tensor", "gpsimd")


class Rec:
    def __init__(self, sems):
        self.q = {k: [] for k in ENGS}
        self.sems = sems
        self.trace = {k: [] for k in ENGS}

    @staticmethod
    def _site():
        return "?"

    def o(self, eng, fn, inc=None, k=1):
        self.trace[eng].append(("inc" if inc else "op", inc, k, self._site()))
        if inc is not None:
            self.sems.n[inc] += k
            h = self.sems.h[inc]
            def wrapped(e, env, fn=fn, h=h, k=k):
                fn(e, env).then_inc(h, k)
            self.q[eng].append(wrapped)
        else:
            self.q[eng].append(lambda e, env, fn=fn: fn(e, env))
        return self.sems.n[inc] if inc is not None else None

    def w(self, eng, sem, v=None):
        val = self.sems.n[sem] if v is None else v
        h = self.sems.h[sem]
        self.trace[eng].append(("wait", sem, val, self._site()))
        self.q[eng].append(lambda e, env, h=h, val=val: e.wait_ge(h, val))

    def drain(self, eng):
        self.trace[eng].append(("drain", None, 0, self._site()))
        self.q[eng].append(lambda e, env: e.drain())

    def switch(self, eng, body, incs=()):
        for s, k in incs:
            self.sems.n[s] += k
            self.trace[eng].append(("inc", s, k, self._site()))
        def fn(e, env, body=body):
            for rho in env["nc"].Switch(engines=[e], index=[env["pars"][env["_engname"]]], n=2):
                body(e, env, rho)
        self.q[eng].append(("SWITCH", fn))

    def replay(self, block, env):
        def mk(name):
            def body(e):
                pre = env.get("prelude", {}).get(name)
                if pre is not None:
                    pre(e, env)
                if name in env.get("offload", {}):
                    e.wait_ge(env["ld_h"], env["ld_v"])
                    _, vals = env["nc"].values_load_multi_w_load_instructions(
                        env["par_ap"], engines=[env["offload"][name]], min_val=0,
                        max_val=1, skip_runtime_bounds_check=True)
                    env.setdefault("pars", {})[name] = vals[0]
                env["_engname"] = name
                for fn in self.q[name]:
                    if isinstance(fn, tuple):
                        fn[1](e, env)
                    else:
                        fn(e, env)
            return body
        for name in ENGS:
            getattr(block, name)(mk(name))


def ceil_div(a, b):
    return (a + b - 1) // b


def _noop():
    pass


class Arena:
    def __init__(self, nc, bytes_per_part):
        self.t = nc.alloc_sbuf_tensor("arena", [128, bytes_per_part], I8)
        self.views = {F32: self.t.bitcast(F32), BF16: self.t.bitcast(BF16), I8: self.t, I32: self.t.bitcast(I32)}
        self.bpp = bytes_per_part

    def v(self, dtype, byte_off, cols, rows=128):
        iz = {F32: 4, BF16: 2, I8: 1, I32: 4}[dtype]
        assert byte_off % iz == 0 and byte_off + cols * iz <= self.bpp, (byte_off, cols, dtype)
        e0 = byte_off // iz
        ap = self.views[dtype].ap()[0:rows, e0:e0 + cols]
        return ap



class MK:

    def __init__(self, c: CFG):
        self.c = c
        nc = bass.Bass("TRN2", target_bir_lowering=False)
        self.nc = nc
        TOK, DM, T, DFF, OUTF = c.TOK, c.DM, c.T, c.DFF, c.OUTF
        HALF_F = DFF // 2
        self.x_in = nc.dram_tensor("x", [TOK, DM], F32, kind="ExternalInput").ap()
        self.qkvwT = nc.dram_tensor("qkvwT", [DM, OUTF], F32, kind="ExternalInput").ap()
        self.qkvb = nc.dram_tensor("qkvb", [1, OUTF], F32, kind="ExternalInput").ap()
        self.outwT = nc.dram_tensor("outwT", [DM, DM], F32, kind="ExternalInput").ap()
        self.outb = nc.dram_tensor("outb", [1, DM], F32, kind="ExternalInput").ap()
        self.w1Th = nc.dram_tensor("w1Th", [DM, HALF_F], F32, kind="ExternalInput").ap()
        self.w2Th = nc.dram_tensor("w2Th", [HALF_F, DM], F32, kind="ExternalInput").ap()
        self.poffs = nc.dram_tensor("poffs", [1, 8], I32, kind="ExternalInput").ap()
        self.y_out = nc.dram_tensor("y", [TOK, DM], F32, kind="ExternalOutput").ap()
        self.d_x1 = nc.dram_tensor("d_x1", [TOK, DM], F32).ap()
        self.d_x2 = nc.dram_tensor("d_x2", [TOK, DM], F32).ap()
        self.d_qkv = nc.dram_tensor("d_qkv", [OUTF, T], F32).ap()
        self.d_emb = nc.dram_tensor("d_emb", [256, T], F32).ap()
        self.bar_in = nc.dram_tensor("bar_in", [1, 8], I32).ap()
        self.bar_out = nc.dram_tensor("bar_out", [8, 8], I32).ap()
        self.d_ln2Tl = nc.dram_tensor("d_ln2Tl", [DM, TOK], F32).ap()
        self.d_oTl = nc.dram_tensor("d_oTl", [c.HC * 128, T], F32).ap()
        self.d_ocl = nc.dram_tensor("d_ocl", [DM, TOK], F32).ap()
        self.d_t1Tl = nc.dram_tensor("d_t1Tl", [DM, HALF_F], BF16).ap()
        self.d_t2Tl = nc.dram_tensor("d_t2Tl", [HALF_F, DM], BF16).ap()
        self.cc_in = nc.dram_tensor("cc_in", [1, 2], F32).ap()
        self.cc_out = nc.dram_tensor("cc_out", [1, 2], F32).ap()
        self.sh_ln2T = nc.dram_tensor("sh_ln2T", [DM, T], F32, addr_space="Shared").ap()
        self.sh_ocat = nc.dram_tensor("sh_ocat", [DM, T], F32, addr_space="Shared").ap()
        self.sh_t1T = nc.dram_tensor("sh_t1T", [DM, DFF], BF16, addr_space="Shared").ap()
        self.sh_t2T = nc.dram_tensor("sh_t2T", [DFF, DM], BF16, addr_space="Shared").ap()

        from contextlib import ExitStack
        self.ctx = ExitStack()
        self.sems = Sems(nc, self.ctx)
        self.DMA_SEMS = ("sLd1", "sLdX", "sWA0", "sWA1", "sWB0", "sWB1", "sLdQW",
                         "sQL0", "sQL1", "sLdQKV", "sLdOC", "sWO0", "sWO1",
                         "sStCX", "sStLN", "sStQ0", "sStQ1", "sStO0", "sStO1",
                         "sStX2", "sStEMB", "sStW", "sStW2", "sLd2")
        for s in ("sT", "sV", "sS", "sG", "sCC") + self.DMA_SEMS:
            self.sems.new(s)
        self.R = Rec(self.sems)
        self.ar = Arena(nc, 196 * 1024)
        self._psum = [self.ctx.enter_context(nc.psum_tensor("pb%d" % i, [128, 512], F32))
                      for i in range(8)]
        self.pguard = [None] * 8
        self._arena_map()

    def _arena_map(self):
        c, ar = self.c, self.ar
        NT, DM, DFF, T, G, DCH, FCH, OUTF = c.NT, c.DM, c.DFF, c.T, c.G, c.DCH, c.FCH, c.OUTF
        v = ar.v
        self.ID32 = v(F32, 0, 128); self.IDBF = v(BF16, 512, 128)
        self.MASK = v(F32, 768, 128); self.IOTA = v(I32, 1280, 128)
        self.ONESB = v(F32, 1792, 128)
        self.ONESC = v(F32, 1792, 1)
        self.ONESR = v(F32, 1792, 128, rows=1)
        self.POFF = v(I32, 2304, 8, rows=1)
        self._stat_base = 2368
        self._pers_end = self._stat_base + 4 * 96
        assert self._pers_end <= 4096
        FFN = 4096
        def seq(base, *sizes):
            offs = []
            o = base
            for s in sizes:
                offs.append(o); o += s
            offs.append(o)
            return offs
        (self.A_H, self.B_LN, self.C_X, self.D_ST, self.QT8, self.HQ,
         self.HT, self.WA, self.WB, ffn_end) = seq(
            FFN, NT * DFF * 4, max(DM * 4, DFF), NT * DM * 4,
            max(DCH * 128 * 4, DM * 2 + DM), DCH * G * 2,
            DFF * 2, FCH * G * 2, 2 * DCH * 512 * 2, 4 * 512 * 2)
        self.I8A = self.B_LN
        self.QB = self.D_ST
        self.I8A_A = self.D_ST + self.c.DM * 2
        assert ffn_end <= ar.bpp, ffn_end
        self.OGT = min(6, OUTF // 128)
        (self.QWT, self.QLN, self.QEMB, self.QST, qkv_end) = seq(
            FFN, DCH * self.OGT * 128 * 4, 2 * DCH * 512 * 4, 2 * T * 4, 2 * 512 * 4)
        assert qkv_end <= ar.bpp, qkv_end
        (self.AQ, self.AK, self.AVV, self.AVT, self.APS, self.APT, self.AO,
         self.AOS, self.AP0, att_end) = seq(
            FFN, T * 4, T * 4, T * 4, T * 4, T * 4, T * 4, T * 4, 1024 * 4, 4096 * 4)
        assert att_end <= ar.bpp, att_end
        self.OPG = min(512, c.TOK)
        (self.OC, self.WO, self.X1G, self.X2S, self.OB, op_end) = seq(
            FFN, DCH * self.OPG * 4, 2 * 512 * 4, (self.OPG // 128) * DM * 4,
            (self.OPG // 128) * DM * 4, DM * 4)
        assert op_end <= ar.bpp, op_end

    def stat(self, i, n=1, rows=128):
        return self.ar.v(F32, self._stat_base + 4 * i, n, rows=rows)

    def mm(self, out_ap, lhsT, rhs, start, stop, inc=None):
        return self.R.o("tensor", lambda e, _: e.matmul(out_ap, lhsT, rhs, start=start, stop=stop),
                        inc=inc)

    def fence(self, name=""):
        R, sems = self.R, self.sems
        eng_sem = {"vector": "sV", "scalar": "sS", "tensor": "sT", "gpsimd": "sG"}
        vals = {}
        for eng, s in eng_sem.items():
            R.o(eng, lambda e, _: e.drain(), inc=s)
            vals[eng] = sems.val(s)
        vals_dma = {s: sems.val(s) for s in self.DMA_SEMS}
        for eng in ENGS:
            for other, s in eng_sem.items():
                if other != eng:
                    R.w(eng, s, vals[other])
            for s, vv in vals_dma.items():
                R.w(eng, s, vv)

    def barrier(self):
        R = self.R
        v = R.o("gpsimd", lambda e, _: e.collective_compute(
            "AllGather", ALU.bypass, replica_groups=[list(range(8))],
            ins=[self.bar_in], outs=[self.bar_out]), inc="sCC")
        R.w("gpsimd", "sCC", v)
        return v

    def vop(self, fn, inc="sV"):
        self.R.drain("vector")
        return self.R.o("vector", fn, inc=inc)

    def transp_batch(self, src_ap_fn, n_tiles, dst_base_fn, banks, bf16=False,
                     evac_eng="vector", evac_scale=None):
        R, S = self.R, self.sems
        ident = self.IDBF if bf16 else self.ID32
        nb = len(banks)
        for i0 in range(0, n_tiles, 4):
            w = min(4, n_tiles - i0)
            bk = banks[(i0 // 4) % nb]
            psbank = self._psum[bk]
            psap = psbank.bitcast(BF16).ap() if bf16 else psbank.ap()
            if self.pguard[bk] is not None:
                sem, val = self.pguard[bk]
                R.w("tensor", sem, val)
            for j in range(w):
                tv = self.R.o("tensor",
                              lambda e, _, j=j, i0=i0, psap=psap: e.transpose(
                                  psap[:, j * 128:(j + 1) * 128], src_ap_fn(i0 + j), ident),
                              inc="sT")
            R.w(evac_eng, "sT", tv)
            dst = dst_base_fn(i0, w * 128)
            if evac_scale is not None:
                ev = R.o(evac_eng, lambda e, _, psap=psap, dst=dst, w=w: e.activation(
                    out=dst, in_=psap[:, 0:w * 128], func=AF.Copy, scale=evac_scale),
                    inc="sS" if evac_eng == "scalar" else "sV")
            elif evac_eng == "scalar":
                ev = R.o("scalar", lambda e, _, psap=psap, dst=dst, w=w: e.copy(
                    dst, psap[:, 0:w * 128]), inc="sS")
            else:
                ev = R.o("vector", lambda e, _, psap=psap, dst=dst, w=w: e.tensor_copy(
                    dst, psap[:, 0:w * 128]), inc="sV")
            self.pguard[bk] = (("sS" if evac_eng == "scalar" else "sV"), ev)

    def quant(self, x_ap, s_ap, out_ap, i8_ap, u_ap, clip=False):
        R, S = self.R, self.sems
        R.w("scalar", "sG", S.val("sG"))
        R.w("scalar", "sV", S.val("sV"))
        R.w("gpsimd", "sT", S.val("sT"))
        vS = R.o("scalar", lambda e, _: e.activation(out=i8_ap, in_=x_ap, func=AF.Copy,
                                                     scale=s_ap, bias=-1.0), inc="sS")
        R.w("vector", "sS", vS)
        vV = self.vop(lambda e, _: e.scalar_tensor_tensor(
            out=x_ap, in0=x_ap, scalar=s_ap, in1=i8_ap, op0=ALU.mult, op1=ALU.subtract))
        R.w("gpsimd", "sV", vV)
        vG = R.o("gpsimd", lambda e, _: e.tensor_scalar(
            out=u_ap, in0=x_ap, scalar1=0.5, scalar2=None, op0=ALU.is_ge), inc="sG")
        R.w("vector", "sG", vG)
        vV2 = self.vop(lambda e, _: e.scalar_tensor_tensor(
            out=x_ap, in0=x_ap, scalar=1.5, in1=u_ap, op0=ALU.is_ge, op1=ALU.add))
        R.w("gpsimd", "sV", vV2)
        if clip:
            vG2 = R.o("gpsimd", lambda e, _: e.tensor_tensor(
                out=u_ap, in0=i8_ap, in1=x_ap, op=ALU.add), inc="sG")
            R.w("vector", "sG", vG2)
            vfin = self.vop(lambda e, _: e.tensor_scalar(
                out=out_ap, in0=u_ap, scalar1=-1.0, scalar2=1.0, op0=ALU.max, op1=ALU.min))
            return ("sV", vfin)
        vG2 = R.o("gpsimd", lambda e, _: e.tensor_tensor(
            out=out_ap, in0=i8_ap, in1=x_ap, op=ALU.add), inc="sG")
        return ("sG", vG2)

    def ln_tile(self, x_ap, out_ap, sq_scratch_ap, sb):
        R, DM, EPS = self.R, self.c.DM, self.c.EPS
        st_sum, st_mu, st_msq, st_vpe, st_inv, st_nb = (self.stat(sb + i) for i in range(6))
        self.vop(lambda e, _: e.tensor_reduce(out=st_sum, in_=x_ap, axis=AX.X, op=ALU.add))
        vS = R.o("scalar", lambda e, _: e.activation(out=sq_scratch_ap, in_=x_ap,
                                                     func=AF.Square, accum_out=st_msq), inc="sS")
        R.w("vector", "sS", vS)
        self.vop(lambda e, _: e.tensor_scalar(out=st_mu, in0=st_sum, scalar1=1.0 / DM,
                                              scalar2=None, op0=ALU.mult))
        self.vop(lambda e, _: e.tensor_scalar(out=st_msq, in0=st_msq, scalar1=1.0 / DM,
                                              scalar2=None, op0=ALU.mult))
        self.vop(lambda e, _: e.scalar_tensor_tensor(out=st_vpe, in0=st_mu, scalar=st_mu,
                                                     in1=st_msq, op0=ALU.mult, op1=ALU.subtract))
        self.vop(lambda e, _: e.tensor_scalar(out=st_vpe, in0=st_vpe, scalar1=-1.0,
                                              scalar2=EPS, op0=ALU.mult, op1=ALU.add))
        vV = self.vop(lambda e, _: e.reciprocal(out=st_vpe, in_=st_vpe))
        R.w("scalar", "sV", vV)
        vS2 = R.o("scalar", lambda e, _: e.activation(out=st_inv, in_=st_vpe,
                                                      func=AF.Sqrt), inc="sS")
        R.w("vector", "sS", vS2)
        self.vop(lambda e, _: e.tensor_scalar(out=st_nb, in0=st_mu, scalar1=st_inv,
                                              scalar2=-1.0, op0=ALU.mult, op1=ALU.mult))
        va = self.vop(lambda e, _: e.tensor_scalar(out=out_ap, in0=x_ap, scalar1=st_inv,
                                                   scalar2=st_nb, op0=ALU.mult, op1=ALU.add))
        return ("sV", va), st_inv

    def p0(self):
        c, R, S, ar = self.c, self.R, self.sems, self.ar
        nc = self.nc
        T, DM, DFF = c.T, c.DM, c.DFF
        HALF_F = DFF // 2
        R.w("gpsimd", "sLd1", 16)
        bi = R.o("gpsimd", lambda e, _: e.dma_start(out=self.bar_in[:, :], in_=self.POFF),
                 inc="sStW", k=16)
        R.w("gpsimd", "sStW", bi)
        vG = R.o("gpsimd", lambda e, _: e.iota(self.IOTA, pattern=[[-1, 128]], base=0,
                                               channel_multiplier=1), inc="sG")
        R.o("gpsimd", lambda e, _: e.memset(self.ONESB, 1.0), inc="sG")
        R.w("vector", "sG", S.val("sG"))
        self.vop(lambda e, _: e.tensor_scalar(out=self.ID32, in0=self.IOTA, scalar1=0.0,
                                              scalar2=None, op0=ALU.is_equal))
        self.vop(lambda e, _: e.tensor_copy(self.IDBF, self.ID32))
        self.vop(lambda e, _: e.tensor_scalar(out=self.MASK, in0=self.IOTA, scalar1=0.0,
                                              scalar2=-1e30, op0=ALU.is_lt, op1=ALU.mult))
        ANG = ar.v(F32, self.A_H, T)
        EMB = ar.v(F32, self.A_H + T * 4, T)
        KIF = ar.v(I32, self.A_H + 2 * T * 4, T)
        KF = ar.v(F32, self.A_H + 3 * T * 4, T)
        PCOL = ar.v(I32, self.A_H + 4 * T * 4, 1)
        st_pf, st_u, st_fr, st_o1, st_o2 = (self.stat(80 + i) for i in range(5))
        vG2 = R.o("gpsimd", lambda e, _: e.iota(ar.v(I32, self.A_H, T), pattern=[[1, T]],
                                                base=0, channel_multiplier=0), inc="sG")
        R.o("gpsimd", lambda e, _: e.iota(PCOL, pattern=[[0, 1]], base=0,
                                          channel_multiplier=1), inc="sG")
        R.w("vector", "sG", S.val("sG"))
        self.vop(lambda e, _: e.tensor_copy(st_pf, PCOL))
        self.vop(lambda e, _: e.tensor_scalar(out=st_u, in0=st_pf, scalar1=64.0, scalar2=None,
                                              op0=ALU.is_ge))
        self.vop(lambda e, _: e.tensor_scalar(out=st_fr, in0=st_pf, scalar1=1.0 / 64.0,
                                              scalar2=None, op0=ALU.mult))
        self.vop(lambda e, _: e.tensor_tensor(out=st_fr, in0=st_fr, in1=st_u, op=ALU.subtract))
        self.vop(lambda e, _: e.tensor_scalar(out=st_o1, in0=st_u, scalar1=_h1, scalar2=None,
                                              op0=ALU.mult))
        self.vop(lambda e, _: e.tensor_scalar(out=st_o2, in0=st_u, scalar1=_h2, scalar2=None,
                                              op0=ALU.mult))
        self.vop(lambda e, _: e.tensor_scalar(out=ANG, in0=ar.v(I32, self.A_H, T),
                                              scalar1=st_fr, scalar2=st_o1,
                                              op0=ALU.mult, op1=ALU.add))
        self.vop(lambda e, _: e.tensor_scalar(out=ANG, in0=ANG, scalar1=st_o2, scalar2=None,
                                              op0=ALU.add))
        self.vop(lambda e, _: e.tensor_scalar(out=KF, in0=ANG, scalar1=INV_TWO_PI,
                                              scalar2=None, op0=ALU.mult))
        self.vop(lambda e, _: e.tensor_copy(KIF, KF))
        self.vop(lambda e, _: e.tensor_copy(KF, KIF))
        for cc in (_c1, _c2, _c3):
            self.vop(lambda e, _, cc=cc: e.scalar_tensor_tensor(
                out=ANG, in0=KF, scalar=-cc, in1=ANG, op0=ALU.mult, op1=ALU.add))
        self.vop(lambda e, _: e.tensor_scalar(out=KF, in0=ANG, scalar1=float(np.pi),
                                              scalar2=None, op0=ALU.is_gt))
        for cc in (_c1, _c2, _c3):
            self.vop(lambda e, _, cc=cc: e.scalar_tensor_tensor(
                out=ANG, in0=KF, scalar=-cc, in1=ANG, op0=ALU.mult, op1=ALU.add))
        vz = S.val("sV")
        R.w("scalar", "sV", vz)
        vS = R.o("scalar", lambda e, _: e.activation(out=EMB, in_=ANG, func=AF.Sin), inc="sS")
        R.w("vector", "sS", vS)
        self.vop(lambda e, _: e.tensor_scalar(out=ANG, in0=EMB, scalar1=INV_SQRT_D_HI,
                                              scalar2=None, op0=ALU.mult))
        vq = self.vop(lambda e, _: e.scalar_tensor_tensor(out=ANG, in0=EMB,
                                                          scalar=INV_SQRT_D_LO, in1=ANG,
                                                          op0=ALU.mult, op1=ALU.add))
        R.w("gpsimd", "sV", vq)
        R.o("gpsimd", lambda e, _: e.dma_start(out=self.d_emb[0:128, :], in_=ANG), inc="sStEMB", k=16)
        R.o("gpsimd", lambda e, _: e.dma_start(out=self.d_emb[128:256, :], in_=EMB), inc="sStEMB", k=16)

        st_a1, st_a2, st_t = self.stat(86), self.stat(87), self.stat(88)
        st_m = self.stat(89, n=4, rows=1)
        R.w("sync", "sStEMB", S.val("sStEMB"))
        nch1 = DM // 128
        nch2 = HALF_F // 128
        ldsem = ("sLd1", "sLd2")

        def mean_pass(nch, src, cols, st_acc):
            bufs = [ar.v(F32, self.A_H, cols), ar.v(F32, self.A_H + HALF_F * 4, cols)]
            ev = S.val("sV")
            vguard = [ev, ev]
            for i in range(nch):
                b = i % 2
                R.w("sync", ldsem[b], S.val(ldsem[b]))
                if vguard[b]:
                    R.w("sync", "sV", vguard[b])
                ld = R.o("sync", lambda e, _, i=i, b=b: e.dma_start(
                    out=bufs[b], in_=src[i * 128:(i + 1) * 128, :]), inc=ldsem[b], k=16)
                R.w("vector", ldsem[b], ld)
                self.vop(lambda e, _, b=b: e.tensor_reduce(
                    out=st_t, in_=bufs[b], axis=AX.X, op=ALU.add,
                    apply_absolute_value=True))
                if i == 0:
                    self.vop(lambda e, _: e.tensor_copy(st_acc, st_t))
                else:
                    self.vop(lambda e, _: e.tensor_tensor(out=st_acc, in0=st_acc, in1=st_t,
                                                          op=ALU.add))
                vguard[b] = S.val("sV")

        mean_pass(nch1, self.w1Th, HALF_F, st_a1)
        mean_pass(nch2, self.w2Th, DM, st_a2)
        ps7 = self._psum[7]
        vv = S.val("sV")
        R.w("tensor", "sV", vv)
        self.mm(ps7.ap()[0:1, 0:1], self.ONESC, st_a1, True, True, inc="sT")
        t2 = self.mm(ps7.ap()[0:1, 1:2], self.ONESC, st_a2, True, True, inc="sT")
        R.w("vector", "sT", t2)
        vcp = self.vop(lambda e, _: e.tensor_copy(st_m[0:1, 0:2], ps7.ap()[0:1, 0:2]))
        self.pguard[7] = ("sV", vcp)
        R.w("gpsimd", "sV", vcp)
        wv = R.o("gpsimd", lambda e, env: e.dma_start(
            out=self.cc_in[0:1, 0:2], in_=st_m[0:1, 0:2]), inc="sStW", k=16)
        R.w("gpsimd", "sStW", wv)
        cv = R.o("gpsimd", lambda e, _: e.collective_compute(
            "AllReduce", ALU.add, replica_groups=[list(range(8))],
            ins=[self.cc_in], outs=[self.cc_out]), inc="sCC")
        R.w("sync", "sCC", S.val("sCC"))
        ld = R.o("sync", lambda e, _: e.dma_start(out=st_m[0:1, 0:2], in_=self.cc_out[0:1, 0:2]),
                 inc="sLd2", k=16)
        R.w("vector", "sLd2", ld)
        st_mm = self.stat(93, n=4, rows=1)
        self.vop(lambda e, _: e.tensor_scalar(out=st_mm[0:1, 0:2], in0=st_m[0:1, 0:2],
                                              scalar1=1.0 / (4.0 * DM * DFF), scalar2=1e-5,
                                              op0=ALU.mult, op1=ALU.max))
        vr = self.vop(lambda e, _: e.reciprocal(out=st_mm[0:1, 2:4], in_=st_mm[0:1, 0:2]))
        R.w("tensor", "sV", vr)
        tb = self.mm(ps7.ap()[:, 0:4], self.ONESR, st_mm[0:1, 0:4], True, True, inc="sT")
        R.w("vector", "sT", tb)
        self.M1B = self.stat(94)
        self.M2B = self.stat(95)
        self.SW1B = self.stat(76)
        self.SW2B = self.stat(77)
        self.M1B127 = self.stat(78)
        self.M2B127h = self.stat(79)
        vcp2 = self.vop(lambda e, _: e.tensor_copy(self.M1B, ps7.ap()[:, 0:1]))
        self.vop(lambda e, _: e.tensor_copy(self.M2B, ps7.ap()[:, 1:2]))
        self.vop(lambda e, _: e.tensor_copy(self.SW1B, ps7.ap()[:, 2:3]))
        vcp3 = self.vop(lambda e, _: e.tensor_copy(self.SW2B, ps7.ap()[:, 3:4]))
        self.pguard[7] = ("sV", vcp3)
        self.vop(lambda e, _: e.tensor_scalar(out=self.M1B127, in0=self.M1B,
                                              scalar1=1.0 / 127.0, scalar2=None, op0=ALU.mult))
        self.vop(lambda e, _: e.tensor_scalar(out=self.M2B127h, in0=self.M2B,
                                              scalar1=0.5 / 127.0, scalar2=None, op0=ALU.mult))

        stsem = ("sStW", "sStW2")

        def tern_pass(nch, src, dst, cols, sw_b):
            WCb = [ar.v(F32, self.A_H, cols), ar.v(F32, self.A_H + HALF_F * 4, cols)]
            WQb = [ar.v(BF16, self.A_H + 2 * HALF_F * 4 + j * HALF_F * 2, cols)
                   for j in range(2)]
            WI8b = [ar.v(I8, self.A_H + 3 * HALF_F * 4 + j * HALF_F, cols)
                    for j in range(2)]
            eg, ev = S.val("sG"), S.val("sV")
            g_guard = [eg, eg]
            st_guard = [S.val(stsem[0]), S.val(stsem[1])]
            for i in range(nch):
                b = i % 2
                R.w("sync", ldsem[b], S.val(ldsem[b]))
                if g_guard[b]:
                    R.w("sync", "sG", g_guard[b])
                if i < 2:
                    R.w("sync", "sV", ev)
                ld = R.o("sync", lambda e, _, i=i, b=b: e.dma_start(
                    out=WCb[b], in_=src[i * 128:(i + 1) * 128, :]), inc=ldsem[b], k=16)
                for eng in ("scalar", "vector", "gpsimd"):
                    R.w(eng, ldsem[b], ld)
                if st_guard[b]:
                    R.w("gpsimd", stsem[b], st_guard[b])
                    R.w("vector", stsem[b], st_guard[b])
                sem, v = self.quant(WCb[b], sw_b, WQb[b], WI8b[b], WQb[b], clip=True)
                g_guard[b] = S.val("sG")
                R.w("gpsimd", sem, v)
                st_guard[b] = R.o("gpsimd", lambda e, env, i=i, b=b: e.dma_start(
                    out=dst[i * 128:(i + 1) * 128, :], in_=WQb[b]), inc=stsem[b], k=16)

        tern_pass(nch1, self.w1Th, self.d_t1Tl, HALF_F, self.SW1B)
        tern_pass(nch2, self.w2Th, self.d_t2Tl, DM, self.SW2B)
        R.w("gpsimd", "sStW", S.val("sStW"))
        R.w("gpsimd", "sStW2", S.val("sStW2"))
        semh = S.h["sStW"]
        def _tcopy(e, env, rho):
            e.dma_start(out=self.sh_t1T[:, rho * HALF_F:(rho + 1) * HALF_F],
                        in_=self.d_t1Tl[:, :]).then_inc(semh, 16)
            e.dma_start(out=self.sh_t2T[rho * HALF_F:(rho + 1) * HALF_F, :],
                        in_=self.d_t2Tl[:, :]).then_inc(semh, 16)
        R.switch("gpsimd", _tcopy, incs=(("sStW", 32),))
        R.w("gpsimd", "sStW", S.val("sStW"))
        self.barrier()

    def ffn_call(self, x_src, dst, do_ln2, bar_weights):
        c, R, S, ar = self.c, self.R, self.sems, self.ar
        G, NT, DM, DFF, DCH, FCH, F512, NG, TOK = (c.G, c.NT, c.DM, c.DFF, c.DCH,
                                                   c.FCH, c.F512, c.NG, c.TOK)
        NF512 = DFF // 512
        DOW = min(512, DM)
        WAb = [ar.v(BF16, self.WA + i * DCH * 512 * 2, DCH * 512) for i in range(2)]
        WBb = [ar.v(BF16, self.WB + i * 512 * 2, DOW) for i in range(4)]
        CX = ar.v(F32, self.C_X, NT * DM)
        BLN = ar.v(F32, self.B_LN, DM)
        DST = ar.v(F32, self.D_ST, DM)
        QB8 = ar.v(BF16, self.QB, DM)
        R.w("sync", "sCC", bar_weights)
        for g in range(NG):
            sb = 8
            ldx = 0
            R.w("sync", "sLdX", S.val("sLdX"))
            for tt in range(NT):
                ldx = R.o("sync", lambda e, _, g=g, tt=tt: e.dma_start(
                    out=CX[:, tt * DM:(tt + 1) * DM],
                    in_=x_src[(g * G + tt * 128):(g * G + tt * 128 + 128), :]),
                    inc="sLdX", k=16)
            R.w("scalar", "sV", S.val("sV"))
            for tt in range(NT):
                for eng in ("vector", "scalar"):
                    R.w(eng, "sLdX", ldx)
                R.w("scalar", "sG", S.val("sG"))
                R.w("scalar", "sS", S.val("sS"))
                R.w("scalar", "sV", S.val("sV"))
                R.w("vector", "sG", S.val("sG"))
                xa = CX[:, tt * DM:(tt + 1) * DM]
                (sA, vA), st_inv = self.ln_tile(xa, BLN, BLN, sb + tt * 12)
                st_mx = self.stat(sb + tt * 12 + 6)
                st_sa = self.stat(sb + tt * 12 + 7)
                st_scA = self.stat(sb + tt * 12 + 8)
                if sA != "sV":
                    R.w("vector", sA, vA)
                self.vop(lambda e, _, st_mx=st_mx: e.tensor_reduce(
                    out=st_mx, in_=BLN, axis=AX.X, op=ALU.max, apply_absolute_value=True))
                self.vop(lambda e, _, st_mx=st_mx: e.tensor_scalar(
                    out=st_mx, in0=st_mx, scalar1=1e-5, scalar2=None, op0=ALU.max))
                self.vop(lambda e, _, st_sa=st_sa, st_mx=st_mx: e.reciprocal(out=st_sa, in_=st_mx))
                self.vop(lambda e, _, st_sa=st_sa: e.tensor_scalar(
                    out=st_sa, in0=st_sa, scalar1=127.0, scalar2=None, op0=ALU.mult))
                vscl = self.vop(lambda e, _, st_scA=st_scA, st_mx=st_mx: e.tensor_tensor(
                    out=st_scA, in0=st_mx, in1=self.M1B127, op=ALU.mult))
                R.w("scalar", "sV", vscl)
                R.w("gpsimd", "sStLN", S.val("sStLN"))
                R.w("scalar", "sStLN", S.val("sStLN"))
                R.w("vector", "sStLN", S.val("sStLN"))
                qsem, qv = self.quant(BLN, st_sa, QB8, ar.v(I8, self.I8A_A, DM), QB8)
                R.w("tensor", qsem, qv)
                self.transp_batch(lambda i: QB8[:, i * 128:(i + 1) * 128], DCH,
                                  lambda i0, w, tt=tt: ar.v(BF16, self.QT8, w,)[:, 0:0] if False else
                                  ar.v(BF16, self.QT8 + (tt * DCH + i0) * 128 * 2, w),
                                  banks=(4, 5), bf16=True, evac_eng="vector")
            qt8_ready = S.val("sV")
            R.w("tensor", "sV", qt8_ready)
            qt8_tile = lambda dc, tt: ar.v(BF16, self.QT8 + (tt * DCH + dc) * 128 * 2, 128)
            AH = lambda tt: ar.v(F32, self.A_H + tt * DFF * 4, DFF)
            for fs in range(NF512):
                if fs >= 2:
                    R.w("sync", "sT", self._wa_guard[fs % 2])
                wsem = "sWA0" if fs % 2 == 0 else "sWA1"
                R.w("sync", wsem, S.val(wsem))
                ldw = R.o("sync", lambda e, _, fs=fs: e.dma_start(
                    out=WAb[fs % 2].rearrange("p (c f) -> p c f", c=DCH),
                    in_=self.sh_t1T[:, fs * 512:(fs + 1) * 512].rearrange(
                        "(c p) f -> p c f", p=128)), inc=wsem, k=16)
                R.w("tensor", wsem, ldw)
                for tt in range(NT):
                    bk = (fs * NT + tt) % 4
                    ps = self._psum[bk]
                    if self.pguard[bk] is not None:
                        R.w("tensor", *self.pguard[bk])
                        self.pguard[bk] = None
                    for dc in range(DCH):
                        tv = self.mm(ps.ap()[:, 0:512], qt8_tile(dc, tt),
                                     WAb[fs % 2][:, dc * 512:(dc + 1) * 512],
                                     start=(dc == 0), stop=(dc == DCH - 1),
                                     inc="sT" if dc == DCH - 1 else None)
                    if tt == NT - 1:
                        self._wa_guard[fs % 2] = tv
                    R.w("scalar", "sT", tv)
                    st_scA = self.stat(sb + tt * 12 + 8)
                    sv = R.o("scalar", lambda e, _, ps=ps, fs=fs, tt=tt, st_scA=st_scA:
                             e.activation(out=AH(tt)[:, fs * 512:(fs + 1) * 512],
                                          in_=ps.ap()[:, 0:512], func=AF.Silu,
                                          scale=st_scA), inc="sS")
                    self.pguard[bk] = ("sS", sv)
            hsil_v = S.val("sS")
            for tt in range(NT):
                st_mh = self.stat(sb + tt * 12 + 9)
                st_sh = self.stat(sb + tt * 12 + 10)
                st_sc2 = self.stat(sb + tt * 12 + 11)
                R.w("vector", "sS", hsil_v)
                self.vop(lambda e, _, st_mh=st_mh, tt=tt: e.tensor_reduce(
                    out=st_mh, in_=AH(tt), axis=AX.X, op=ALU.max, apply_absolute_value=True))
                self.vop(lambda e, _, st_mh=st_mh: e.tensor_scalar(
                    out=st_mh, in0=st_mh, scalar1=1e-5, scalar2=None, op0=ALU.max))
                self.vop(lambda e, _, st_sh=st_sh, st_mh=st_mh: e.reciprocal(
                    out=st_sh, in_=st_mh))
                self.vop(lambda e, _, st_sh=st_sh: e.tensor_scalar(
                    out=st_sh, in0=st_sh, scalar1=127.0, scalar2=None, op0=ALU.mult))
                vsc2 = self.vop(lambda e, _, st_sc2=st_sc2, st_mh=st_mh: e.tensor_tensor(
                    out=st_sc2, in0=st_mh, in1=self.M2B127h, op=ALU.mult))
                R.w("scalar", "sV", vsc2)
                HQb = ar.v(BF16, self.HQ, DFF)
                qsem, qv = self.quant(AH(tt), st_sh, HQb, ar.v(I8, self.I8A, DFF), HQb)
                R.w("tensor", qsem, qv)
                self.transp_batch(lambda i: HQb[:, i * 128:(i + 1) * 128], FCH,
                                  lambda i0, w, tt=tt: ar.v(
                                      BF16, self.HT + (tt * FCH + i0) * 128 * 2, w),
                                  banks=(4, 5), bf16=True, evac_eng="vector")
            ht_ready = S.val("sV")
            R.w("tensor", "sV", ht_ready)
            ht_tile = lambda fc, tt: ar.v(BF16, self.HT + (tt * FCH + fc) * 128 * 2, 128)
            for do in range(DM // DOW):
                for fc in range(FCH):
                    wbuf = fc % 4
                    wsem2 = "sWB0" if fc % 2 == 0 else "sWB1"
                    if self._wb_guard[wbuf]:
                        R.w("scalar", "sT", self._wb_guard[wbuf])
                    R.w("scalar", wsem2, S.val(wsem2))
                    ldw2 = R.o("scalar", lambda e, _, fc=fc, do=do, wbuf=wbuf: e.dma_start(
                        out=WBb[wbuf],
                        in_=self.sh_t2T[fc * 128:(fc + 1) * 128, do * DOW:(do + 1) * DOW]),
                        inc=wsem2, k=16)
                    R.w("tensor", wsem2, ldw2)
                    last_in_batch = (fc % 4 == 3) or (fc == FCH - 1)
                    for tt in range(NT):
                        bk = 6 if tt == 0 else 7
                        ps = self._psum[bk]
                        if fc == 0 and self.pguard[bk] is not None:
                            R.w("tensor", *self.pguard[bk])
                            self.pguard[bk] = None
                        tv2 = self.mm(ps.ap()[:, 0:DOW], ht_tile(fc, tt), WBb[wbuf],
                                      start=(fc == 0), stop=(fc == FCH - 1),
                                      inc="sT" if (last_in_batch and tt == NT - 1) else None)
                    if last_in_batch:
                        for b in range(4):
                            self._wb_guard[b] = tv2
                for tt in range(NT):
                    bk = 6 if tt == 0 else 7
                    ps = self._psum[bk]
                    st_sc2 = self.stat(sb + tt * 12 + 11)
                    R.w("vector", "sT", S.val("sT"))
                    R.w("vector", "sS", S.val("sS"))
                    R.w("vector", "sLdX", S.val("sLdX"))
                    ev = self.vop(lambda e, _, ps=ps, tt=tt, do=do, st_sc2=st_sc2:
                                  e.scalar_tensor_tensor(
                        out=CX[:, tt * DM + do * DOW: tt * DM + (do + 1) * DOW],
                        in0=ps.ap()[:, 0:DOW], scalar=st_sc2,
                        in1=CX[:, tt * DM + do * DOW: tt * DM + (do + 1) * DOW],
                        op0=ALU.mult, op1=ALU.add))
                    self.pguard[bk] = ("sV", ev)
            for tt in range(NT):
                R.w("gpsimd", "sV", S.val("sV"))
                stv = R.o("gpsimd", lambda e, _, g=g, tt=tt: e.dma_start(
                    out=dst[(g * G + tt * 128):(g * G + tt * 128 + 128), :],
                    in_=CX[:, tt * DM:(tt + 1) * DM]), inc="sStCX", k=16)
                if do_ln2:
                    R.w("scalar", "sV", S.val("sV"))
                    R.w("scalar", "sT", S.val("sT"))
                    R.w("vector", "sT", S.val("sT"))
                    R.w("scalar", "sG", S.val("sG"))
                    R.w("vector", "sG", S.val("sG"))
                    (sA2, vA2), _ = self.ln_tile(CX[:, tt * DM:(tt + 1) * DM], BLN, BLN,
                                                 56 + tt * 12)
                    R.w("tensor", sA2, vA2)
                    R.w("vector", "sStLN", S.val("sStLN"))
                    self.transp_batch(lambda i: BLN[:, i * 128:(i + 1) * 128], DCH,
                                      lambda i0, w: ar.v(F32, self.D_ST + i0 * 128 * 4, w),
                                      banks=(4, 5), bf16=False, evac_eng="vector")
                    R.w("gpsimd", "sV", S.val("sV"))
                    R.o("gpsimd", lambda e, env, g=g, tt=tt: e.dma_start(
                        out=self.d_ln2Tl.rearrange("(c p) t -> p c t", p=128)[
                            :, :, (g * G + tt * 128):(g * G + tt * 128 + 128)],
                        in_=DST.rearrange("p (c t) -> p c t", c=DCH)), inc="sStLN", k=16)
            R.w("sync", "sStCX", S.val("sStCX"))
            R.w("sync", "sS", S.val("sS"))
            R.w("sync", "sV", S.val("sV"))
            R.w("sync", "sG", S.val("sG"))
            R.w("sync", "sT", S.val("sT"))

    def qkv(self, bar_ln2):
        c, R, S, ar = self.c, self.R, self.sems, self.ar
        T, DM, DCH, OUTF, HC = c.T, c.DM, c.DCH, c.OUTF, c.HC
        OGT = self.OGT
        NTILE = OUTF // 128
        NOG = ceil_div(NTILE, OGT)
        TC = min(512, T)
        EMQ = ar.v(F32, self.QEMB, T)
        EMK = ar.v(F32, self.QEMB + T * 4, T)
        QSTb = [ar.v(F32, self.QST + i * 512 * 4, TC) for i in range(2)]
        QLNb = [ar.v(F32, self.QLN + i * DCH * 512 * 4, DCH * TC) for i in range(2)]
        BIASQ = self.stat(32, n=NTILE)
        l1 = R.o("sync", lambda e, _: e.dma_start(out=EMQ, in_=self.d_emb[0:128, :]), inc="sLd1", k=16)
        l2 = R.o("sync", lambda e, _: e.dma_start(out=EMK, in_=self.d_emb[128:256, :]), inc="sLd1", k=16)
        l3 = R.o("sync", lambda e, _: e.dma_start(
            out=BIASQ, in_=self.qkvb[0, :].rearrange("(o p) -> p o", p=128),
            allow_slow_non_contiguous=True), inc="sLd1", k=16)
        R.w("sync", "sCC", bar_ln2)
        qst_guard = [0, 0]
        qln_guard = [0, 0]
        for og in range(NOG):
            nt = min(OGT, NTILE - og * OGT)
            if og > 0:
                R.w("sync", "sT", S.val("sT"))
            R.w("sync", "sLdQW", S.val("sLdQW"))
            lw = R.o("sync", lambda e, _, og=og, nt=nt: e.dma_start(
                out=ar.v(F32, self.QWT, DCH * nt * 128).rearrange("p (c o) -> p c o", c=DCH),
                in_=self.qkvwT[:, og * OGT * 128: og * OGT * 128 + nt * 128].rearrange(
                    "(c p) o -> p c o", p=128)), inc="sLdQW", k=16)
            for tc in range(T // TC):
                bln = tc % 2
                lnsem = "sQL0" if bln == 0 else "sQL1"
                if qln_guard[bln]:
                    R.w("sync", "sT", qln_guard[bln])
                R.w("sync", lnsem, S.val(lnsem))
                ln = R.o("sync", lambda e, _, tc=tc, bln=bln: e.dma_start(
                    out=QLNb[bln].rearrange("p (c t) -> p c t", c=DCH),
                    in_=self.sh_ln2T[:, tc * TC:(tc + 1) * TC].rearrange(
                        "(c p) t -> p c t", p=128)), inc=lnsem, k=16)
                for ot in range(nt):
                    gt = og * OGT + ot
                    bk = (tc * nt + ot) % 2
                    ps = self._psum[bk]
                    if self.pguard[bk] is not None:
                        R.w("tensor", *self.pguard[bk])
                        self.pguard[bk] = None
                    R.w("tensor", "sLdQW", lw)
                    R.w("tensor", lnsem, ln)
                    R.w("tensor", "sLd1", l3)
                    for dc in range(DCH):
                        tv = self.mm(ps.ap()[:, 0:TC],
                                     ar.v(F32, self.QWT + (dc * nt + ot) * 128 * 4, 128),
                                     QLNb[bln][:, dc * TC:(dc + 1) * TC],
                                     start=(dc == 0), stop=(dc == DCH - 1),
                                     inc="sT" if dc == DCH - 1 else None)
                    if ot == nt - 1:
                        qln_guard[bln] = tv
                    sbuf_i = (tc * nt + ot) % 2
                    if qst_guard[sbuf_i]:
                        R.w("vector", "sStQ0" if sbuf_i == 0 else "sStQ1", qst_guard[sbuf_i])
                    R.w("vector", "sT", tv)
                    R.w("vector", "sLd1", max(l1, l2, l3))
                    kind = gt // HC
                    if kind < 2:
                        emb = EMQ if kind == 0 else EMK
                        vv = self.vop(lambda e, _, ps=ps, gt=gt, sbuf_i=sbuf_i, emb=emb, tc=tc:
                                      e.scalar_tensor_tensor(
                            out=QSTb[sbuf_i], in0=ps.ap()[:, 0:TC], scalar=BIASQ[:, gt:gt + 1],
                            in1=emb[:, tc * TC:(tc + 1) * TC], op0=ALU.add, op1=ALU.mult))
                    else:
                        vv = self.vop(lambda e, _, ps=ps, gt=gt, sbuf_i=sbuf_i: e.tensor_scalar(
                            out=QSTb[sbuf_i], in0=ps.ap()[:, 0:TC], scalar1=BIASQ[:, gt:gt + 1],
                            scalar2=None, op0=ALU.add))
                    self.pguard[bk] = ("sV", vv)
                    dep = ("sV", vv)
                    R.w("gpsimd", *dep)
                    gv = R.o("gpsimd", lambda e, _, gt=gt, tc=tc, sbuf_i=sbuf_i: e.dma_start(
                        out=self.d_qkv[gt * 128:(gt + 1) * 128, tc * TC:(tc + 1) * TC],
                        in_=QSTb[sbuf_i]), inc="sStQ0" if sbuf_i == 0 else "sStQ1", k=16)
                    qst_guard[sbuf_i] = gv

    def attn(self):
        c, R, S, ar = self.c, self.R, self.sems, self.ar
        T, HC, QT = c.T, c.HC, c.QT
        NKT = T // 128
        AQ = ar.v(F32, self.AQ, T); AK = ar.v(F32, self.AK, T)
        AVV = ar.v(F32, self.AVV, T); AVT = ar.v(F32, self.AVT, T)
        APS = ar.v(F32, self.APS, T); APT = ar.v(F32, self.APT, T)
        AO = ar.v(F32, self.AO, T)
        AOSb = [ar.v(F32, self.AOS + i * 512 * 4, 512) for i in range(2)]
        p0 = self.AP0
        KMX = ar.v(F32, p0, 32, rows=1); KPR = ar.v(F32, p0 + 128, 32, rows=1)
        QMX = ar.v(F32, p0 + 256, 32, rows=1); NGC = ar.v(F32, p0 + 384, 32, rows=1)
        NEGC = ar.v(F32, p0 + 512, 32)
        LPART = ar.v(F32, p0 + 1024, 16)
        RML = ar.v(F32, p0 + 1536, 1)
        aos_guard = [0, 0]
        for h in range(HC):
            base_q, base_k, base_v = h, HC + h, 2 * HC + h
            R.w("sync", "sStO0", S.val("sStO0"))
            R.w("sync", "sStO1", S.val("sStO1"))
            R.w("sync", "sLdQKV", S.val("sLdQKV"))
            R.w("sync", "sStQ0", S.val("sStQ0"))
            R.w("sync", "sStQ1", S.val("sStQ1"))
            lq = R.o("sync", lambda e, _, r=base_q: e.dma_start(
                out=AQ, in_=self.d_qkv[r * 128:(r + 1) * 128, :]), inc="sLdQKV", k=16)
            lk = R.o("sync", lambda e, _, r=base_k: e.dma_start(
                out=AK, in_=self.d_qkv[r * 128:(r + 1) * 128, :]), inc="sLdQKV", k=16)
            lv = R.o("sync", lambda e, _, r=base_v: e.dma_start(
                out=AVV, in_=self.d_qkv[r * 128:(r + 1) * 128, :]), inc="sLdQKV", k=16)
            R.w("tensor", "sLdQKV", lv)
            R.w("vector", "sLdQKV", lv)
            self.transp_batch(lambda i: AVV[:, i * 128:(i + 1) * 128], NKT,
                              lambda i0, w: ar.v(F32, self.AVT + i0 * 128 * 4, w),
                              banks=(2, 3), bf16=False, evac_eng="vector")
            ps7 = self._psum[7]
            for which, src, ld, dst in (("k", AK, lk, KMX), ("q", AQ, lq, QMX)):
                R.w("vector", "sLdQKV", lv)
                vv = self.vop(lambda e, _, src=src: e.tensor_tensor(
                    out=APS, in0=src, in1=src, op=ALU.mult))
                R.w("tensor", "sV", vv)
                for ch in range(T // 512):
                    if self.pguard[7] is not None:
                        R.w("tensor", *self.pguard[7])
                        self.pguard[7] = None
                    tv = self.mm(ps7.ap()[0:1, 0:512], self.ONESC,
                                 APS[:, ch * 512:(ch + 1) * 512], True, True, inc="sT")
                    R.w("vector", "sT", tv)
                    rv = self.vop(lambda e, _, ch=ch, dst=dst: e.tensor_reduce(
                        out=dst[0:1, ch * 4:(ch + 1) * 4],
                        in_=ps7.ap()[0:1, 0:512].rearrange("p (c t) -> p c t", t=128),
                        axis=AX.X, op=ALU.max))
                    self.pguard[7] = ("sV", rv)
            for qt in range(QT):
                self.R.q["vector"].append(lambda e, _: None) if False else None
                rv = self.vop(lambda e, _, qt=qt: e.tensor_reduce(
                    out=KPR[0:1, qt:qt + 1], in_=KMX[0:1, 0:qt + 1], axis=AX.X, op=ALU.max))
            self.vop(lambda e, _: e.tensor_tensor(out=NGC[0:1, 0:QT], in0=QMX[0:1, 0:QT],
                                                  in1=KPR[0:1, 0:QT], op=ALU.mult))
            vs = S.val("sV")
            R.w("scalar", "sV", vs)
            sv = R.o("scalar", lambda e, _: e.activation(out=NGC[0:1, 0:QT],
                                                         in_=NGC[0:1, 0:QT], func=AF.Sqrt),
                     inc="sS")
            R.w("vector", "sS", sv)
            vneg = self.vop(lambda e, _: e.tensor_scalar(out=NGC[0:1, 0:QT],
                                                         in0=NGC[0:1, 0:QT], scalar1=-1.0,
                                                         scalar2=None, op0=ALU.mult))
            R.w("tensor", "sV", vneg)
            if self.pguard[7] is not None:
                R.w("tensor", *self.pguard[7]); self.pguard[7] = None
            tb = self.mm(ps7.ap()[:, 0:QT], self.ONESR, NGC[0:1, 0:QT], True, True, inc="sT")
            R.w("vector", "sT", tb)
            vb = self.vop(lambda e, _: e.tensor_copy(NEGC[:, 0:QT], ps7.ap()[:, 0:QT]))
            self.pguard[7] = ("sV", vb)
            aps_guard_t = 0
            aps_guard_v = 0
            for qt in range(QT):
                Lk = 128 * (qt + 1)
                R.w("scalar", "sT", aps_guard_t)
                R.w("scalar", "sV", aps_guard_v)
                chunks = []
                c0 = 0
                while c0 + 512 <= Lk - 128:
                    chunks.append((c0, 512)); c0 += 512
                if Lk - c0 > 0:
                    chunks.append((c0, Lk - c0))
                nch = len(chunks)
                for ci, (cb, cw) in enumerate(chunks):
                    bk = ci % 2
                    ps = self._psum[bk]
                    if self.pguard[bk] is not None:
                        R.w("tensor", *self.pguard[bk]); self.pguard[bk] = None
                    tv = self.mm(ps.ap()[:, 0:cw], AQ[:, qt * 128:(qt + 1) * 128],
                                 AK[:, cb:cb + cw], True, True, inc="sT")
                    last = (ci == nch - 1)
                    dep_sem, dep_v = "sT", tv
                    if last:
                        R.w("vector", "sT", tv)
                        mv = self.vop(lambda e, _, ps=ps, cw=cw: e.tensor_tensor(
                            out=ps.ap()[:, cw - 128:cw], in0=ps.ap()[:, cw - 128:cw],
                            in1=self.MASK, op=ALU.add))
                        dep_sem, dep_v = "sV", mv
                    R.w("scalar", dep_sem, dep_v)
                    sv = R.o("scalar", lambda e, _, ps=ps, cb=cb, cw=cw, ci=ci, qt=qt:
                             e.activation(out=APS[:, cb:cb + cw], in_=ps.ap()[:, 0:cw],
                                          func=AF.Exp, bias=NEGC[:, qt:qt + 1],
                                          accum_out=LPART[:, ci:ci + 1]), inc="sS")
                    self.pguard[bk] = ("sS", sv)
                R.w("vector", "sS", S.val("sS"))
                self.vop(lambda e, _, nch=nch: e.tensor_reduce(
                    out=RML, in_=LPART[:, 0:nch], axis=AX.X, op=ALU.add))
                vr = self.vop(lambda e, _: e.reciprocal(out=RML, in_=RML))
                R.w("tensor", "sS", S.val("sS"))
                self.transp_batch(lambda i: APS[:, i * 128:(i + 1) * 128], qt + 1,
                                  lambda i0, w: ar.v(F32, self.APT + i0 * 128 * 4, w),
                                  banks=(2, 3), bf16=False, evac_eng="vector")
                aps_guard_t = S.val("sT")
                aps_guard_v = S.val("sV")
                R.w("tensor", "sV", S.val("sV"))
                bk = 4
                if self.pguard[bk] is not None:
                    R.w("tensor", *self.pguard[bk]); self.pguard[bk] = None
                ps4 = self._psum[bk]
                for kt in range(qt + 1):
                    tv = self.mm(ps4.ap()[:, 0:128], APT[:, kt * 128:(kt + 1) * 128],
                                 AVT[:, kt * 128:(kt + 1) * 128],
                                 start=(kt == 0), stop=(kt == qt),
                                 inc="sT" if kt == qt else None)
                R.w("scalar", "sT", tv)
                R.w("scalar", "sV", vr)
                so = R.o("scalar", lambda e, _, qt=qt: e.activation(
                    out=AO[:, qt * 128:(qt + 1) * 128], in_=ps4.ap()[:, 0:128],
                    func=AF.Copy, scale=RML), inc="sS")
                self.pguard[bk] = ("sS", so)
            sT_before = S.val("sS")
            for i0 in range(0, NKT, 4):
                w = min(4, NKT - i0)
                bk = 5 if (i0 // 4) % 2 == 0 else 6
                ps = self._psum[bk]
                if self.pguard[bk] is not None:
                    R.w("tensor", *self.pguard[bk]); self.pguard[bk] = None
                R.w("tensor", "sS", sT_before)
                for j in range(w):
                    tv = R.o("tensor", lambda e, _, j=j, i0=i0, ps=ps: e.transpose(
                        ps.ap()[:, j * 128:(j + 1) * 128],
                        AO[:, (i0 + j) * 128:(i0 + j + 1) * 128], self.ID32), inc="sT")
                half = (i0 // 4) % 2
                if aos_guard[half]:
                    R.w("vector", "sStO0" if half == 0 else "sStO1", aos_guard[half])
                R.w("vector", "sT", tv)
                ev = self.vop(lambda e, _, ps=ps, half=half, w=w: e.tensor_copy(
                    AOSb[half][:, 0:w * 128], ps.ap()[:, 0:w * 128]))
                self.pguard[bk] = ("sV", ev)
                R.w("gpsimd", "sV", ev)
                gv = R.o("gpsimd", lambda e, env, h=h, i0=i0, w=w, half=half: e.dma_start(
                    out=self.d_oTl[h * 128:(h + 1) * 128, i0 * 128:(i0 + w) * 128],
                    in_=AOSb[half][:, 0:w * 128]), inc="sStO0" if half == 0 else "sStO1", k=16)
                aos_guard[half] = gv

    def outproj(self, bar_o):
        c, R, S, ar = self.c, self.R, self.sems, self.ar
        T, DM, DCH, TOK = c.T, c.DM, c.DCH, c.TOK
        OPG = self.OPG
        DOW = min(512, DM)
        NT2 = OPG // 128
        OC = ar.v(F32, self.OC, DCH * OPG)
        WOb = [ar.v(F32, self.WO + i * 512 * 4, DOW) for i in range(2)]
        X1Gv = ar.v(F32, self.X1G, NT2 * DM)
        X2Sv = ar.v(F32, self.X2S, NT2 * DM)
        OBROW = self.stat(32, n=DM // 128, rows=1)
        ob = ar.v(F32, self.OB, DM, rows=1)
        lob = R.o("sync", lambda e, _: e.dma_start(out=ob, in_=self.outb[0:1, :]), inc="sLd1", k=16)
        R.w("sync", "sCC", bar_o)
        wo_guard = [0, 0]
        R.w("sync", "sStCX", S.val("sStCX"))
        R.w("sync", "sCC", bar_o)
        R.w("sync", "sLd1", S.val("sLd1"))
        semoc2 = S.h["sLd1"]
        TOKc = TOK
        def _ocfetch(e, env, rho):
            e.dma_start(out=self.d_ocl[:, :],
                        in_=self.sh_ocat[:, rho * TOKc:(rho + 1) * TOKc]).then_inc(semoc2, 16)
        R.switch("sync", _ocfetch, incs=(("sLd1", 16),))
        ocf = S.val("sLd1")
        for grp in range(TOK // OPG):
            R.w("sync", "sStX2", S.val("sStX2"))
            R.w("sync", "sLdOC", S.val("sLdOC"))
            R.w("sync", "sLd1", ocf)
            lo = R.o("sync", lambda e, env, grp=grp: e.dma_start(
                out=OC.rearrange("p (c t) -> p c t", c=DCH),
                in_=self.d_ocl.rearrange("(c p) t -> p c t", p=128)[
                    :, :, grp * OPG:(grp + 1) * OPG]), inc="sLdOC", k=16)
            lx = R.o("sync", lambda e, _, grp=grp: e.dma_start(
                out=X1Gv.rearrange("p (n m) -> p n m", n=NT2),
                in_=self.d_x1[grp * OPG:(grp + 1) * OPG, :].rearrange(
                    "(n p) m -> p n m", p=128)), inc="sLdOC", k=16)
            for do in range(DM // DOW):
                for tt2 in range(NT2):
                    bk = tt2
                    if self.pguard[bk] is not None:
                        R.w("tensor", *self.pguard[bk]); self.pguard[bk] = None
                for dc in range(DCH):
                    wbuf = (do * DCH + dc) % 2
                    wos = "sWO0" if wbuf == 0 else "sWO1"
                    if wo_guard[wbuf]:
                        R.w("sync", "sT", wo_guard[wbuf])
                    R.w("sync", wos, S.val(wos))
                    lw = R.o("sync", lambda e, _, dc=dc, do=do, wbuf=wbuf: e.dma_start(
                        out=WOb[wbuf],
                        in_=self.outwT[dc * 128:(dc + 1) * 128, do * DOW:(do + 1) * DOW]),
                        inc=wos, k=16)
                    R.w("tensor", wos, lw)
                    R.w("tensor", "sLdOC", lx)
                    R.w("tensor", "sLd1", lob)
                    for tt2 in range(NT2):
                        tv = self.mm(self._psum[tt2].ap()[:, 0:DOW],
                                     OC[:, dc * OPG + tt2 * 128: dc * OPG + tt2 * 128 + 128],
                                     WOb[wbuf], start=(dc == 0), stop=False,
                                     inc="sT" if tt2 == NT2 - 1 else None)
                    wo_guard[wbuf] = tv
                for tt2 in range(NT2):
                    tb = self.mm(self._psum[tt2].ap()[:, 0:DOW], self.ONESR,
                                 ob[0:1, do * DOW:(do + 1) * DOW], False, True, inc="sT")
                    R.w("vector", "sT", tb)
                    R.w("vector", "sLdOC", lx)
                    ev = self.vop(lambda e, _, tt2=tt2, do=do: e.scalar_tensor_tensor(
                        out=X2Sv[:, tt2 * DM + do * DOW: tt2 * DM + (do + 1) * DOW],
                        in0=self._psum[tt2].ap()[:, 0:DOW], scalar=1.0,
                        in1=X1Gv[:, tt2 * DM + do * DOW: tt2 * DM + (do + 1) * DOW],
                        op0=ALU.mult, op1=ALU.add))
                    self.pguard[tt2] = ("sV", ev)
            for tt2 in range(NT2):
                R.w("gpsimd", "sV", S.val("sV"))
                R.o("gpsimd", lambda e, _, grp=grp, tt2=tt2: e.dma_start(
                    out=self.d_x2[grp * OPG + tt2 * 128: grp * OPG + tt2 * 128 + 128, :],
                    in_=X2Sv[:, tt2 * DM:(tt2 + 1) * DM]), inc="sStX2", k=16)

    def build(self):
        R, S = self.R, self.sems
        self._wa_guard = [0, 0]
        self._wb_guard = [0, 0, 0, 0]
        S.n["sLd1"] += 16
        self.p0()
        bar_w = S.val("sCC")
        self.fence()
        self.ffn_call(self.x_in, self.d_x1, True, bar_w)
        R.w("gpsimd", "sStLN", S.val("sStLN"))
        R.w("gpsimd", "sStCX", S.val("sStCX"))
        semln = S.h["sStLN"]
        TOKc = self.c.TOK
        def _lncopy(e, env, rho):
            e.dma_start(out=self.sh_ln2T[:, rho * TOKc:(rho + 1) * TOKc],
                        in_=self.d_ln2Tl[:, :]).then_inc(semln, 16)
        R.switch("gpsimd", _lncopy, incs=(("sStLN", 16),))
        R.w("gpsimd", "sStLN", S.val("sStLN"))
        bar1 = self.barrier()
        self.fence()
        self.qkv(bar1)
        self.fence()
        self.attn()
        R.w("gpsimd", "sStO0", S.val("sStO0"))
        R.w("gpsimd", "sStO1", S.val("sStO1"))
        semoc = S.h["sStO0"]
        HCD = self.c.HC * 128
        def _occopy(e, env, rho):
            e.dma_start(out=self.sh_ocat[rho * HCD:(rho + 1) * HCD, :],
                        in_=self.d_oTl[:, :]).then_inc(semoc, 16)
        R.switch("gpsimd", _occopy, incs=(("sStO0", 16),))
        R.w("gpsimd", "sStO0", S.val("sStO0"))
        bar2 = self.barrier()
        self.fence()
        self.outproj(bar2)
        self.fence()
        self.ffn_call(self.d_x2, self.y_out, False, bar_w)
        self.fence()
        env = {
            "nc": self.nc,
            "ld_h": S.h["sLd1"], "ld_v": 16,
            "offload": {"sync": EngT.SP, "gpsimd": EngT.Pool},
            "par_ap": self.POFF[0:1, 3:4],
            "prelude": {"sync": lambda e, env: e.dma_start(
                out=self.POFF, in_=self.poffs[:, :]).then_inc(S.h["sLd1"], 16)},
        }
        with self.nc.Block() as block:
            self.R.replay(block, env)
        return self.nc


def host_in_maps(c: CFG, x_full, qkv_w, qkv_b, out_w, out_b, W1, W2):
    DM, DH, HC, TOK, HALF_F = c.DM, c.DH, c.HC, c.TOK, c.DFF // 2
    H = c.H
    w1T = np.ascontiguousarray(W1.T)
    w2T = np.ascontiguousarray(W2.T)
    outwT = np.ascontiguousarray(out_w.T)
    in_maps = []
    for core in range(8):
        p, rho = core // 2, core % 2
        heads = [rho * HC + hh for hh in range(HC)]
        rows = []
        for kind in range(3):
            for hh in heads:
                rows.extend(range(kind * H * DH + hh * DH, kind * H * DH + (hh + 1) * DH))
        rows = np.array(rows)
        qkvwT = np.ascontiguousarray(qkv_w[rows, :].T)
        qkvb_s = np.ascontiguousarray(qkv_b[rows])[None, :]
        poffs = np.array([[rho * TOK, rho * (DM // 2), rho * HALF_F, rho,
                           rho * 2, rho * HALF_F, 0, 0]], dtype=np.int32)
        in_maps.append({
            "x": np.ascontiguousarray(x_full[p, rho * TOK:(rho + 1) * TOK, :]),
            "qkvwT": qkvwT, "qkvb": qkvb_s,
            "outwT": outwT, "outb": np.ascontiguousarray(out_b)[None, :],
            "w1Th": np.ascontiguousarray(w1T[:, rho * HALF_F:(rho + 1) * HALF_F]),
            "w2Th": np.ascontiguousarray(w2T[rho * HALF_F:(rho + 1) * HALF_F, :]),
            "poffs": poffs,
        })
    return in_maps


def host_gather(c: CFG, results, B):
    out = np.zeros((B, c.T, c.DM), dtype=np.float32)
    for core in range(8):
        p, rho = core // 2, core % 2
        out[p, rho * c.TOK:(rho + 1) * c.TOK, :] = results[core]["y"]
    return out




def kernel(L0_attn_x, L0_attn_qkv_w, L0_attn_qkv_b, L0_attn_out_w, L0_attn_out_b,
           L0_ffn_W1, L0_ffn_W2):
    c = CFG()
    mk = MK(c)
    nc = mk.build()
    in_maps = host_in_maps(c, np.asarray(L0_attn_x), np.asarray(L0_attn_qkv_w),
                           np.asarray(L0_attn_qkv_b), np.asarray(L0_attn_out_w),
                           np.asarray(L0_attn_out_b), np.asarray(L0_ffn_W1),
                           np.asarray(L0_ffn_W2))
    res = run_bass_kernel_spmd(nc, in_maps, core_ids=list(range(8)))
    return host_gather(c, res.results, 4)


# version 03 (final)
# speedup vs baseline: 1.3265x; 1.2704x over previous
import numpy as np
import ml_dtypes
import concourse.bass as bass
import concourse.mybir as mybir
from concourse.bass_utils import run_bass_kernel_spmd

F32 = mybir.dt.float32
BF16 = mybir.dt.bfloat16
I8 = mybir.dt.int8
I32 = mybir.dt.int32
AF = mybir.ActivationFunctionType
ALU = mybir.AluOpType
AX = mybir.AxisListType
ds = bass.ds
EngT = mybir.EngineType


class CFG:
    B = 4
    T = 4096
    DM = 2048
    H = 16
    DH = 128
    DFF = 8192
    G = 256
    EPS = 1e-5

    @property
    def TOK(self): return self.T // 2
    @property
    def HC(self): return self.H // 2
    @property
    def DCH(self): return self.DM // 128
    @property
    def FCH(self): return self.DFF // 128
    @property
    def F512(self): return self.DFF // 512
    @property
    def NG(self): return self.TOK // self.G
    @property
    def NT(self): return self.G // 128
    @property
    def QT(self): return self.T // 128
    @property
    def OUTF(self): return 3 * self.HC * self.DH


cfg = CFG()

INV_SQRT_D_HI = float(np.float32(1.0 / np.sqrt(128.0)))
INV_SQRT_D_LO = float(np.float32(1.0 / np.sqrt(np.float64(128.0)) - np.float64(np.float32(1.0 / np.sqrt(128.0)))))
TWO_PI = 2.0 * np.pi
_c1 = float(np.float32(6.28125))
_c2 = float(np.float32(TWO_PI - _c1))
_c3 = float(np.float32(TWO_PI - _c1 - np.float64(np.float32(TWO_PI - _c1))))
HALF_PI = np.pi / 2.0
_h1 = float(np.float32(1.5703125))
_h2 = float(np.float32(HALF_PI - _h1))
INV_TWO_PI = float(np.float32(1.0 / TWO_PI))


class Sems:
    def __init__(self, nc, ctx):
        self.nc = nc
        self._ctx = ctx
        self.h = {}
        self.n = {}

    def new(self, name):
        self.h[name] = self._ctx.enter_context(self.nc.semaphore(name))
        self.n[name] = 0
        return name

    def inc(self, name, inst, k=1):
        inst.then_inc(self.h[name], k)
        self.n[name] += k
        return self.n[name]

    def val(self, name):
        return self.n[name]

    def wait(self, eng, name, v=None):
        eng.wait_ge(self.h[name], self.n[name] if v is None else v)


ENGS = ("sync", "vector", "scalar", "tensor", "gpsimd")


class Rec:
    def __init__(self, sems):
        self.q = {k: [] for k in ENGS}
        self.sems = sems
        self.trace = {k: [] for k in ENGS}

    @staticmethod
    def _site():
        return "?"

    def o(self, eng, fn, inc=None, k=1):
        self.trace[eng].append(("inc" if inc else "op", inc, k, self._site()))
        if inc is not None:
            self.sems.n[inc] += k
            h = self.sems.h[inc]
            def wrapped(e, env, fn=fn, h=h, k=k):
                fn(e, env).then_inc(h, k)
            self.q[eng].append(wrapped)
        else:
            self.q[eng].append(lambda e, env, fn=fn: fn(e, env))
        return self.sems.n[inc] if inc is not None else None

    def w(self, eng, sem, v=None):
        val = self.sems.n[sem] if v is None else v
        h = self.sems.h[sem]
        self.trace[eng].append(("wait", sem, val, self._site()))
        self.q[eng].append(lambda e, env, h=h, val=val: e.wait_ge(h, val))

    def drain(self, eng):
        self.trace[eng].append(("drain", None, 0, self._site()))
        self.q[eng].append(lambda e, env: e.drain())

    def switch(self, eng, body, incs=()):
        for s, k in incs:
            self.sems.n[s] += k
            self.trace[eng].append(("inc", s, k, self._site()))
        def fn(e, env, body=body):
            for rho in env["nc"].Switch(engines=[e], index=[env["pars"][env["_engname"]]], n=2):
                body(e, env, rho)
        self.q[eng].append(("SWITCH", fn))

    def replay(self, block, env):
        def mk(name):
            def body(e):
                pre = env.get("prelude", {}).get(name)
                if pre is not None:
                    pre(e, env)
                if name in env.get("offload", {}):
                    e.wait_ge(env["ld_h"], env["ld_v"])
                    _, vals = env["nc"].values_load_multi_w_load_instructions(
                        env["par_ap"], engines=[env["offload"][name]], min_val=0,
                        max_val=1, skip_runtime_bounds_check=True)
                    env.setdefault("pars", {})[name] = vals[0]
                env["_engname"] = name
                for fn in self.q[name]:
                    if isinstance(fn, tuple):
                        fn[1](e, env)
                    else:
                        fn(e, env)
            return body
        for name in ENGS:
            getattr(block, name)(mk(name))


def ceil_div(a, b):
    return (a + b - 1) // b


def _noop():
    pass


class Arena:
    def __init__(self, nc, bytes_per_part):
        self.t = nc.alloc_sbuf_tensor("arena", [128, bytes_per_part], I8)
        self.views = {F32: self.t.bitcast(F32), BF16: self.t.bitcast(BF16), I8: self.t, I32: self.t.bitcast(I32)}
        self.bpp = bytes_per_part

    def v(self, dtype, byte_off, cols, rows=128):
        iz = {F32: 4, BF16: 2, I8: 1, I32: 4}[dtype]
        assert byte_off % iz == 0 and byte_off + cols * iz <= self.bpp, (byte_off, cols, dtype)
        e0 = byte_off // iz
        ap = self.views[dtype].ap()[0:rows, e0:e0 + cols]
        return ap



class MK:

    def __init__(self, c: CFG):
        self.c = c
        nc = bass.Bass("TRN2", target_bir_lowering=False)
        self.nc = nc
        TOK, DM, T, DFF, OUTF = c.TOK, c.DM, c.T, c.DFF, c.OUTF
        HALF_F = DFF // 2
        self.x_in = nc.dram_tensor("x", [TOK, DM], F32, kind="ExternalInput").ap()
        self.qkvwT = nc.dram_tensor("qkvwT", [DM, OUTF], F32, kind="ExternalInput").ap()
        self.qkvb = nc.dram_tensor("qkvb", [1, OUTF], F32, kind="ExternalInput").ap()
        self.outwT = nc.dram_tensor("outwT", [DM, DM], F32, kind="ExternalInput").ap()
        self.outb = nc.dram_tensor("outb", [1, DM], F32, kind="ExternalInput").ap()
        self.w1Th = nc.dram_tensor("w1Th", [DM, HALF_F], F32, kind="ExternalInput").ap()
        self.w2Th = nc.dram_tensor("w2Th", [HALF_F, DM], F32, kind="ExternalInput").ap()
        self.poffs = nc.dram_tensor("poffs", [1, 8], I32, kind="ExternalInput").ap()
        self.y_out = nc.dram_tensor("y", [TOK, DM], F32, kind="ExternalOutput").ap()
        self.d_x1 = nc.dram_tensor("d_x1", [TOK, DM], F32).ap()
        self.d_x2 = nc.dram_tensor("d_x2", [TOK, DM], F32).ap()
        self.d_qkv = nc.dram_tensor("d_qkv", [OUTF, T], F32).ap()
        self.d_emb = nc.dram_tensor("d_emb", [256, T], F32).ap()
        self.bar_in = nc.dram_tensor("bar_in", [1, 8], I32).ap()
        self.bar_out = nc.dram_tensor("bar_out", [8, 8], I32).ap()
        self.d_ln2Tl = nc.dram_tensor("d_ln2Tl", [DM, TOK], F32).ap()
        self.d_oTl = nc.dram_tensor("d_oTl", [c.HC * 128, T], F32).ap()
        self.d_ocl = nc.dram_tensor("d_ocl", [DM, TOK], F32).ap()
        self.d_t1Tl = nc.dram_tensor("d_t1Tl", [DM, HALF_F], BF16).ap()
        self.d_t2Tl = nc.dram_tensor("d_t2Tl", [HALF_F, DM], BF16).ap()
        self.cc_in = nc.dram_tensor("cc_in", [1, 2], F32).ap()
        self.cc_out = nc.dram_tensor("cc_out", [1, 2], F32).ap()
        self.sh_ln2T = nc.dram_tensor("sh_ln2T", [DM, T], F32, addr_space="Shared").ap()
        self.sh_ocat = nc.dram_tensor("sh_ocat", [DM, T], F32, addr_space="Shared").ap()
        self.sh_t1T = nc.dram_tensor("sh_t1T", [DM, DFF], BF16, addr_space="Shared").ap()
        self.sh_t2T = nc.dram_tensor("sh_t2T", [DFF, DM], BF16, addr_space="Shared").ap()

        from contextlib import ExitStack
        self.ctx = ExitStack()
        self.sems = Sems(nc, self.ctx)
        self.DMA_SEMS = ("sLd1", "sLdX", "sWA0", "sWA1", "sWB0", "sWB1", "sLdQW",
                         "sQL0", "sQL1", "sLdQKV", "sLdOC", "sWO0", "sWO1",
                         "sStCX", "sStLN", "sStQ0", "sStQ1", "sStO0", "sStO1",
                         "sStX2", "sStEMB", "sStW", "sStW2", "sLd2")
        for s in ("sT", "sV", "sS", "sG", "sCC") + self.DMA_SEMS:
            self.sems.new(s)
        self.R = Rec(self.sems)
        self.ar = Arena(nc, 196 * 1024)
        self._psum = [self.ctx.enter_context(nc.psum_tensor("pb%d" % i, [128, 512], F32))
                      for i in range(8)]
        self.pguard = [None] * 8
        self._arena_map()

    def _arena_map(self):
        c, ar = self.c, self.ar
        NT, DM, DFF, T, G, DCH, FCH, OUTF = c.NT, c.DM, c.DFF, c.T, c.G, c.DCH, c.FCH, c.OUTF
        v = ar.v
        self.ID32 = v(F32, 0, 128); self.IDBF = v(BF16, 512, 128)
        self.MASK = v(F32, 768, 128); self.IOTA = v(I32, 1280, 128)
        self.ONESB = v(F32, 1792, 128)
        self.ONESC = v(F32, 1792, 1)
        self.ONESR = v(F32, 1792, 128, rows=1)
        self.POFF = v(I32, 2304, 8, rows=1)
        self._stat_base = 2368
        self._pers_end = self._stat_base + 4 * 96
        assert self._pers_end <= 4096
        FFN = 4096
        def seq(base, *sizes):
            offs = []
            o = base
            for s in sizes:
                offs.append(o); o += s
            offs.append(o)
            return offs
        (self.A_H, self.B_LN, self.C_X, self.D_ST, self.QT8, self.HQ,
         self.HT, self.WA, self.WB, ffn_end) = seq(
            FFN, NT * DFF * 4, max(DM * 4, DFF), NT * DM * 4,
            max(DCH * 128 * 4, DM * 2 + DM), DCH * G * 2,
            DFF * 2, FCH * G * 2, 2 * DCH * 512 * 2, 4 * 512 * 2)
        self.I8A = self.B_LN
        self.QB = self.D_ST
        self.I8A_A = self.D_ST + self.c.DM * 2
        assert ffn_end <= ar.bpp, ffn_end
        self.OGT = min(6, OUTF // 128)
        (self.QWT, self.QLN, self.QEMB, self.QST, qkv_end) = seq(
            FFN, DCH * self.OGT * 128 * 4, 2 * DCH * 512 * 4, 2 * T * 4, 2 * 512 * 4)
        assert qkv_end <= ar.bpp, qkv_end
        (self.AQ, self.AK, self.AVV, self.AVT, self.APS, self.APT, self.AO,
         self.AOS, self.AP0, att_end) = seq(
            FFN, T * 4, T * 4, T * 4, T * 4, T * 4, T * 4, T * 4, 1024 * 4, 4096 * 4)
        assert att_end <= ar.bpp, att_end
        self.OPG = min(512, c.TOK)
        (self.OC, self.WO, self.X1G, self.X2S, self.OB, op_end) = seq(
            FFN, DCH * self.OPG * 4, 2 * 512 * 4, (self.OPG // 128) * DM * 4,
            (self.OPG // 128) * DM * 4, DM * 4)
        assert op_end <= ar.bpp, op_end

    def stat(self, i, n=1, rows=128):
        return self.ar.v(F32, self._stat_base + 4 * i, n, rows=rows)

    def mm(self, out_ap, lhsT, rhs, start, stop, inc=None):
        return self.R.o("tensor", lambda e, _: e.matmul(out_ap, lhsT, rhs, start=start, stop=stop),
                        inc=inc)

    def fence(self, name=""):
        R, sems = self.R, self.sems
        eng_sem = {"vector": "sV", "scalar": "sS", "tensor": "sT", "gpsimd": "sG"}
        vals = {}
        for eng, s in eng_sem.items():
            R.o(eng, lambda e, _: e.drain(), inc=s)
            vals[eng] = sems.val(s)
        vals_dma = {s: sems.val(s) for s in self.DMA_SEMS}
        for eng in ENGS:
            for other, s in eng_sem.items():
                if other != eng:
                    R.w(eng, s, vals[other])
            for s, vv in vals_dma.items():
                R.w(eng, s, vv)

    def barrier(self):
        R = self.R
        v = R.o("gpsimd", lambda e, _: e.collective_compute(
            "AllGather", ALU.bypass, replica_groups=[list(range(8))],
            ins=[self.bar_in], outs=[self.bar_out]), inc="sCC")
        R.w("gpsimd", "sCC", v)
        return v

    def vop(self, fn, inc="sV"):
        self.R.drain("vector")
        return self.R.o("vector", fn, inc=inc)

    def transp_batch(self, src_ap_fn, n_tiles, dst_base_fn, banks, bf16=False,
                     evac_eng="vector", evac_scale=None):
        R, S = self.R, self.sems
        ident = self.IDBF if bf16 else self.ID32
        nb = len(banks)
        for i0 in range(0, n_tiles, 4):
            w = min(4, n_tiles - i0)
            bk = banks[(i0 // 4) % nb]
            psbank = self._psum[bk]
            psap = psbank.bitcast(BF16).ap() if bf16 else psbank.ap()
            if self.pguard[bk] is not None:
                sem, val = self.pguard[bk]
                R.w("tensor", sem, val)
            for j in range(w):
                tv = self.R.o("tensor",
                              lambda e, _, j=j, i0=i0, psap=psap: e.transpose(
                                  psap[:, j * 128:(j + 1) * 128], src_ap_fn(i0 + j), ident),
                              inc="sT")
            R.w(evac_eng, "sT", tv)
            dst = dst_base_fn(i0, w * 128)
            if evac_scale is not None:
                ev = R.o(evac_eng, lambda e, _, psap=psap, dst=dst, w=w: e.activation(
                    out=dst, in_=psap[:, 0:w * 128], func=AF.Copy, scale=evac_scale),
                    inc="sS" if evac_eng == "scalar" else "sV")
            elif evac_eng == "scalar":
                ev = R.o("scalar", lambda e, _, psap=psap, dst=dst, w=w: e.copy(
                    dst, psap[:, 0:w * 128]), inc="sS")
            else:
                ev = R.o("vector", lambda e, _, psap=psap, dst=dst, w=w: e.tensor_copy(
                    dst, psap[:, 0:w * 128]), inc="sV")
            self.pguard[bk] = (("sS" if evac_eng == "scalar" else "sV"), ev)

    def quant(self, x_ap, s_ap, out_ap, i8_ap, u_ap, clip=False):
        R, S = self.R, self.sems
        R.w("scalar", "sG", S.val("sG"))
        R.w("scalar", "sV", S.val("sV"))
        vS = R.o("scalar", lambda e, _: e.activation(out=i8_ap, in_=x_ap, func=AF.Copy,
                                                     scale=s_ap, bias=-1.0), inc="sS")
        R.w("vector", "sS", vS)
        self.vop(lambda e, _: e.scalar_tensor_tensor(
            out=x_ap, in0=x_ap, scalar=s_ap, in1=i8_ap, op0=ALU.mult, op1=ALU.subtract))
        self.vop(lambda e, _: e.tensor_scalar(
            out=u_ap, in0=x_ap, scalar1=0.5, scalar2=None, op0=ALU.is_ge))
        self.vop(lambda e, _: e.scalar_tensor_tensor(
            out=x_ap, in0=x_ap, scalar=1.5, in1=u_ap, op0=ALU.is_ge, op1=ALU.add))
        if clip:
            self.vop(lambda e, _: e.tensor_tensor(
                out=u_ap, in0=i8_ap, in1=x_ap, op=ALU.add))
            vfin = self.vop(lambda e, _: e.tensor_scalar(
                out=out_ap, in0=u_ap, scalar1=-1.0, scalar2=1.0, op0=ALU.max, op1=ALU.min))
            return ("sV", vfin)
        vV2 = self.vop(lambda e, _: e.tensor_tensor(
            out=out_ap, in0=i8_ap, in1=x_ap, op=ALU.add))
        return ("sV", vV2)

    def ln_tile(self, x_ap, out_ap, sq_scratch_ap, sb):
        R, DM, EPS = self.R, self.c.DM, self.c.EPS
        st_sum, st_mu, st_msq, st_vpe, st_inv, st_nb = (self.stat(sb + i) for i in range(6))
        self.vop(lambda e, _: e.tensor_reduce(out=st_sum, in_=x_ap, axis=AX.X, op=ALU.add))
        vS = R.o("scalar", lambda e, _: e.activation(out=sq_scratch_ap, in_=x_ap,
                                                     func=AF.Square, accum_out=st_msq), inc="sS")
        R.w("vector", "sS", vS)
        self.vop(lambda e, _: e.tensor_scalar(out=st_mu, in0=st_sum, scalar1=1.0 / DM,
                                              scalar2=None, op0=ALU.mult))
        self.vop(lambda e, _: e.tensor_scalar(out=st_msq, in0=st_msq, scalar1=1.0 / DM,
                                              scalar2=None, op0=ALU.mult))
        self.vop(lambda e, _: e.scalar_tensor_tensor(out=st_vpe, in0=st_mu, scalar=st_mu,
                                                     in1=st_msq, op0=ALU.mult, op1=ALU.subtract))
        self.vop(lambda e, _: e.tensor_scalar(out=st_vpe, in0=st_vpe, scalar1=-1.0,
                                              scalar2=EPS, op0=ALU.mult, op1=ALU.add))
        vV = self.vop(lambda e, _: e.reciprocal(out=st_vpe, in_=st_vpe))
        R.w("scalar", "sV", vV)
        vS2 = R.o("scalar", lambda e, _: e.activation(out=st_inv, in_=st_vpe,
                                                      func=AF.Sqrt), inc="sS")
        R.w("vector", "sS", vS2)
        self.vop(lambda e, _: e.tensor_scalar(out=st_nb, in0=st_mu, scalar1=st_inv,
                                              scalar2=-1.0, op0=ALU.mult, op1=ALU.mult))
        va = self.vop(lambda e, _: e.tensor_scalar(out=out_ap, in0=x_ap, scalar1=st_inv,
                                                   scalar2=st_nb, op0=ALU.mult, op1=ALU.add))
        return ("sV", va), st_inv

    def p0(self):
        c, R, S, ar = self.c, self.R, self.sems, self.ar
        nc = self.nc
        T, DM, DFF = c.T, c.DM, c.DFF
        HALF_F = DFF // 2
        R.w("gpsimd", "sLd1", 16)
        bi = R.o("gpsimd", lambda e, _: e.dma_start(out=self.bar_in[:, :], in_=self.POFF),
                 inc="sStW", k=16)
        R.w("gpsimd", "sStW", bi)
        vG = R.o("gpsimd", lambda e, _: e.iota(self.IOTA, pattern=[[-1, 128]], base=0,
                                               channel_multiplier=1), inc="sG")
        R.o("gpsimd", lambda e, _: e.memset(self.ONESB, 1.0), inc="sG")
        R.w("vector", "sG", S.val("sG"))
        self.vop(lambda e, _: e.tensor_scalar(out=self.ID32, in0=self.IOTA, scalar1=0.0,
                                              scalar2=None, op0=ALU.is_equal))
        self.vop(lambda e, _: e.tensor_copy(self.IDBF, self.ID32))
        self.vop(lambda e, _: e.tensor_scalar(out=self.MASK, in0=self.IOTA, scalar1=0.0,
                                              scalar2=-1e30, op0=ALU.is_lt, op1=ALU.mult))
        ANG = ar.v(F32, self.A_H, T)
        EMB = ar.v(F32, self.A_H + T * 4, T)
        KIF = ar.v(I32, self.A_H + 2 * T * 4, T)
        KF = ar.v(F32, self.A_H + 3 * T * 4, T)
        PCOL = ar.v(I32, self.A_H + 4 * T * 4, 1)
        st_pf, st_u, st_fr, st_o1, st_o2 = (self.stat(80 + i) for i in range(5))
        vG2 = R.o("gpsimd", lambda e, _: e.iota(ar.v(I32, self.A_H, T), pattern=[[1, T]],
                                                base=0, channel_multiplier=0), inc="sG")
        R.o("gpsimd", lambda e, _: e.iota(PCOL, pattern=[[0, 1]], base=0,
                                          channel_multiplier=1), inc="sG")
        R.w("vector", "sG", S.val("sG"))
        self.vop(lambda e, _: e.tensor_copy(st_pf, PCOL))
        self.vop(lambda e, _: e.tensor_scalar(out=st_u, in0=st_pf, scalar1=64.0, scalar2=None,
                                              op0=ALU.is_ge))
        self.vop(lambda e, _: e.tensor_scalar(out=st_fr, in0=st_pf, scalar1=1.0 / 64.0,
                                              scalar2=None, op0=ALU.mult))
        self.vop(lambda e, _: e.tensor_tensor(out=st_fr, in0=st_fr, in1=st_u, op=ALU.subtract))
        self.vop(lambda e, _: e.tensor_scalar(out=st_o1, in0=st_u, scalar1=_h1, scalar2=None,
                                              op0=ALU.mult))
        self.vop(lambda e, _: e.tensor_scalar(out=st_o2, in0=st_u, scalar1=_h2, scalar2=None,
                                              op0=ALU.mult))
        self.vop(lambda e, _: e.tensor_scalar(out=ANG, in0=ar.v(I32, self.A_H, T),
                                              scalar1=st_fr, scalar2=st_o1,
                                              op0=ALU.mult, op1=ALU.add))
        self.vop(lambda e, _: e.tensor_scalar(out=ANG, in0=ANG, scalar1=st_o2, scalar2=None,
                                              op0=ALU.add))
        self.vop(lambda e, _: e.tensor_scalar(out=KF, in0=ANG, scalar1=INV_TWO_PI,
                                              scalar2=None, op0=ALU.mult))
        self.vop(lambda e, _: e.tensor_copy(KIF, KF))
        self.vop(lambda e, _: e.tensor_copy(KF, KIF))
        for cc in (_c1, _c2, _c3):
            self.vop(lambda e, _, cc=cc: e.scalar_tensor_tensor(
                out=ANG, in0=KF, scalar=-cc, in1=ANG, op0=ALU.mult, op1=ALU.add))
        self.vop(lambda e, _: e.tensor_scalar(out=KF, in0=ANG, scalar1=float(np.pi),
                                              scalar2=None, op0=ALU.is_gt))
        for cc in (_c1, _c2, _c3):
            self.vop(lambda e, _, cc=cc: e.scalar_tensor_tensor(
                out=ANG, in0=KF, scalar=-cc, in1=ANG, op0=ALU.mult, op1=ALU.add))
        vz = S.val("sV")
        R.w("scalar", "sV", vz)
        vS = R.o("scalar", lambda e, _: e.activation(out=EMB, in_=ANG, func=AF.Sin), inc="sS")
        R.w("vector", "sS", vS)
        self.vop(lambda e, _: e.tensor_scalar(out=ANG, in0=EMB, scalar1=INV_SQRT_D_HI,
                                              scalar2=None, op0=ALU.mult))
        vq = self.vop(lambda e, _: e.scalar_tensor_tensor(out=ANG, in0=EMB,
                                                          scalar=INV_SQRT_D_LO, in1=ANG,
                                                          op0=ALU.mult, op1=ALU.add))
        R.w("gpsimd", "sV", vq)
        R.o("gpsimd", lambda e, _: e.dma_start(out=self.d_emb[0:128, :], in_=ANG), inc="sStEMB", k=16)
        R.o("gpsimd", lambda e, _: e.dma_start(out=self.d_emb[128:256, :], in_=EMB), inc="sStEMB", k=16)

        st_a1, st_a2, st_t = self.stat(86), self.stat(87), self.stat(88)
        st_m = self.stat(89, n=4, rows=1)
        R.w("sync", "sStEMB", S.val("sStEMB"))
        nch1 = DM // 128
        nch2 = HALF_F // 128
        ldsem = ("sLd1", "sLd2")

        def mean_pass(nch, src, cols, st_acc):
            bufs = [ar.v(F32, self.A_H, cols), ar.v(F32, self.A_H + HALF_F * 4, cols)]
            ev = S.val("sV")
            vguard = [ev, ev]
            for i in range(nch):
                b = i % 2
                R.w("sync", ldsem[b], S.val(ldsem[b]))
                if vguard[b]:
                    R.w("sync", "sV", vguard[b])
                ld = R.o("sync", lambda e, _, i=i, b=b: e.dma_start(
                    out=bufs[b], in_=src[i * 128:(i + 1) * 128, :]), inc=ldsem[b], k=16)
                R.w("vector", ldsem[b], ld)
                self.vop(lambda e, _, b=b: e.tensor_reduce(
                    out=st_t, in_=bufs[b], axis=AX.X, op=ALU.add,
                    apply_absolute_value=True))
                if i == 0:
                    self.vop(lambda e, _: e.tensor_copy(st_acc, st_t))
                else:
                    self.vop(lambda e, _: e.tensor_tensor(out=st_acc, in0=st_acc, in1=st_t,
                                                          op=ALU.add))
                vguard[b] = S.val("sV")

        mean_pass(nch1, self.w1Th, HALF_F, st_a1)
        mean_pass(nch2, self.w2Th, DM, st_a2)
        ps7 = self._psum[7]
        vv = S.val("sV")
        R.w("tensor", "sV", vv)
        self.mm(ps7.ap()[0:1, 0:1], self.ONESC, st_a1, True, True, inc="sT")
        t2 = self.mm(ps7.ap()[0:1, 1:2], self.ONESC, st_a2, True, True, inc="sT")
        R.w("vector", "sT", t2)
        vcp = self.vop(lambda e, _: e.tensor_copy(st_m[0:1, 0:2], ps7.ap()[0:1, 0:2]))
        self.pguard[7] = ("sV", vcp)
        R.w("gpsimd", "sV", vcp)
        wv = R.o("gpsimd", lambda e, env: e.dma_start(
            out=self.cc_in[0:1, 0:2], in_=st_m[0:1, 0:2]), inc="sStW", k=16)
        R.w("gpsimd", "sStW", wv)
        cv = R.o("gpsimd", lambda e, _: e.collective_compute(
            "AllReduce", ALU.add, replica_groups=[list(range(8))],
            ins=[self.cc_in], outs=[self.cc_out]), inc="sCC")
        R.w("sync", "sCC", S.val("sCC"))
        ld = R.o("sync", lambda e, _: e.dma_start(out=st_m[0:1, 0:2], in_=self.cc_out[0:1, 0:2]),
                 inc="sLd2", k=16)
        R.w("vector", "sLd2", ld)
        st_mm = self.stat(93, n=4, rows=1)
        self.vop(lambda e, _: e.tensor_scalar(out=st_mm[0:1, 0:2], in0=st_m[0:1, 0:2],
                                              scalar1=1.0 / (4.0 * DM * DFF), scalar2=1e-5,
                                              op0=ALU.mult, op1=ALU.max))
        vr = self.vop(lambda e, _: e.reciprocal(out=st_mm[0:1, 2:4], in_=st_mm[0:1, 0:2]))
        R.w("tensor", "sV", vr)
        tb = self.mm(ps7.ap()[:, 0:4], self.ONESR, st_mm[0:1, 0:4], True, True, inc="sT")
        R.w("vector", "sT", tb)
        self.M1B = self.stat(94)
        self.M2B = self.stat(95)
        self.SW1B = self.stat(76)
        self.SW2B = self.stat(77)
        self.M1B127 = self.stat(78)
        self.M2B127h = self.stat(79)
        vcp2 = self.vop(lambda e, _: e.tensor_copy(self.M1B, ps7.ap()[:, 0:1]))
        self.vop(lambda e, _: e.tensor_copy(self.M2B, ps7.ap()[:, 1:2]))
        self.vop(lambda e, _: e.tensor_copy(self.SW1B, ps7.ap()[:, 2:3]))
        vcp3 = self.vop(lambda e, _: e.tensor_copy(self.SW2B, ps7.ap()[:, 3:4]))
        self.pguard[7] = ("sV", vcp3)
        self.vop(lambda e, _: e.tensor_scalar(out=self.M1B127, in0=self.M1B,
                                              scalar1=1.0 / 127.0, scalar2=None, op0=ALU.mult))
        self.vop(lambda e, _: e.tensor_scalar(out=self.M2B127h, in0=self.M2B,
                                              scalar1=0.5 / 127.0, scalar2=None, op0=ALU.mult))

        stsem = ("sStW", "sStW2")

        def tern_pass(nch, src, dst, cols, sw_b):
            WCb = [ar.v(F32, self.A_H, cols), ar.v(F32, self.A_H + HALF_F * 4, cols)]
            WQb = [ar.v(BF16, self.A_H + 2 * HALF_F * 4 + j * HALF_F * 2, cols)
                   for j in range(2)]
            WI8b = [ar.v(I8, self.A_H + 3 * HALF_F * 4 + j * HALF_F, cols)
                    for j in range(2)]
            ev = S.val("sV")
            g_guard = [ev, ev]
            st_guard = [S.val(stsem[0]), S.val(stsem[1])]
            for i in range(nch):
                b = i % 2
                R.w("sync", ldsem[b], S.val(ldsem[b]))
                if g_guard[b]:
                    R.w("sync", "sV", g_guard[b])
                ld = R.o("sync", lambda e, _, i=i, b=b: e.dma_start(
                    out=WCb[b], in_=src[i * 128:(i + 1) * 128, :]), inc=ldsem[b], k=16)
                for eng in ("scalar", "vector", "gpsimd"):
                    R.w(eng, ldsem[b], ld)
                if st_guard[b]:
                    R.w("gpsimd", stsem[b], st_guard[b])
                    R.w("vector", stsem[b], st_guard[b])
                sem, v = self.quant(WCb[b], sw_b, WQb[b], WI8b[b], WQb[b], clip=True)
                g_guard[b] = S.val("sV")
                R.w("gpsimd", sem, v)
                st_guard[b] = R.o("gpsimd", lambda e, env, i=i, b=b: e.dma_start(
                    out=dst[i * 128:(i + 1) * 128, :], in_=WQb[b]), inc=stsem[b], k=16)

        tern_pass(nch1, self.w1Th, self.d_t1Tl, HALF_F, self.SW1B)
        tern_pass(nch2, self.w2Th, self.d_t2Tl, DM, self.SW2B)
        R.w("gpsimd", "sStW", S.val("sStW"))
        R.w("gpsimd", "sStW2", S.val("sStW2"))
        semh = S.h["sStW"]
        def _tcopy(e, env, rho):
            e.dma_start(out=self.sh_t1T[:, rho * HALF_F:(rho + 1) * HALF_F],
                        in_=self.d_t1Tl[:, :]).then_inc(semh, 16)
            e.dma_start(out=self.sh_t2T[rho * HALF_F:(rho + 1) * HALF_F, :],
                        in_=self.d_t2Tl[:, :]).then_inc(semh, 16)
        R.switch("gpsimd", _tcopy, incs=(("sStW", 32),))
        R.w("gpsimd", "sStW", S.val("sStW"))
        self.barrier()

    def ffn_call(self, x_src, dst, do_ln2, bar_weights):
        c, R, S, ar = self.c, self.R, self.sems, self.ar
        G, NT, DM, DFF, DCH, FCH, F512, NG, TOK = (c.G, c.NT, c.DM, c.DFF, c.DCH,
                                                   c.FCH, c.F512, c.NG, c.TOK)
        NF512 = DFF // 512
        DOW = min(512, DM)
        WAb = [ar.v(BF16, self.WA + i * DCH * 512 * 2, DCH * 512) for i in range(2)]
        WBb = [ar.v(BF16, self.WB + i * 512 * 2, DOW) for i in range(4)]
        CX = ar.v(F32, self.C_X, NT * DM)
        BLN = ar.v(F32, self.B_LN, DM)
        DST = ar.v(F32, self.D_ST, DM)
        QB8 = ar.v(BF16, self.QB, DM)
        R.w("sync", "sCC", bar_weights)
        for g in range(NG):
            sb = 8
            ldx = 0
            R.w("sync", "sLdX", S.val("sLdX"))
            for tt in range(NT):
                ldx = R.o("sync", lambda e, _, g=g, tt=tt: e.dma_start(
                    out=CX[:, tt * DM:(tt + 1) * DM],
                    in_=x_src[(g * G + tt * 128):(g * G + tt * 128 + 128), :]),
                    inc="sLdX", k=16)
            R.w("scalar", "sV", S.val("sV"))
            for tt in range(NT):
                for eng in ("vector", "scalar"):
                    R.w(eng, "sLdX", ldx)
                R.w("scalar", "sG", S.val("sG"))
                R.w("scalar", "sS", S.val("sS"))
                R.w("scalar", "sV", S.val("sV"))
                R.w("vector", "sG", S.val("sG"))
                xa = CX[:, tt * DM:(tt + 1) * DM]
                (sA, vA), st_inv = self.ln_tile(xa, BLN, BLN, sb + tt * 12)
                st_mx = self.stat(sb + tt * 12 + 6)
                st_sa = self.stat(sb + tt * 12 + 7)
                st_scA = self.stat(sb + tt * 12 + 8)
                if sA != "sV":
                    R.w("vector", sA, vA)
                self.vop(lambda e, _, st_mx=st_mx: e.tensor_reduce(
                    out=st_mx, in_=BLN, axis=AX.X, op=ALU.max, apply_absolute_value=True))
                self.vop(lambda e, _, st_mx=st_mx: e.tensor_scalar(
                    out=st_mx, in0=st_mx, scalar1=1e-5, scalar2=None, op0=ALU.max))
                self.vop(lambda e, _, st_sa=st_sa, st_mx=st_mx: e.reciprocal(out=st_sa, in_=st_mx))
                self.vop(lambda e, _, st_sa=st_sa: e.tensor_scalar(
                    out=st_sa, in0=st_sa, scalar1=127.0, scalar2=None, op0=ALU.mult))
                vscl = self.vop(lambda e, _, st_scA=st_scA, st_mx=st_mx: e.tensor_tensor(
                    out=st_scA, in0=st_mx, in1=self.M1B127, op=ALU.mult))
                R.w("scalar", "sV", vscl)
                R.w("gpsimd", "sStLN", S.val("sStLN"))
                R.w("scalar", "sStLN", S.val("sStLN"))
                R.w("vector", "sStLN", S.val("sStLN"))
                qsem, qv = self.quant(BLN, st_sa, QB8, ar.v(I8, self.I8A_A, DM), QB8)
                R.w("tensor", qsem, qv)
                self.transp_batch(lambda i: QB8[:, i * 128:(i + 1) * 128], DCH,
                                  lambda i0, w, tt=tt: ar.v(BF16, self.QT8, w,)[:, 0:0] if False else
                                  ar.v(BF16, self.QT8 + (tt * DCH + i0) * 128 * 2, w),
                                  banks=(4, 5), bf16=True, evac_eng="vector")
            qt8_ready = S.val("sV")
            R.w("tensor", "sV", qt8_ready)
            qt8_tile = lambda dc, tt: ar.v(BF16, self.QT8 + (tt * DCH + dc) * 128 * 2, 128)
            AH = lambda tt: ar.v(F32, self.A_H + tt * DFF * 4, DFF)
            for fs in range(NF512):
                if fs >= 2:
                    R.w("sync", "sT", self._wa_guard[fs % 2])
                wsem = "sWA0" if fs % 2 == 0 else "sWA1"
                R.w("sync", wsem, S.val(wsem))
                ldw = R.o("sync", lambda e, _, fs=fs: e.dma_start(
                    out=WAb[fs % 2].rearrange("p (c f) -> p c f", c=DCH),
                    in_=self.sh_t1T[:, fs * 512:(fs + 1) * 512].rearrange(
                        "(c p) f -> p c f", p=128)), inc=wsem, k=16)
                R.w("tensor", wsem, ldw)
                for tt in range(NT):
                    bk = (fs * NT + tt) % 4
                    ps = self._psum[bk]
                    if self.pguard[bk] is not None:
                        R.w("tensor", *self.pguard[bk])
                        self.pguard[bk] = None
                    for dc in range(DCH):
                        tv = self.mm(ps.ap()[:, 0:512], qt8_tile(dc, tt),
                                     WAb[fs % 2][:, dc * 512:(dc + 1) * 512],
                                     start=(dc == 0), stop=(dc == DCH - 1),
                                     inc="sT" if dc == DCH - 1 else None)
                    if tt == NT - 1:
                        self._wa_guard[fs % 2] = tv
                    R.w("scalar", "sT", tv)
                    st_scA = self.stat(sb + tt * 12 + 8)
                    sv = R.o("scalar", lambda e, _, ps=ps, fs=fs, tt=tt, st_scA=st_scA:
                             e.activation(out=AH(tt)[:, fs * 512:(fs + 1) * 512],
                                          in_=ps.ap()[:, 0:512], func=AF.Silu,
                                          scale=st_scA), inc="sS")
                    self.pguard[bk] = ("sS", sv)
            hsil_v = S.val("sS")
            for tt in range(NT):
                st_mh = self.stat(sb + tt * 12 + 9)
                st_sh = self.stat(sb + tt * 12 + 10)
                st_sc2 = self.stat(sb + tt * 12 + 11)
                R.w("vector", "sS", hsil_v)
                self.vop(lambda e, _, st_mh=st_mh, tt=tt: e.tensor_reduce(
                    out=st_mh, in_=AH(tt), axis=AX.X, op=ALU.max, apply_absolute_value=True))
                self.vop(lambda e, _, st_mh=st_mh: e.tensor_scalar(
                    out=st_mh, in0=st_mh, scalar1=1e-5, scalar2=None, op0=ALU.max))
                self.vop(lambda e, _, st_sh=st_sh, st_mh=st_mh: e.reciprocal(
                    out=st_sh, in_=st_mh))
                self.vop(lambda e, _, st_sh=st_sh: e.tensor_scalar(
                    out=st_sh, in0=st_sh, scalar1=127.0, scalar2=None, op0=ALU.mult))
                vsc2 = self.vop(lambda e, _, st_sc2=st_sc2, st_mh=st_mh: e.tensor_tensor(
                    out=st_sc2, in0=st_mh, in1=self.M2B127h, op=ALU.mult))
                R.w("scalar", "sV", vsc2)
                HQb = ar.v(BF16, self.HQ, DFF)
                qsem, qv = self.quant(AH(tt), st_sh, HQb, ar.v(I8, self.I8A, DFF), HQb)
                R.w("tensor", qsem, qv)
                self.transp_batch(lambda i: HQb[:, i * 128:(i + 1) * 128], FCH,
                                  lambda i0, w, tt=tt: ar.v(
                                      BF16, self.HT + (tt * FCH + i0) * 128 * 2, w),
                                  banks=(4, 5), bf16=True, evac_eng="vector")
            ht_ready = S.val("sV")
            R.w("tensor", "sV", ht_ready)
            ht_tile = lambda fc, tt: ar.v(BF16, self.HT + (tt * FCH + fc) * 128 * 2, 128)
            for do in range(DM // DOW):
                for fc in range(FCH):
                    wbuf = fc % 4
                    wsem2 = "sWB0" if fc % 2 == 0 else "sWB1"
                    if self._wb_guard[wbuf]:
                        R.w("scalar", "sT", self._wb_guard[wbuf])
                    R.w("scalar", wsem2, S.val(wsem2))
                    ldw2 = R.o("scalar", lambda e, _, fc=fc, do=do, wbuf=wbuf: e.dma_start(
                        out=WBb[wbuf],
                        in_=self.sh_t2T[fc * 128:(fc + 1) * 128, do * DOW:(do + 1) * DOW]),
                        inc=wsem2, k=16)
                    R.w("tensor", wsem2, ldw2)
                    last_in_batch = (fc % 4 == 3) or (fc == FCH - 1)
                    for tt in range(NT):
                        bk = 6 if tt == 0 else 7
                        ps = self._psum[bk]
                        if fc == 0 and self.pguard[bk] is not None:
                            R.w("tensor", *self.pguard[bk])
                            self.pguard[bk] = None
                        tv2 = self.mm(ps.ap()[:, 0:DOW], ht_tile(fc, tt), WBb[wbuf],
                                      start=(fc == 0), stop=(fc == FCH - 1),
                                      inc="sT" if (last_in_batch and tt == NT - 1) else None)
                    if last_in_batch:
                        for b in range(4):
                            self._wb_guard[b] = tv2
                for tt in range(NT):
                    bk = 6 if tt == 0 else 7
                    ps = self._psum[bk]
                    st_sc2 = self.stat(sb + tt * 12 + 11)
                    R.w("vector", "sT", S.val("sT"))
                    R.w("vector", "sS", S.val("sS"))
                    R.w("vector", "sLdX", S.val("sLdX"))
                    ev = self.vop(lambda e, _, ps=ps, tt=tt, do=do, st_sc2=st_sc2:
                                  e.scalar_tensor_tensor(
                        out=CX[:, tt * DM + do * DOW: tt * DM + (do + 1) * DOW],
                        in0=ps.ap()[:, 0:DOW], scalar=st_sc2,
                        in1=CX[:, tt * DM + do * DOW: tt * DM + (do + 1) * DOW],
                        op0=ALU.mult, op1=ALU.add))
                    self.pguard[bk] = ("sV", ev)
            for tt in range(NT):
                R.w("gpsimd", "sV", S.val("sV"))
                stv = R.o("gpsimd", lambda e, _, g=g, tt=tt: e.dma_start(
                    out=dst[(g * G + tt * 128):(g * G + tt * 128 + 128), :],
                    in_=CX[:, tt * DM:(tt + 1) * DM]), inc="sStCX", k=16)
                if do_ln2:
                    R.w("scalar", "sV", S.val("sV"))
                    R.w("scalar", "sT", S.val("sT"))
                    R.w("vector", "sT", S.val("sT"))
                    R.w("scalar", "sG", S.val("sG"))
                    R.w("vector", "sG", S.val("sG"))
                    (sA2, vA2), _ = self.ln_tile(CX[:, tt * DM:(tt + 1) * DM], BLN, BLN,
                                                 56 + tt * 12)
                    R.w("tensor", sA2, vA2)
                    R.w("vector", "sStLN", S.val("sStLN"))
                    self.transp_batch(lambda i: BLN[:, i * 128:(i + 1) * 128], DCH,
                                      lambda i0, w: ar.v(F32, self.D_ST + i0 * 128 * 4, w),
                                      banks=(4, 5), bf16=False, evac_eng="vector")
                    R.w("gpsimd", "sV", S.val("sV"))
                    R.o("gpsimd", lambda e, env, g=g, tt=tt: e.dma_start(
                        out=self.d_ln2Tl.rearrange("(c p) t -> p c t", p=128)[
                            :, :, (g * G + tt * 128):(g * G + tt * 128 + 128)],
                        in_=DST.rearrange("p (c t) -> p c t", c=DCH)), inc="sStLN", k=16)
            R.w("sync", "sStCX", S.val("sStCX"))
            R.w("sync", "sS", S.val("sS"))
            R.w("sync", "sV", S.val("sV"))
            R.w("sync", "sG", S.val("sG"))
            R.w("sync", "sT", S.val("sT"))

    def qkv(self, bar_ln2):
        c, R, S, ar = self.c, self.R, self.sems, self.ar
        T, DM, DCH, OUTF, HC = c.T, c.DM, c.DCH, c.OUTF, c.HC
        OGT = self.OGT
        NTILE = OUTF // 128
        NOG = ceil_div(NTILE, OGT)
        TC = min(512, T)
        EMQ = ar.v(F32, self.QEMB, T)
        EMK = ar.v(F32, self.QEMB + T * 4, T)
        QSTb = [ar.v(F32, self.QST + i * 512 * 4, TC) for i in range(2)]
        QLNb = [ar.v(F32, self.QLN + i * DCH * 512 * 4, DCH * TC) for i in range(2)]
        BIASQ = self.stat(32, n=NTILE)
        l1 = R.o("sync", lambda e, _: e.dma_start(out=EMQ, in_=self.d_emb[0:128, :]), inc="sLd1", k=16)
        l2 = R.o("sync", lambda e, _: e.dma_start(out=EMK, in_=self.d_emb[128:256, :]), inc="sLd1", k=16)
        l3 = R.o("sync", lambda e, _: e.dma_start(
            out=BIASQ, in_=self.qkvb[0, :].rearrange("(o p) -> p o", p=128),
            allow_slow_non_contiguous=True), inc="sLd1", k=16)
        R.w("sync", "sCC", bar_ln2)
        qst_guard = [0, 0]
        qln_guard = [0, 0]
        for og in range(NOG):
            nt = min(OGT, NTILE - og * OGT)
            if og > 0:
                R.w("sync", "sT", S.val("sT"))
            R.w("sync", "sLdQW", S.val("sLdQW"))
            lw = R.o("sync", lambda e, _, og=og, nt=nt: e.dma_start(
                out=ar.v(F32, self.QWT, DCH * nt * 128).rearrange("p (c o) -> p c o", c=DCH),
                in_=self.qkvwT[:, og * OGT * 128: og * OGT * 128 + nt * 128].rearrange(
                    "(c p) o -> p c o", p=128)), inc="sLdQW", k=16)
            for tc in range(T // TC):
                bln = tc % 2
                lnsem = "sQL0" if bln == 0 else "sQL1"
                if qln_guard[bln]:
                    R.w("sync", "sT", qln_guard[bln])
                R.w("sync", lnsem, S.val(lnsem))
                ln = R.o("sync", lambda e, _, tc=tc, bln=bln: e.dma_start(
                    out=QLNb[bln].rearrange("p (c t) -> p c t", c=DCH),
                    in_=self.sh_ln2T[:, tc * TC:(tc + 1) * TC].rearrange(
                        "(c p) t -> p c t", p=128)), inc=lnsem, k=16)
                for ot in range(nt):
                    gt = og * OGT + ot
                    bk = (tc * nt + ot) % 2
                    ps = self._psum[bk]
                    if self.pguard[bk] is not None:
                        R.w("tensor", *self.pguard[bk])
                        self.pguard[bk] = None
                    R.w("tensor", "sLdQW", lw)
                    R.w("tensor", lnsem, ln)
                    R.w("tensor", "sLd1", l3)
                    for dc in range(DCH):
                        tv = self.mm(ps.ap()[:, 0:TC],
                                     ar.v(F32, self.QWT + (dc * nt + ot) * 128 * 4, 128),
                                     QLNb[bln][:, dc * TC:(dc + 1) * TC],
                                     start=(dc == 0), stop=(dc == DCH - 1),
                                     inc="sT" if dc == DCH - 1 else None)
                    if ot == nt - 1:
                        qln_guard[bln] = tv
                    sbuf_i = (tc * nt + ot) % 2
                    if qst_guard[sbuf_i]:
                        R.w("vector", "sStQ0" if sbuf_i == 0 else "sStQ1", qst_guard[sbuf_i])
                    R.w("vector", "sT", tv)
                    R.w("vector", "sLd1", max(l1, l2, l3))
                    kind = gt // HC
                    if kind < 2:
                        emb = EMQ if kind == 0 else EMK
                        vv = self.vop(lambda e, _, ps=ps, gt=gt, sbuf_i=sbuf_i, emb=emb, tc=tc:
                                      e.scalar_tensor_tensor(
                            out=QSTb[sbuf_i], in0=ps.ap()[:, 0:TC], scalar=BIASQ[:, gt:gt + 1],
                            in1=emb[:, tc * TC:(tc + 1) * TC], op0=ALU.add, op1=ALU.mult))
                    else:
                        vv = self.vop(lambda e, _, ps=ps, gt=gt, sbuf_i=sbuf_i: e.tensor_scalar(
                            out=QSTb[sbuf_i], in0=ps.ap()[:, 0:TC], scalar1=BIASQ[:, gt:gt + 1],
                            scalar2=None, op0=ALU.add))
                    self.pguard[bk] = ("sV", vv)
                    dep = ("sV", vv)
                    R.w("gpsimd", *dep)
                    gv = R.o("gpsimd", lambda e, _, gt=gt, tc=tc, sbuf_i=sbuf_i: e.dma_start(
                        out=self.d_qkv[gt * 128:(gt + 1) * 128, tc * TC:(tc + 1) * TC],
                        in_=QSTb[sbuf_i]), inc="sStQ0" if sbuf_i == 0 else "sStQ1", k=16)
                    qst_guard[sbuf_i] = gv

    def attn(self):
        c, R, S, ar = self.c, self.R, self.sems, self.ar
        T, HC, QT = c.T, c.HC, c.QT
        NKT = T // 128
        AQ = ar.v(F32, self.AQ, T); AK = ar.v(F32, self.AK, T)
        AVV = ar.v(F32, self.AVV, T); AVT = ar.v(F32, self.AVT, T)
        APS = ar.v(F32, self.APS, T); APT = ar.v(F32, self.APT, T)
        AO = ar.v(F32, self.AO, T)
        AOSb = [ar.v(F32, self.AOS + i * 512 * 4, 512) for i in range(2)]
        p0 = self.AP0
        KMX = ar.v(F32, p0, 32, rows=1); KPR = ar.v(F32, p0 + 128, 32, rows=1)
        QMX = ar.v(F32, p0 + 256, 32, rows=1); NGC = ar.v(F32, p0 + 384, 32, rows=1)
        NEGC = ar.v(F32, p0 + 512, 32)
        LPART = ar.v(F32, p0 + 1024, 16)
        RML = ar.v(F32, p0 + 1536, 1)
        aos_guard = [0, 0]
        for h in range(HC):
            base_q, base_k, base_v = h, HC + h, 2 * HC + h
            R.w("sync", "sStO0", S.val("sStO0"))
            R.w("sync", "sStO1", S.val("sStO1"))
            R.w("sync", "sLdQKV", S.val("sLdQKV"))
            R.w("sync", "sStQ0", S.val("sStQ0"))
            R.w("sync", "sStQ1", S.val("sStQ1"))
            lq = R.o("sync", lambda e, _, r=base_q: e.dma_start(
                out=AQ, in_=self.d_qkv[r * 128:(r + 1) * 128, :]), inc="sLdQKV", k=16)
            lk = R.o("sync", lambda e, _, r=base_k: e.dma_start(
                out=AK, in_=self.d_qkv[r * 128:(r + 1) * 128, :]), inc="sLdQKV", k=16)
            lv = R.o("sync", lambda e, _, r=base_v: e.dma_start(
                out=AVV, in_=self.d_qkv[r * 128:(r + 1) * 128, :]), inc="sLdQKV", k=16)
            R.w("tensor", "sLdQKV", lv)
            R.w("vector", "sLdQKV", lv)
            self.transp_batch(lambda i: AVV[:, i * 128:(i + 1) * 128], NKT,
                              lambda i0, w: ar.v(F32, self.AVT + i0 * 128 * 4, w),
                              banks=(2, 3), bf16=False, evac_eng="vector")
            ps7 = self._psum[7]
            for which, src, ld, dst in (("k", AK, lk, KMX), ("q", AQ, lq, QMX)):
                R.w("vector", "sLdQKV", lv)
                vv = self.vop(lambda e, _, src=src: e.tensor_tensor(
                    out=APS, in0=src, in1=src, op=ALU.mult))
                R.w("tensor", "sV", vv)
                for ch in range(T // 512):
                    if self.pguard[7] is not None:
                        R.w("tensor", *self.pguard[7])
                        self.pguard[7] = None
                    tv = self.mm(ps7.ap()[0:1, 0:512], self.ONESC,
                                 APS[:, ch * 512:(ch + 1) * 512], True, True, inc="sT")
                    R.w("vector", "sT", tv)
                    rv = self.vop(lambda e, _, ch=ch, dst=dst: e.tensor_reduce(
                        out=dst[0:1, ch * 4:(ch + 1) * 4],
                        in_=ps7.ap()[0:1, 0:512].rearrange("p (c t) -> p c t", t=128),
                        axis=AX.X, op=ALU.max))
                    self.pguard[7] = ("sV", rv)
            for qt in range(QT):
                self.R.q["vector"].append(lambda e, _: None) if False else None
                rv = self.vop(lambda e, _, qt=qt: e.tensor_reduce(
                    out=KPR[0:1, qt:qt + 1], in_=KMX[0:1, 0:qt + 1], axis=AX.X, op=ALU.max))
            self.vop(lambda e, _: e.tensor_tensor(out=NGC[0:1, 0:QT], in0=QMX[0:1, 0:QT],
                                                  in1=KPR[0:1, 0:QT], op=ALU.mult))
            vs = S.val("sV")
            R.w("scalar", "sV", vs)
            sv = R.o("scalar", lambda e, _: e.activation(out=NGC[0:1, 0:QT],
                                                         in_=NGC[0:1, 0:QT], func=AF.Sqrt),
                     inc="sS")
            R.w("vector", "sS", sv)
            vneg = self.vop(lambda e, _: e.tensor_scalar(out=NGC[0:1, 0:QT],
                                                         in0=NGC[0:1, 0:QT], scalar1=-1.0,
                                                         scalar2=None, op0=ALU.mult))
            R.w("tensor", "sV", vneg)
            if self.pguard[7] is not None:
                R.w("tensor", *self.pguard[7]); self.pguard[7] = None
            tb = self.mm(ps7.ap()[:, 0:QT], self.ONESR, NGC[0:1, 0:QT], True, True, inc="sT")
            R.w("vector", "sT", tb)
            vb = self.vop(lambda e, _: e.tensor_copy(NEGC[:, 0:QT], ps7.ap()[:, 0:QT]))
            self.pguard[7] = ("sV", vb)
            aps_guard_t = 0
            aps_guard_v = 0
            for qt in range(QT):
                Lk = 128 * (qt + 1)
                R.w("scalar", "sT", aps_guard_t)
                R.w("scalar", "sV", aps_guard_v)
                chunks = []
                c0 = 0
                while c0 + 512 <= Lk - 128:
                    chunks.append((c0, 512)); c0 += 512
                if Lk - c0 > 0:
                    chunks.append((c0, Lk - c0))
                nch = len(chunks)
                for ci, (cb, cw) in enumerate(chunks):
                    bk = ci % 2
                    ps = self._psum[bk]
                    if self.pguard[bk] is not None:
                        R.w("tensor", *self.pguard[bk]); self.pguard[bk] = None
                    tv = self.mm(ps.ap()[:, 0:cw], AQ[:, qt * 128:(qt + 1) * 128],
                                 AK[:, cb:cb + cw], True, True, inc="sT")
                    last = (ci == nch - 1)
                    dep_sem, dep_v = "sT", tv
                    if last:
                        R.w("vector", "sT", tv)
                        mv = self.vop(lambda e, _, ps=ps, cw=cw: e.tensor_tensor(
                            out=ps.ap()[:, cw - 128:cw], in0=ps.ap()[:, cw - 128:cw],
                            in1=self.MASK, op=ALU.add))
                        dep_sem, dep_v = "sV", mv
                    R.w("scalar", dep_sem, dep_v)
                    sv = R.o("scalar", lambda e, _, ps=ps, cb=cb, cw=cw, ci=ci, qt=qt:
                             e.activation(out=APS[:, cb:cb + cw], in_=ps.ap()[:, 0:cw],
                                          func=AF.Exp, bias=NEGC[:, qt:qt + 1],
                                          accum_out=LPART[:, ci:ci + 1]), inc="sS")
                    self.pguard[bk] = ("sS", sv)
                R.w("vector", "sS", S.val("sS"))
                self.vop(lambda e, _, nch=nch: e.tensor_reduce(
                    out=RML, in_=LPART[:, 0:nch], axis=AX.X, op=ALU.add))
                vr = self.vop(lambda e, _: e.reciprocal(out=RML, in_=RML))
                R.w("tensor", "sS", S.val("sS"))
                self.transp_batch(lambda i: APS[:, i * 128:(i + 1) * 128], qt + 1,
                                  lambda i0, w: ar.v(F32, self.APT + i0 * 128 * 4, w),
                                  banks=(2, 3), bf16=False, evac_eng="vector")
                aps_guard_t = S.val("sT")
                aps_guard_v = S.val("sV")
                R.w("tensor", "sV", S.val("sV"))
                bk = 4
                if self.pguard[bk] is not None:
                    R.w("tensor", *self.pguard[bk]); self.pguard[bk] = None
                ps4 = self._psum[bk]
                for kt in range(qt + 1):
                    tv = self.mm(ps4.ap()[:, 0:128], APT[:, kt * 128:(kt + 1) * 128],
                                 AVT[:, kt * 128:(kt + 1) * 128],
                                 start=(kt == 0), stop=(kt == qt),
                                 inc="sT" if kt == qt else None)
                R.w("scalar", "sT", tv)
                R.w("scalar", "sV", vr)
                so = R.o("scalar", lambda e, _, qt=qt: e.activation(
                    out=AO[:, qt * 128:(qt + 1) * 128], in_=ps4.ap()[:, 0:128],
                    func=AF.Copy, scale=RML), inc="sS")
                self.pguard[bk] = ("sS", so)
            sT_before = S.val("sS")
            for i0 in range(0, NKT, 4):
                w = min(4, NKT - i0)
                bk = 5 if (i0 // 4) % 2 == 0 else 6
                ps = self._psum[bk]
                if self.pguard[bk] is not None:
                    R.w("tensor", *self.pguard[bk]); self.pguard[bk] = None
                R.w("tensor", "sS", sT_before)
                for j in range(w):
                    tv = R.o("tensor", lambda e, _, j=j, i0=i0, ps=ps: e.transpose(
                        ps.ap()[:, j * 128:(j + 1) * 128],
                        AO[:, (i0 + j) * 128:(i0 + j + 1) * 128], self.ID32), inc="sT")
                half = (i0 // 4) % 2
                if aos_guard[half]:
                    R.w("vector", "sStO0" if half == 0 else "sStO1", aos_guard[half])
                R.w("vector", "sT", tv)
                ev = self.vop(lambda e, _, ps=ps, half=half, w=w: e.tensor_copy(
                    AOSb[half][:, 0:w * 128], ps.ap()[:, 0:w * 128]))
                self.pguard[bk] = ("sV", ev)
                R.w("gpsimd", "sV", ev)
                gv = R.o("gpsimd", lambda e, env, h=h, i0=i0, w=w, half=half: e.dma_start(
                    out=self.d_oTl[h * 128:(h + 1) * 128, i0 * 128:(i0 + w) * 128],
                    in_=AOSb[half][:, 0:w * 128]), inc="sStO0" if half == 0 else "sStO1", k=16)
                aos_guard[half] = gv

    def outproj(self, bar_o):
        c, R, S, ar = self.c, self.R, self.sems, self.ar
        T, DM, DCH, TOK = c.T, c.DM, c.DCH, c.TOK
        OPG = self.OPG
        DOW = min(512, DM)
        NT2 = OPG // 128
        OC = ar.v(F32, self.OC, DCH * OPG)
        WOb = [ar.v(F32, self.WO + i * 512 * 4, DOW) for i in range(2)]
        X1Gv = ar.v(F32, self.X1G, NT2 * DM)
        X2Sv = ar.v(F32, self.X2S, NT2 * DM)
        OBROW = self.stat(32, n=DM // 128, rows=1)
        ob = ar.v(F32, self.OB, DM, rows=1)
        lob = R.o("sync", lambda e, _: e.dma_start(out=ob, in_=self.outb[0:1, :]), inc="sLd1", k=16)
        R.w("sync", "sCC", bar_o)
        wo_guard = [0, 0]
        R.w("sync", "sStCX", S.val("sStCX"))
        R.w("sync", "sCC", bar_o)
        R.w("sync", "sLd1", S.val("sLd1"))
        semoc2 = S.h["sLd1"]
        TOKc = TOK
        def _ocfetch(e, env, rho):
            e.dma_start(out=self.d_ocl[:, :],
                        in_=self.sh_ocat[:, rho * TOKc:(rho + 1) * TOKc]).then_inc(semoc2, 16)
        R.switch("sync", _ocfetch, incs=(("sLd1", 16),))
        ocf = S.val("sLd1")
        for grp in range(TOK // OPG):
            R.w("sync", "sStX2", S.val("sStX2"))
            R.w("sync", "sLdOC", S.val("sLdOC"))
            R.w("sync", "sLd1", ocf)
            lo = R.o("sync", lambda e, env, grp=grp: e.dma_start(
                out=OC.rearrange("p (c t) -> p c t", c=DCH),
                in_=self.d_ocl.rearrange("(c p) t -> p c t", p=128)[
                    :, :, grp * OPG:(grp + 1) * OPG]), inc="sLdOC", k=16)
            lx = R.o("sync", lambda e, _, grp=grp: e.dma_start(
                out=X1Gv.rearrange("p (n m) -> p n m", n=NT2),
                in_=self.d_x1[grp * OPG:(grp + 1) * OPG, :].rearrange(
                    "(n p) m -> p n m", p=128)), inc="sLdOC", k=16)
            for do in range(DM // DOW):
                for tt2 in range(NT2):
                    bk = tt2
                    if self.pguard[bk] is not None:
                        R.w("tensor", *self.pguard[bk]); self.pguard[bk] = None
                for dc in range(DCH):
                    wbuf = (do * DCH + dc) % 2
                    wos = "sWO0" if wbuf == 0 else "sWO1"
                    if wo_guard[wbuf]:
                        R.w("sync", "sT", wo_guard[wbuf])
                    R.w("sync", wos, S.val(wos))
                    lw = R.o("sync", lambda e, _, dc=dc, do=do, wbuf=wbuf: e.dma_start(
                        out=WOb[wbuf],
                        in_=self.outwT[dc * 128:(dc + 1) * 128, do * DOW:(do + 1) * DOW]),
                        inc=wos, k=16)
                    R.w("tensor", wos, lw)
                    R.w("tensor", "sLdOC", lx)
                    R.w("tensor", "sLd1", lob)
                    for tt2 in range(NT2):
                        tv = self.mm(self._psum[tt2].ap()[:, 0:DOW],
                                     OC[:, dc * OPG + tt2 * 128: dc * OPG + tt2 * 128 + 128],
                                     WOb[wbuf], start=(dc == 0), stop=False,
                                     inc="sT" if tt2 == NT2 - 1 else None)
                    wo_guard[wbuf] = tv
                for tt2 in range(NT2):
                    tb = self.mm(self._psum[tt2].ap()[:, 0:DOW], self.ONESR,
                                 ob[0:1, do * DOW:(do + 1) * DOW], False, True, inc="sT")
                    R.w("vector", "sT", tb)
                    R.w("vector", "sLdOC", lx)
                    ev = self.vop(lambda e, _, tt2=tt2, do=do: e.scalar_tensor_tensor(
                        out=X2Sv[:, tt2 * DM + do * DOW: tt2 * DM + (do + 1) * DOW],
                        in0=self._psum[tt2].ap()[:, 0:DOW], scalar=1.0,
                        in1=X1Gv[:, tt2 * DM + do * DOW: tt2 * DM + (do + 1) * DOW],
                        op0=ALU.mult, op1=ALU.add))
                    self.pguard[tt2] = ("sV", ev)
            for tt2 in range(NT2):
                R.w("gpsimd", "sV", S.val("sV"))
                R.o("gpsimd", lambda e, _, grp=grp, tt2=tt2: e.dma_start(
                    out=self.d_x2[grp * OPG + tt2 * 128: grp * OPG + tt2 * 128 + 128, :],
                    in_=X2Sv[:, tt2 * DM:(tt2 + 1) * DM]), inc="sStX2", k=16)

    def build(self):
        R, S = self.R, self.sems
        self._wa_guard = [0, 0]
        self._wb_guard = [0, 0, 0, 0]
        S.n["sLd1"] += 16
        self.p0()
        bar_w = S.val("sCC")
        self.fence()
        self.ffn_call(self.x_in, self.d_x1, True, bar_w)
        R.w("gpsimd", "sStLN", S.val("sStLN"))
        R.w("gpsimd", "sStCX", S.val("sStCX"))
        semln = S.h["sStLN"]
        TOKc = self.c.TOK
        def _lncopy(e, env, rho):
            e.dma_start(out=self.sh_ln2T[:, rho * TOKc:(rho + 1) * TOKc],
                        in_=self.d_ln2Tl[:, :]).then_inc(semln, 16)
        R.switch("gpsimd", _lncopy, incs=(("sStLN", 16),))
        R.w("gpsimd", "sStLN", S.val("sStLN"))
        bar1 = self.barrier()
        self.fence()
        self.qkv(bar1)
        self.fence()
        self.attn()
        R.w("gpsimd", "sStO0", S.val("sStO0"))
        R.w("gpsimd", "sStO1", S.val("sStO1"))
        semoc = S.h["sStO0"]
        HCD = self.c.HC * 128
        def _occopy(e, env, rho):
            e.dma_start(out=self.sh_ocat[rho * HCD:(rho + 1) * HCD, :],
                        in_=self.d_oTl[:, :]).then_inc(semoc, 16)
        R.switch("gpsimd", _occopy, incs=(("sStO0", 16),))
        R.w("gpsimd", "sStO0", S.val("sStO0"))
        bar2 = self.barrier()
        self.fence()
        self.outproj(bar2)
        self.fence()
        self.ffn_call(self.d_x2, self.y_out, False, bar_w)
        self.fence()
        env = {
            "nc": self.nc,
            "ld_h": S.h["sLd1"], "ld_v": 16,
            "offload": {"sync": EngT.SP, "gpsimd": EngT.Pool},
            "par_ap": self.POFF[0:1, 3:4],
            "prelude": {"sync": lambda e, env: e.dma_start(
                out=self.POFF, in_=self.poffs[:, :]).then_inc(S.h["sLd1"], 16)},
        }
        with self.nc.Block() as block:
            self.R.replay(block, env)
        return self.nc


def host_in_maps(c: CFG, x_full, qkv_w, qkv_b, out_w, out_b, W1, W2):
    DM, DH, HC, TOK, HALF_F = c.DM, c.DH, c.HC, c.TOK, c.DFF // 2
    H = c.H
    w1T = np.ascontiguousarray(W1.T)
    w2T = np.ascontiguousarray(W2.T)
    outwT = np.ascontiguousarray(out_w.T)
    in_maps = []
    for core in range(8):
        p, rho = core // 2, core % 2
        heads = [rho * HC + hh for hh in range(HC)]
        rows = []
        for kind in range(3):
            for hh in heads:
                rows.extend(range(kind * H * DH + hh * DH, kind * H * DH + (hh + 1) * DH))
        rows = np.array(rows)
        qkvwT = np.ascontiguousarray(qkv_w[rows, :].T)
        qkvb_s = np.ascontiguousarray(qkv_b[rows])[None, :]
        poffs = np.array([[rho * TOK, rho * (DM // 2), rho * HALF_F, rho,
                           rho * 2, rho * HALF_F, 0, 0]], dtype=np.int32)
        in_maps.append({
            "x": np.ascontiguousarray(x_full[p, rho * TOK:(rho + 1) * TOK, :]),
            "qkvwT": qkvwT, "qkvb": qkvb_s,
            "outwT": outwT, "outb": np.ascontiguousarray(out_b)[None, :],
            "w1Th": np.ascontiguousarray(w1T[:, rho * HALF_F:(rho + 1) * HALF_F]),
            "w2Th": np.ascontiguousarray(w2T[rho * HALF_F:(rho + 1) * HALF_F, :]),
            "poffs": poffs,
        })
    return in_maps


def host_gather(c: CFG, results, B):
    out = np.zeros((B, c.T, c.DM), dtype=np.float32)
    for core in range(8):
        p, rho = core // 2, core % 2
        out[p, rho * c.TOK:(rho + 1) * c.TOK, :] = results[core]["y"]
    return out




def kernel(L0_attn_x, L0_attn_qkv_w, L0_attn_qkv_b, L0_attn_out_w, L0_attn_out_b,
           L0_ffn_W1, L0_ffn_W2):
    c = CFG()
    mk = MK(c)
    nc = mk.build()
    in_maps = host_in_maps(c, np.asarray(L0_attn_x), np.asarray(L0_attn_qkv_w),
                           np.asarray(L0_attn_qkv_b), np.asarray(L0_attn_out_w),
                           np.asarray(L0_attn_out_b), np.asarray(L0_ffn_W1),
                           np.asarray(L0_ffn_W2))
    res = run_bass_kernel_spmd(nc, in_maps, core_ids=list(range(8)))
    return host_gather(c, res.results, 4)
